# version 11
# baseline (speedup 1.0000x reference)
import numpy as np
import ml_dtypes
import concourse.bacc as bacc
import concourse.mybir as mybir
from concourse.bass_utils import run_bass_kernel_spmd
from concourse.dve_spec import Spec, lower, Src0, Src1, C0, C1, maxx, minn, AluOp, scan
from concourse.dve_op import DveOp
from concourse.dve_table_gen import dve_ver_for
import concourse.dve_ops as dve_ops

F32 = mybir.dt.float32
BF16 = mybir.dt.bfloat16
I32 = mybir.dt.int32
AL = mybir.AluOpType
AF = mybir.ActivationFunctionType
AX = mybir.AxisListType

P = 128
D = 1024
T = 2048
G = D // P
NST = 32
T1 = 128
NC1 = T // T1
T2 = 8
NSC = T1 // T2
GN = G * NST
RW = GN + 1
FF = 4096
GF = FF // P
H = 8
TC = 128
EPS = 1e-5

_VER = dve_ver_for("TRN2")


def _register_ops():
    def mk(name, spec, reference):
        for o in dve_ops.OPS:
            if o.name == name:
                return o
        op = DveOp(name, spec, subdim=False,
                   uops_sha={_VER: lower(spec, ver=_VER, perf_en=False, name=name).sha(_VER)},
                   reference=reference)
        dve_ops.OPS.append(op)
        dve_ops._SUB_OPCODE_FOR_NAME[name] = dve_ops._CUSTOM_DVE_ROW_BASE + len(dve_ops.OPS) - 1
        dve_ops.CUSTOM_DVE_SPECS[name] = spec
        dve_ops.CUSTOM_DVE_REFS[name] = reference
        return op

    def _ref_mulscan(in0, in1, s0, s1, imm2):
        a = np.asarray(in0, np.float32) * np.asarray(in1, np.float32)
        f = a.reshape(a.shape[0], -1)
        return np.cumsum(f, axis=-1).reshape(np.asarray(in0).shape)

    addclip = mk("MK_ADDCLIP", Spec(body=minn(maxx(Src0 + Src1, C0), C1)),
                 lambda in0, in1, s0, s1, imm2: np.minimum(np.maximum(in0 + in1, s0), s1))
    mulscan = mk("MK_MULSCAN", Spec(body=scan(AluOp.ADD, Src0 * Src1)), _ref_mulscan)
    return addclip, mulscan


ADDCLIP, MULSCAN = _register_ops()


class Bld:
    def __init__(self, stage=1):
        self.stage = stage
        self.nc = bacc.Bacc("TRN2", target_bir_lowering=False, detect_race_conditions=False)
        self._ctx = []
        self._declare()

    def _enter(self, cm):
        v = cm.__enter__()
        self._ctx.append(cm)
        return v

    def _declare(self):
        nc = self.nc

        def din(name, shape, dt=F32):
            return nc.dram_tensor(name, list(shape), dt, kind="ExternalInput").ap()

        def dscr(name, shape, dt=BF16):
            return nc.dram_tensor(name, list(shape), dt).ap()

        i = self.i = {}
        i["x"] = din("x_cm", (D, T))
        i["alog"] = din("alog_r", (P, GN))
        i["inwT"] = din("inwT", (D, 2 * D))
        i["brow_l0"] = din("brow_l0", (1, 1024))
        i["inb_xa"] = din("inb_xa_r", (P, G))
        i["inb_z"] = din("inb_z_r", (P, G))
        i["convw"] = din("convw_r", (P, 4 * G))
        i["convb"] = din("convb_r", (P, G))
        i["bcwT"] = din("bcwT", (D, 2 * NST))
        i["bcb"] = din("bcb", (1, 2 * NST))
        i["outwT"] = din("outwT", (D, D))
        i["lnw"] = din("lnw_r", (P, G))
        i["lnb"] = din("lnb_r", (P, G))
        i["ones"] = din("ones", (P, 512))
        i["identity"] = din("identity", (P, P))
        if self.stage >= 2:
            i["ffg"] = din("ffg_r", (P, G))
            i["ffb"] = din("ffb_r", (P, G))
            i["fc1q"] = din("fc1q", (D, FF), I32)
            i["fc1s"] = din("fc1s", (8, FF))
            i["fc1b"] = din("fc1b_r", (P, GF))
            i["fc2q"] = din("fc2q", (FF, D), I32)
            i["fc2s"] = din("fc2s", (32, D))
            i["fc2b"] = din("fc2b_r", (P, G))
        if self.stage >= 3:
            i["wqT"] = din("wqT", (D, D))
            i["wkT"] = din("wkT", (D, D))
            i["wvT"] = din("wvT", (D, D))
            i["wg1T"] = din("wg1T", (D, P))
            i["wg2T"] = din("wg2T", (P, D))
            i["gnorm"] = din("gnorm_r", (P, H))
            i["woT"] = din("woT", (D, D))
            i["trimask"] = din("trimask", (P, P))
        if self.stage >= 4:
            i["w1T"] = din("w1T", (D, FF))
            i["w1b"] = din("w1b_r", (P, GF))
            i["w2T"] = din("w2T", (FF, D))
            i["w2b"] = din("w2b_r", (P, G))

        self.y_out = nc.dram_tensor("y_cm", [D, T], F32, kind="ExternalOutput").ap()

        s = self.scr = {}
        s["inw"] = dscr("s_inw", (P, 16 * G * P))
        s["outw"] = dscr("s_outw", (P, G * G * P))
        s["bm"] = dscr("s_bm", (1, T * 2 * NST), F32)
        if self.stage >= 2:
            s["fc1"] = dscr("s_fc1", (P, GF * G * P))
            s["fc2"] = dscr("s_fc2", (P, G * GF * P))
        if self.stage >= 3:
            s["wq"] = dscr("s_wq", (P, G * G * P))
            s["wk"] = dscr("s_wk", (P, G * G * P))
            s["wv"] = dscr("s_wv", (P, G * G * P))
            s["wo"] = dscr("s_wo", (P, G * G * P))
            s["wg1"] = dscr("s_wg1", (P, G * P))
            s["wg2"] = dscr("s_wg2", (P, G * P))
        if self.stage >= 4:
            s["w1"] = dscr("s_w1", (P, GF * G * P))
            s["w2"] = dscr("s_w2", (P, G * GF * P))

        sb = self.sb = {}

        def sbuf(name, shape, dt=F32):
            sb[name] = self._enter(nc.sbuf_tensor(name, list(shape), dt))

        def psum(name, shape, dt=F32):
            sb[name] = self._enter(nc.psum_tensor(name, list(shape), dt))

        sbuf("stream", (P, G * T))
        sbuf("xab", (P, 3 * G * (T1 + 3)))
        sbuf("szb", (P, 2 * G * T1))
        sbuf("xcb", (P, 2 * G * T1))
        sbuf("xcbb", (P, G * T1), BF16)
        sbuf("ln0b", (P, 2 * G * T1), BF16)
        sbuf("ln0f", (P, 3 * G * T1))
        sbuf("bbb", (P, 3 * T2 * NST))
        sbuf("cmb", (P, 3 * T2 * NST))
        sbuf("bxb", (P, 2 * T2 * GN), BF16)
        sbuf("rbuf", (P, T2 * RW))
        sbuf("sprime", (P, GN))
        sbuf("yb", (P, G * T1))
        sbuf("ymb", (P, 2 * G * T1), BF16)
        sbuf("hb", (P, 2 * G * T1))
        sbuf("sq", (P, G * T1))
        sbuf("bmcm", (P, 2 * NST))
        sbuf("wst", (P, 6 * G * P), BF16)
        sbuf("stage", (P, 1024))
        sbuf("stageb", (P, 1024), BF16)
        sbuf("onesT", (P, P))
        sbuf("identf", (P, P))
        sbuf("identb", (P, P), BF16)
        sbuf("amisc", (P, GN))
        sbuf("smallA", (P, 128))
        sbuf("smallB", (P, 64))
        sbuf("bcbrow", (1, 2 * NST))
        sbuf("strip", (1, 1536))
        sbuf("convm", (P, 2 * 4 * T1))
        sbuf("cst", (P, 4))
        sbuf("brow", (1, 3072))
        sbuf("sgx", (P, 1024))
        sbuf("pin", (P, 512))
        sbuf("psc", (P, 512))
        sbuf("ptmp", (P, 512))
        sbuf("pacc", (P, 256))

        psum("ps_a", (P, 512))
        psum("ps_cb0", (P, 512))
        psum("ps_cb1", (P, 512))
        psum("ps_w0", (P, 512))
        psum("ps_w1", (P, 512))
        psum("ps_m0", (P, 512))
        psum("ps_m1", (P, 512))
        psum("ps_x", (P, 512))

        self.semh = {}
        for nm in ["c_v", "c_s", "c_t", "c_g", "c_cb", "d_in", "d_c", "d_cs", "d_w0", "d_w1",
                   "d_w2", "d_w3", "d_w4", "d_w5", "d_b0", "d_b1", "d_bm", "d_out", "d_misc",
                   "d_pin", "d_po"]:
            self.semh[nm] = self._enter(nc.semaphore(nm))
        self.block = self._enter(nc.Block())
        self._pump_tasks()

    def _mk_cast(self, src, dst4, k, c0, w, endmark=None):
        sb = self.sb

        def dma(me, j):
            sl = (j % 2) * 256
            me.dma("scalar", sb["pin"][:, sl:sl + w], src[k * P:(k + 1) * P, c0:c0 + w],
                   "d_pin")

        def cmp(me, j, pc):
            sl = (j % 2) * 256
            slb = (j % 2) * 512
            me.wait("scalar", "d_pin", pc["din"])
            me.on("scalar", lambda sc: sc.activation(
                out=sb["psc"].bitcast(BF16)[:, slb:slb + w], in_=sb["pin"][:, sl:sl + w],
                func=AF.Copy, bias=0.0, scale=1.0), inc="c_s")
            me.mark(("ppc", j), "c_s")

        def fin(me, j, pc):
            slb = (j % 2) * 512
            m0, nm = c0 // P, w // P
            me.dma("scalar", dst4[:, m0:m0 + nm, k, :],
                   sb["psc"].bitcast(BF16)[:, slb:slb + w].rearrange("p (m q) -> p m q", q=P),
                   "d_po")
            pc["dout"] = me.cnt["d_po"]
            if endmark is not None:
                me.mark(("pd", endmark), "d_po")
        return {"dma": dma, "cmp": cmp, "fin": fin}

    def _mk_dq(self, qsrc, ssrc, dst4, k, c0, w, eng, endmark=None):
        sb = self.sb
        eng = "vector"
        csem = "c_v"

        def dma(me, j):
            sl = (j % 2) * 256
            me.dma("scalar", sb["pin"].bitcast(I32)[:, sl:sl + w],
                   qsrc[k * P:(k + 1) * P, c0:c0 + w], "d_pin")
            me.dma("scalar", sb["psc"][:, sl:sl + w],
                   ssrc[k:k + 1, c0:c0 + w].broadcast_to([P, w]), "d_pin")

        def cmp(me, j, pc):
            sl = (j % 2) * 256
            slb = (j % 2) * 512
            me.wait("scalar", "d_pin", pc["din"])
            me.on("scalar", lambda sc: sc.activation(
                out=sb["ptmp"][:, sl:sl + w], in_=sb["pin"].bitcast(I32)[:, sl:sl + w],
                func=AF.Copy, bias=-128.0, scale=1.0), inc="c_s")
            cs = me.cnt["c_s"]
            me.wait(eng, "c_s", cs)
            me.wait(eng, "d_pin", pc["din"])
            me.on(eng, lambda e: e.tensor_tensor(
                out=sb["pin"].bitcast(BF16)[:, slb:slb + w], in0=sb["ptmp"][:, sl:sl + w],
                in1=sb["psc"][:, sl:sl + w], op=AL.mult), inc=csem)
            me.mark(("ppc", j), csem)
            pc["cval"] = me.cnt[csem]

        def fin(me, j, pc):
            slb = (j % 2) * 512
            me.wait("scalar", csem, pc["cval"])
            m0, nm = c0 // P, w // P
            me.dma("scalar", dst4[:, m0:m0 + nm, k, :],
                   sb["pin"].bitcast(BF16)[:, slb:slb + w].rearrange("p (m q) -> p m q", q=P),
                   "d_po")
            pc["dout"] = me.cnt["d_po"]
            if endmark is not None:
                me.mark(("pd", endmark), "d_po")
        return {"dma": dma, "cmp": cmp, "fin": fin}

    def _mk_t1(self, src, k, c0, w, aidx, wn, last=False):
        sb = self.sb

        def dma(me, j):
            sl = (j % 2) * 256
            me.dma("scalar", sb["pin"][:, sl:sl + w], src[k * P:(k + 1) * P, c0:c0 + w],
                   "d_pin")

        def cmp(me, j, pc):
            sl = (j % 2) * 256
            me.wait("scalar", "d_pin", pc["din"])
            me.on("scalar", lambda sc: sc.activation(
                out=sb["ptmp"][:, sl:sl + w], in_=sb["pin"][:, sl:sl + w],
                func=AF.Abs, bias=0.0, scale=1.0,
                accum_out=sb["pacc"][:, aidx:aidx + 1]), inc="c_s")
            me.mark(("ppc", j), "c_s")
            if last:
                me.mark(("t1done", wn), "c_s")
        return {"dma": dma, "cmp": cmp, "fin": None}

    def _mk_treduce(self, wn, a0, nacc, nelem):
        sb = self.sb
        cA, cB = 300 + wn, 304 + 4 * wn

        def cmp(me, j, pc):
            me.wmark("vector", ("t1done", wn))
            me.on("vector", lambda v: v.tensor_reduce(
                out=sb["bmcm"][:, wn:wn + 1], in_=sb["pacc"][:, a0:a0 + nacc],
                axis=AX.X, op=AL.add), inc="c_v")
            cv = me.cnt["c_v"]
            me.wait("tensor", "c_v", cv)
            me.wmark("tensor", ("blk1", NC1 - 1))
            me.on("tensor", lambda t: t.matmul(
                sb["ps_x"][0:1, cA:cA + 1], sb["bmcm"][:, wn:wn + 1],
                sb["onesT"][:, 0:1], start=True, stop=True, skip_group_check=True),
                inc="c_t")
            ct = me.cnt["c_t"]
            me.wait("vector", "c_t", ct)
            me.on("vector", lambda v: v.tensor_scalar(
                out=sb["bmcm"][0:1, 2 + wn:3 + wn], in0=sb["ps_x"][0:1, cA:cA + 1],
                scalar1=1.0 / nelem, scalar2=1e-8, op0=AL.mult, op1=AL.max), inc="c_v")
            me.on("vector", lambda v: v.reciprocal(
                out=sb["bmcm"][0:1, 2 + wn:3 + wn],
                in_=sb["bmcm"][0:1, 2 + wn:3 + wn]), inc="c_v")
            cv2 = me.cnt["c_v"]
            me.wait("tensor", "c_v", cv2)
            me.on("tensor", lambda t: t.matmul(
                sb["ps_x"][:, cB:cB + 1], sb["onesT"][0:1, 0:P],
                sb["bmcm"][0:1, 2 + wn:3 + wn], start=True, stop=True,
                skip_group_check=True), inc="c_t")
            ct2 = me.cnt["c_t"]
            me.wait("vector", "c_t", ct2)
            me.on("vector", lambda v: v.tensor_copy(
                out=sb["bmcm"][:, 4 + wn:5 + wn], in_=sb["ps_x"][:, cB:cB + 1]),
                inc="c_v")
            me.mark(("trec", wn), "c_v")
            me.mark(("ppc", j), "c_v")
        return {"dma": None, "cmp": cmp, "fin": None,
                "hold": ("blk1", NC1 - 1)}

    def _mk_t2(self, src, dst4, k, c0, w, wn, eng, endmark=None):
        sb = self.sb
        eng = "vector"
        csem = "c_v"

        def dma(me, j):
            sl = (j % 2) * 256
            me.dma("scalar", sb["pin"][:, sl:sl + w], src[k * P:(k + 1) * P, c0:c0 + w],
                   "d_pin")

        def cmp(me, j, pc):
            sl = (j % 2) * 256
            slb = (j % 2) * 512
            me.wait("scalar", "d_pin", pc["din"])
            me.wmark("scalar", ("trec", wn))
            recip = sb["bmcm"][:, 4 + wn:5 + wn]
            me.on("scalar", lambda sc: sc.activation(
                out=sb["ptmp"][:, sl:sl + w], in_=sb["pin"][:, sl:sl + w],
                func=AF.Abs, bias=0.0, scale=recip))
            me.on("scalar", lambda sc: sc.activation(
                out=sb["psc"][:, sl:sl + w], in_=sb["pin"][:, sl:sl + w],
                func=AF.Sign, bias=0.0, scale=1.0), inc="c_s")
            cs = me.cnt["c_s"]
            me.wait(eng, "c_s", cs)
            me.on(eng, lambda e: e.tensor_scalar(
                out=sb["ptmp"][:, sl:sl + w], in0=sb["ptmp"][:, sl:sl + w],
                scalar1=0.5, scalar2=None, op0=AL.is_gt))
            me.on(eng, lambda e: e.tensor_tensor(
                out=sb["pin"].bitcast(BF16)[:, slb:slb + w], in0=sb["ptmp"][:, sl:sl + w],
                in1=sb["psc"][:, sl:sl + w], op=AL.mult), inc=csem)
            me.mark(("ppc", j), csem)
            pc["cval"] = me.cnt[csem]

        def fin(me, j, pc):
            slb = (j % 2) * 512
            me.wait("scalar", csem, pc["cval"])
            m0, nm = c0 // P, w // P
            me.dma("scalar", dst4[:, m0:m0 + nm, k, :],
                   sb["pin"].bitcast(BF16)[:, slb:slb + w].rearrange("p (m q) -> p m q", q=P),
                   "d_po")
            pc["dout"] = me.cnt["d_po"]
            if endmark is not None:
                me.mark(("pd", endmark), "d_po")
        return {"dma": dma, "cmp": cmp, "fin": fin}

    def _pump_tasks(self):
        self.ptasks = []
        i, s = self.i, self.scr

        def pieces(ncols, wmax=256):
            out = []
            c0 = 0
            while c0 < ncols:
                out.append((c0, min(wmax, ncols - c0)))
                c0 += wmax
            return out

        def add_dq(qap, sap, dst, kchunks, ncols, eng, name):
            dst4 = dst.rearrange("p (m kk q) -> p m kk q", kk=kchunks, q=P)
            items = [(k, c0, w) for k in range(kchunks) for (c0, w) in pieces(ncols)]
            for n, (k, c0, w) in enumerate(items):
                self.ptasks.append(self._mk_dq(
                    qap, sap, dst4, k, c0, w, eng,
                    endmark=(name if n == len(items) - 1 else None)))

        def add_cast(src, dst, kchunks, ncols, name, cmajor=False):
            dst4 = dst.rearrange("p (m kk q) -> p m kk q", kk=kchunks, q=P)
            if cmajor:
                items = [(k, c0, w) for (c0, w) in pieces(ncols) for k in range(kchunks)]
            else:
                items = [(k, c0, w) for k in range(kchunks) for (c0, w) in pieces(ncols)]
            for n, (k, c0, w) in enumerate(items):
                self.ptasks.append(self._mk_cast(
                    src, dst4, k, c0, w,
                    endmark=(name if n == len(items) - 1 else None)))

        def add_tern(src, dst, kchunks, ncols, wn, a0, name):
            dst4 = dst.rearrange("p (m kk q) -> p m kk q", kk=kchunks, q=P)
            items = [(k, c0, w) for k in range(kchunks) for (c0, w) in pieces(ncols)]
            t1 = []
            for n, (k, c0, w) in enumerate(items):
                t1.append(self._mk_t1(src, k, c0, w, a0 + n, wn,
                                      last=(n == len(items) - 1)))
            t2 = [self._mk_t2(src, dst4, k, c0, w, wn, "gpsimd",
                              endmark=(name if n == len(items) - 1 else None))
                  for n, (k, c0, w) in enumerate(items)]
            red = self._mk_treduce(wn, a0, len(items), kchunks * P * ncols)
            return t1, red, t2

        add_cast(i["inwT"], s["inw"], G, 2 * D, "inw", cmajor=True)
        add_cast(i["outwT"], s["outw"], G, D, "outw", cmajor=True)
        self.n_preload = len(self.ptasks)
        if self.stage < 2:
            return
        add_dq(i["fc1q"], i["fc1s"], s["fc1"], G, FF, "gpsimd", "fc1")
        add_dq(i["fc2q"], i["fc2s"], s["fc2"], GF, D, "vector", "fc2")
        if self.stage >= 3:
            add_cast(i["wqT"], s["wq"], G, D, "wq")
            add_cast(i["wkT"], s["wk"], G, D, "wk")
            add_cast(i["wvT"], s["wv"], G, D, "wv")
            add_cast(i["woT"], s["wo"], G, D, "wo")
            add_cast(i["wg1T"], s["wg1"], G, P, "wg1")
            add_cast(i["wg2T"], s["wg2"], 1, D, "wg2")
        if self.stage >= 4:
            t1a, reda, t2a = add_tern(i["w1T"], s["w1"], G, FF, 0, 0, "w1")
            t1b, redb, t2b = add_tern(i["w2T"], s["w2"], GF, D, 1, 128, "w2")
            self.ptasks += t1a + t1b + [reda, redb] + t2a + t2b

    def pump_step(self, n=1):
        for _ in range(n):
            self._pump_step1()

    def _pump_step1(self):
        tl = self.ptasks
        n = len(tl)
        jh = min(self.p_cmp, self.p_dma)
        if jh < n and tl[jh].get("hold") is not None and tl[jh]["hold"] not in self.marks:
            return
        if self.p_out < n and self.p_out < self.p_cmp:
            j = self.p_out
            if tl[j]["fin"] is not None:
                tl[j]["fin"](self, j, self.pstate[j])
            self.p_out += 1
        if self.p_cmp < n and self.p_cmp < self.p_dma:
            j = self.p_cmp
            tl[j]["cmp"](self, j, self.pstate[j])
            self.p_cmp += 1
        j = self.p_dma
        if j < n:
            self.pstate[j] = {"din": 0}
            if tl[j]["dma"] is not None:
                if j >= 2:
                    self.wmark("scalar", ("ppc", j - 2))
                    self.wait("scalar", "d_po", self.pstate[j - 2].get("dout", 0))
                tl[j]["dma"](self, j)
                self.pstate[j]["din"] = self.cnt["d_pin"]
            self.p_dma += 1

    def pump_drain(self):
        guard = 0
        while self.p_out < len(self.ptasks):
            self.pump_step()
            guard += 1
            if guard > 10000:
                raise RuntimeError("pump_drain stuck")

    def _reset(self, eng_name, eng):
        self.eng_name = eng_name
        self.eng = eng
        self.cnt = {k: 0 for k in self.semh}
        self.marks = {}
        self.pmgate = {0: None, 1: None}
        self.sqgate = None
        self._selfw = {}
        self.slot_gate = {}
        self._mk = 0
        self.p_dma = 0
        self.p_cmp = 0
        self.p_out = 0
        self.pstate = {}
        self.scanfeed = {}
        self.wstfeed = {}

    def on(self, name, fn, inc=None, n=1):
        if self.eng_name == name:
            if inc is not None and self.cnt[inc] > self._selfw.get(inc, 0):
                self.eng.wait_ge(self.semh[inc], self.cnt[inc])
                self._selfw[inc] = self.cnt[inc]
            ins = fn(self.eng)
            if inc is not None:
                ins.then_inc(self.semh[inc], n)
        if inc is not None:
            self.cnt[inc] += n
        return self.cnt.get(inc)

    def wait(self, name, sem, val):
        if val is None or val <= 0:
            return
        if self.eng_name == name:
            self.eng.wait_ge(self.semh[sem], val)

    def dma(self, q, out, in_, sem="d_w", **kw):
        if self.eng_name == q:
            self.eng.dma_start(out=out, in_=in_, **kw).then_inc(self.semh[sem], 16)
        self.cnt[sem] += 16
        return self.cnt[sem]

    def mark(self, key, sem):
        self.marks[key] = (sem, self.cnt[sem])

    def wmark(self, name, key):
        if key in self.marks:
            sem, val = self.marks[key]
            self.wait(name, sem, val)

    def walk(self):
        self.prologue()
        self.l0_pipeline()
        if self.stage >= 2:
            self.ffn0()
        if self.stage >= 3:
            self.l1_attn()
        if self.stage >= 4:
            self.ffn1()
        self.epilogue()

    def prologue(self):
        sb, i = self.sb, self.i
        for g in range(G):
            self.dma("sync", sb["stream"][:, g * T:(g + 1) * T], i["x"][g * P:(g + 1) * P, :], "d_in")
        self.dma("sync", sb["onesT"][:], i["ones"][:, 0:P], "d_misc")
        self.dma("sync", sb["identf"][:], i["identity"], "d_misc")
        self.dma("sync", sb["amisc"][:], i["alog"], "d_misc")
        self.dma("sync", sb["brow"][0:1, 0:1024], i["brow_l0"], "d_misc")
        self.dma("sync", sb["smallA"][:, 40:48], i["inb_z"], "d_misc")
        self.dma("sync", sb["smallA"][:, 48:56], i["inb_xa"], "d_misc")
        self.dma("sync", sb["smallA"][:, 0:32], i["convw"], "d_misc")
        self.dma("sync", sb["smallA"][:, 32:40], i["convb"], "d_misc")
        self.dma("sync", sb["smallB"][:, 0:8], i["lnw"], "d_misc")
        self.dma("sync", sb["smallB"][:, 8:16], i["lnb"], "d_misc")
        self.dma("sync", sb["bcbrow"][:], i["bcb"], "d_misc")
        if self.stage >= 2:
            self.dma("sync", sb["smallB"][:, 16:24], i["ffg"], "d_misc")
            self.dma("sync", sb["smallB"][:, 24:32], i["ffb"], "d_misc")
            self.dma("sync", sb["smallB"][:, 32:40], i["fc2b"], "d_misc")
            self.dma("sync", sb["smallA"][:, 64:96], i["fc1b"], "d_misc")
        if self.stage >= 3:
            self.dma("sync", sb["smallB"][:, 40:48], i["gnorm"], "d_misc")
        if self.stage >= 4:
            self.dma("sync", sb["smallB"][:, 48:56], i["w2b"], "d_misc")
            self.dma("sync", sb["smallA"][:, 96:128], i["w1b"], "d_misc")
        nmisc = self.cnt["d_misc"]

        self.wait("vector", "d_misc", nmisc)
        self.on("vector", lambda v: v.tensor_scalar(out=sb["amisc"][:], in0=sb["amisc"][:],
                                                    scalar1=2.0, scalar2=-5.0, op0=AL.min, op1=AL.max),
                inc="c_v")
        self.wait("scalar", "c_v", self.cnt["c_v"])
        self.on("scalar", lambda sc: sc.activation(out=sb["amisc"][:], in_=sb["amisc"][:],
                                                   func=AF.Exp, bias=0.0, scale=1.0), inc="c_s")
        self.wait("vector", "c_s", self.cnt["c_s"])
        self.on("vector", lambda v: v.tensor_scalar(out=sb["ps_a"][:, 0:GN], in0=sb["amisc"][:],
                                                    scalar1=-1.0, scalar2=None, op0=AL.mult), inc="c_v")
        self.on("vector", lambda v: v.memset(sb["sprime"][:], 0.0), inc="c_v")
        self.on("vector", lambda v: v.memset(sb["rbuf"][:], 0.0), inc="c_v")
        self.on("vector", lambda v: v.memset(sb["cst"][:, 0:1], EPS), inc="c_v")
        
        self.on("vector", lambda v: v.memset(sb["cst"][:, 1:2], 0.5), inc="c_v")
        self.on("vector", lambda v: v.memset(sb["cst"][:, 2:3], 0.08838834764831845), inc="c_v")
        self.on("vector", lambda v: v.tensor_copy(out=sb["identb"][:], in_=sb["identf"][:]), inc="c_v")
        self.on("vector", lambda v: v.tensor_reduce(
            out=sb["smallA"][:, 56:64],
            in_=sb["smallA"][:, 0:32].rearrange("p (g j) -> p g j", j=4),
            axis=AX.X, op=AL.add), inc="c_v")
        self.on("vector", lambda v: v.tensor_tensor(
            out=sb["smallA"][:, 56:64], in0=sb["smallA"][:, 56:64],
            in1=sb["smallA"][:, 48:56], op=AL.mult), inc="c_v")
        self.on("vector", lambda v: v.tensor_tensor(
            out=sb["smallA"][:, 32:40], in0=sb["smallA"][:, 32:40],
            in1=sb["smallA"][:, 56:64], op=AL.add), inc="c_v")
        self.mark("init_v", "c_v")
        self.on("gpsimd", lambda g: g.memset(sb["xab"][:], 0.0), inc="c_g")
        self.mark("init_g", "c_g")


        self.wait("sync", "d_cs", self.cnt["d_cs"])
        for k in range(G):
            self.dma("sync", sb["stage"][:, k * 64:(k + 1) * 64], i["bcwT"][k * P:(k + 1) * P, :], "d_c")
        self.wait("vector", "d_c", self.cnt["d_c"])
        self.wait("vector", "d_cs", self.cnt["d_cs"])
        self.on("vector", lambda v: v.tensor_copy(out=sb["stageb"][:, 0:512],
                                                  in_=sb["stage"][:, 0:512]), inc="c_v")
        self.mark("bcw", "c_v")

    def _cast_tiled(self, src, dst, kchunks, mchunks):
        sb = self.sb
        ncols = mchunks * P
        dst4 = dst.rearrange("p (m kk q) -> p m kk q", kk=kchunks, q=P)
        for k in range(kchunks):
            for c0 in range(0, ncols, 1024):
                w = min(1024, ncols - c0)
                self.wait("sync", "d_cs", self.cnt["d_cs"])
                self.dma("sync", sb["stage"][:, 0:w], src[k * P:(k + 1) * P, c0:c0 + w], "d_c")
                self.wait("vector", "d_c", self.cnt["d_c"])
                self.wait("vector", "d_cs", self.cnt["d_cs"])
                self.on("vector", lambda v, w=w: v.tensor_copy(out=sb["stageb"][:, 0:w],
                                                               in_=sb["stage"][:, 0:w]), inc="c_v")
                self.wait("sync", "c_v", self.cnt["c_v"])
                m0, nm = c0 // P, w // P
                out_ap = dst4[:, m0:m0 + nm, k, :]
                in_ap = sb["stageb"][:, 0:w].rearrange("p (m q) -> p m q", q=P)
                self.dma("sync", out_ap, in_ap, "d_cs")

    def l0_pipeline(self):
        self.wait("scalar", "d_in", 16 * G)
        self.wait("tensor", "d_in", 16 * G)
        guard = 0
        while self.p_out < self.n_preload:
            self.pump_step()
            guard += 1
            if guard > 10000:
                raise RuntimeError("preload pump stuck")
        for w in range(-3, NC1 + 2):
            for k in range(NSC):
                self._slot(w, k)

    def _slot(self, w, k):
        cf, kf = w + (k + 2) // NSC, (k + 2) % NSC
        cc, kc = w + (k + 1) // NSC, (k + 1) % NSC
        if 0 <= cf < NC1:
            self._scan_feed(cf, kf)
        if 0 <= cc < NC1:
            self._scan_bx(cc, kc)
        if k == 0 and 0 <= w - 1 < NC1:
            self._ym(w - 1)
        c_op = w - 2
        if 0 <= c_op < NC1 and k < 7:
            self._outproj_dma(c_op, k + 1)
        if k == 15 and 0 <= w - 1 < NC1:
            self._outproj_dma(w - 1, 0)
        c_ipn = w + 2 + (k + 1) // NSC
        if 0 <= c_ipn < NC1:
            self._inproj_dma(c_ipn, (k + 1) % NSC)
        if w == -3 and k == 0 and 0 <= w + 2 < NC1:
            self._inproj_dma(w + 2, 0)
        if 0 <= c_op < NC1:
            if k == 0:
                self._lnr_a(c_op)
            elif k == 1:
                self._lnr_b(c_op)
            if k < 8:
                self._outproj_tile(c_op, k)
            if 2 <= k < 10:
                self._hadd(c_op, k - 2)
            if k == 9:
                h0 = (c_op % 2) * G * T1
                hsrc = lambda g: self.sb["hb"][:, h0 + g * T1:h0 + (g + 1) * T1]
                self._lnst_a(hsrc, "ps_w0" if c_op % 2 == 0 else "ps_w1",
                             ("lnh", c_op), 512, src_gate=("hadd", c_op))
            elif k == 10:
                self._lnst_b("ps_w0" if c_op % 2 == 0 else "ps_w1", ("lnh", c_op), 512)
            elif k >= 12:
                self._lnh_apply(c_op, (k - 12) * 2)
        if 0 <= w < NC1:
            self._scan_steps(w, k)
        c_ip = w + 2
        if 0 <= c_ip < NC1:
            self._inproj_tile(c_ip, k)
        c_cv = w + 1
        if 0 <= c_cv < NC1:
            if 1 <= k <= 8:
                self._conv_g(c_cv, k - 1)
            elif k == 10:
                self._bmcm(c_cv)
        c_ln = w + 3
        if 0 <= c_ln < NC1:
            if k == 11:
                src = lambda g, c_ln=c_ln: self.sb["stream"][
                    :, g * T + c_ln * T1:g * T + c_ln * T1 + T1]
                self._lnst_a(src, "ps_w0" if c_ln % 2 == 0 else "ps_w1",
                             ("ln0", c_ln), 1024)
            elif k == 12:
                self._lnst_b("ps_w0" if c_ln % 2 == 0 else "ps_w1", ("ln0", c_ln), 1024)
            elif k in (14, 15):
                self._ln0_apply(c_ln, (k - 14) * 4)
        if w == NC1 - 4 and k == 9:
            sb = self.sb
            xa0 = ((NC1 - 1) % 3) * G * (T1 + 3)
            self.wmark("vector", ("convg", NC1 - 4, G - 1))
            self.on("vector", lambda v: v.memset(
                sb["xab"][:, xa0:xa0 + G * (T1 + 3)].rearrange(
                    "p (g q) -> p g q", g=G)[:, :, T1 + 2:T1 + 3], 0.0), inc="c_v")
            for g in range(G):
                self.mark(("halo", NC1 - 1, g), "c_v")
        self.pump_step(2)

    def _scan_feed(self, c, kf):
        sb, s = self.sb, self.scr
        ki = c * NSC + kf
        par = ki % 3
        W = T2 * NST
        t0 = c * T1 + kf * T2
        base = c * T1 * 2 * NST
        off = (t0 - c * T1) * NST
        bsem = "d_b0" if ki % 2 == 0 else "d_b1"
        self.wmark("sync", ("bmdram", c))
        self.wmark("sync", ("bx", ki - 3))
        self.wmark("sync", ("scand", ki - 3))
        self.dma("sync", sb["bbb"][:, par * W:(par + 1) * W],
                 s["bm"][0:1, base + off:base + off + W].broadcast_to([P, W]), bsem)
        self.dma("sync", sb["cmb"][:, par * W:(par + 1) * W],
                 s["bm"][0:1, base + T1 * NST + off:base + T1 * NST + off + W]
                 .broadcast_to([P, W]), bsem)
        self.scanfeed[ki] = self.cnt[bsem]

    def _scan_bx(self, c, kc):
        sb = self.sb
        ki = c * NSC + kc
        par = ki % 2
        fpar = ki % 3
        W = T2 * NST
        xc0 = (c % 2) * G * T1
        bsem = "d_b0" if ki % 2 == 0 else "d_b1"
        self.wait("gpsimd", bsem, self.scanfeed[ki])
        self.wmark("gpsimd", ("xc", c))
        self.wmark("gpsimd", ("scand", ki - 2))
        bb3 = sb["bbb"][:, fpar * W:(fpar + 1) * W].rearrange(
            "p (u n) -> p u n", n=NST).unsqueeze(2).broadcast_to([P, T2, G, NST])
        xc3 = sb["xcb"][:, xc0:xc0 + G * T1].rearrange("p (g q) -> p g q", g=G)[
            :, :, kc * T2:(kc + 1) * T2].transpose([0, 2, 1]).unsqueeze(3).broadcast_to(
            [P, T2, G, NST])
        bx3 = sb["bxb"][:, par * T2 * GN:(par + 1) * T2 * GN].rearrange(
            "p (u g n) -> p u g n", g=G, n=NST)
        self.on("gpsimd", lambda gp, bx3=bx3, bb3=bb3, xc3=xc3: gp.tensor_tensor(
            out=bx3, in0=bb3, in1=xc3, op=AL.mult), inc="c_g")
        self.mark(("bx", ki), "c_g")

    def _scan_steps(self, c, k):
        sb = self.sb
        ki = c * NSC + k
        par = ki % 2
        fpar = ki % 3
        W = T2 * NST
        bsem = "d_b0" if ki % 2 == 0 else "d_b1"
        self.wait("vector", bsem, self.scanfeed[ki])
        self.wmark("vector", ("bx", ki))
        self.wmark("vector", ("extract", ki - 1))
        if ki == 0:
            self.wmark("vector", "init_v")
        for u in range(T2):
            bxo = par * T2 * GN + u * GN
            self.on("vector", lambda v: v.tensor_tensor(
                out=sb["ps_a"][:, 256:256 + GN], in0=sb["sprime"][:],
                in1=sb["ps_a"][:, 0:GN], op=AL.mult))
            self.on("vector", lambda v, bxo=bxo: v._custom_dve(
                ADDCLIP, out=sb["sprime"][:], in0=sb["bxb"][:, bxo:bxo + GN],
                in1=sb["ps_a"][:, 256:256 + GN], s0=-10.0, s1=10.0))
            ro = u * RW + 1
            self.on("vector", lambda v, u=u, ro=ro, fpar=fpar: v._custom_dve(
                MULSCAN,
                out=sb["rbuf"][:, ro:ro + GN].rearrange("p (g n) -> p g n", n=NST),
                in0=sb["sprime"][:].rearrange("p (g n) -> p g n", n=NST),
                in1=sb["cmb"][:, fpar * W + u * NST:fpar * W + (u + 1) * NST]
                .unsqueeze(1).broadcast_to([P, G, NST])),
                    inc=("c_v" if u == T2 - 1 else None))
        self.mark(("scand", ki), "c_v")
        self.wmark("gpsimd", ("scand", ki))
        r3 = sb["rbuf"][:].rearrange("p (u w) -> p u w", w=RW)
        yv = sb["yb"][:, 0:G * T1].rearrange("p (g q) -> p g q", g=G)[
            :, :, k * T2:(k + 1) * T2].transpose([0, 2, 1])
        self.on("gpsimd", lambda gp, r3=r3, yv=yv: gp.tensor_tensor(
            out=yv, in0=r3[:, :, NST:GN + 1:NST], in1=r3[:, :, 0:GN:NST], op=AL.subtract),
            inc="c_g")
        self.mark(("extract", ki), "c_g")
        if k == NSC - 1:
            self.mark(("scanchunk", c), "c_g")

    def _szb(self, c):
        if c % 3 < 2:
            return self.sb["szb"][:, (c % 3) * G * T1:(c % 3 + 1) * G * T1]
        return self.sb["ln0f"][:, 0:G * T1]

    def _ym(self, c):
        sb = self.sb
        szv = self._szb(c)
        y0 = (c % 2) * G * T1
        self.wmark("gpsimd", ("scanchunk", c))
        self.wmark("gpsimd", ("ipz", c))
        self.wmark("gpsimd", ("opw", c - 2, G - 1))
        for g in range(G):
            self.on("gpsimd", lambda gp, g=g, y0=y0, szv=szv: gp.tensor_tensor(
                out=sb["ymb"][:, y0 + g * T1:y0 + (g + 1) * T1],
                in0=sb["yb"][:, g * T1:(g + 1) * T1],
                in1=szv[:, g * T1:(g + 1) * T1], op=AL.mult),
                    inc=("c_g" if g == G - 1 else None))
        self.mark(("ym", c), "c_g")

    def _lnst_a(self, srcf, bank, tag, soff, src_gate=None):
        sb = self.sb
        bk = sb[bank]
        if src_gate is not None:
            self.wmark("scalar", src_gate)
            self.wmark("tensor", src_gate)
        if self.sqgate is not None:
            self.wmark("scalar", self.sqgate)
        for g in range(G):
            self.on("scalar", lambda s, g=g: s.activation(out=sb["sq"][:, g * T1:(g + 1) * T1],
                                                          in_=srcf(g), func=AF.Square,
                                                          bias=0.0, scale=1.0),
                    inc=("c_s" if g == G - 1 else None))
        cs0 = self.cnt["c_s"]
        self.wait("tensor", "c_s", cs0)
        self.wmark("tensor", ("bankfree", bank))
        for g in range(G):
            self.on("tensor", lambda t, g=g: t.matmul(bk[0:1, 0:T1], sb["onesT"][:, 0:1],
                                                      srcf(g), start=(g == 0), stop=(g == G - 1)))
        for g in range(G):
            self.on("tensor", lambda t, g=g: t.matmul(bk[0:1, T1:2 * T1], sb["onesT"][:, 0:1],
                                                      sb["sq"][:, g * T1:(g + 1) * T1],
                                                      start=(g == 0), stop=(g == G - 1)),
                    inc=("c_t" if g == G - 1 else None))
        self.mark(("sqused", tag), "c_t")
        self.sqgate = ("sqused", tag)
        self.marks[("lnsum", tag)] = ("c_t", self.cnt["c_t"])

    def _lnst_b(self, bank, tag, soff):
        sb = self.sb
        bk = sb[bank]
        st = sb["strip"]
        sem, ctv = self.marks[("lnsum", tag)]
        self.wait("vector", sem, ctv)
        self.on("vector", lambda v: v.tensor_scalar(out=st[0:1, soff:soff + T1], in0=bk[0:1, 0:T1],
                                                    scalar1=1.0 / D, scalar2=None, op0=AL.mult))
        self.on("vector", lambda v: v.tensor_scalar(out=st[0:1, soff + T1:soff + 2 * T1],
                                                    in0=bk[0:1, T1:2 * T1],
                                                    scalar1=1.0 / D, scalar2=None, op0=AL.mult))
        self.on("vector", lambda v: v.tensor_tensor(out=st[0:1, soff + 2 * T1:soff + 3 * T1],
                                                    in0=st[0:1, soff:soff + T1],
                                                    in1=st[0:1, soff:soff + T1], op=AL.mult))
        self.on("vector", lambda v: v.tensor_tensor(out=st[0:1, soff + T1:soff + 2 * T1],
                                                    in0=st[0:1, soff + T1:soff + 2 * T1],
                                                    in1=st[0:1, soff + 2 * T1:soff + 3 * T1],
                                                    op=AL.subtract), inc="c_v")
        self.wait("scalar", "c_v", self.cnt["c_v"])
        self.on("scalar", lambda s: s.activation(out=st[0:1, soff + 2 * T1:soff + 3 * T1],
                                                 in_=st[0:1, soff + T1:soff + 2 * T1],
                                                 func=AF.Sqrt, bias=self.sb["cst"][0:1, 0:1],
                                                 scale=1.0), inc="c_s")
        self.wait("vector", "c_s", self.cnt["c_s"])
        self.on("vector", lambda v: v.reciprocal(out=st[0:1, soff + 2 * T1:soff + 3 * T1],
                                                 in_=st[0:1, soff + 2 * T1:soff + 3 * T1]))
        self.on("vector", lambda v: v.tensor_tensor(out=st[0:1, soff + 3 * T1:soff + 4 * T1],
                                                    in0=st[0:1, soff:soff + T1],
                                                    in1=st[0:1, soff + 2 * T1:soff + 3 * T1],
                                                    op=AL.mult), inc="c_v")
        self.wait("tensor", "c_v", self.cnt["c_v"])
        self.on("tensor", lambda t: t.matmul(bk[:, 256:256 + T1], sb["onesT"][0:1, 0:P],
                                             st[0:1, soff + 2 * T1:soff + 3 * T1],
                                             start=True, stop=True))
        self.on("tensor", lambda t: t.matmul(bk[:, 384:384 + T1], sb["onesT"][0:1, 0:P],
                                             st[0:1, soff + 3 * T1:soff + 4 * T1],
                                             start=True, stop=True), inc="c_t")
        self.mark(("lnb", tag), "c_t")

    def _lnstats(self, srcf, bank, tag):
        self._lnst_a(srcf, bank, tag, 0)
        self._lnst_b(bank, tag, 0)
        return self.cnt["c_t"]

    def _lnr_a(self, c):
        src = lambda g: self.sb["stream"][:, g * T + c * T1:g * T + c * T1 + T1]
        sb = self.sb
        bk = sb["ps_x"]
        if self.sqgate is not None:
            self.wmark("scalar", self.sqgate)
        for g in range(G):
            self.on("scalar", lambda s, g=g: s.activation(out=sb["sq"][:, g * T1:(g + 1) * T1],
                                                          in_=src(g), func=AF.Square,
                                                          bias=0.0, scale=1.0),
                    inc=("c_s" if g == G - 1 else None))
        cs0 = self.cnt["c_s"]
        self.wait("tensor", "c_s", cs0)
        self.wmark("tensor", ("hadd", c - 1))
        self.wmark("tensor", ("bmev", c + 2))
        for g in range(G):
            self.on("tensor", lambda t, g=g: t.matmul(bk[0:1, 64:64 + T1], sb["onesT"][:, 0:1],
                                                      src(g), start=(g == 0), stop=(g == G - 1)))
        for g in range(G):
            self.on("tensor", lambda t, g=g: t.matmul(bk[0:1, 64 + T1:64 + 2 * T1],
                                                      sb["onesT"][:, 0:1],
                                                      sb["sq"][:, g * T1:(g + 1) * T1],
                                                      start=(g == 0), stop=(g == G - 1)),
                    inc="c_t")
        self.mark(("sqused", ("lnr", c)), "c_t")
        self.sqgate = ("sqused", ("lnr", c))
        self.marks[("lnsum", ("lnr", c))] = ("c_t", self.cnt["c_t"])

    def _lnr_b(self, c):
        sb = self.sb
        bk = sb["ps_x"]
        st = sb["strip"]
        soff = 0
        sem, ctv = self.marks[("lnsum", ("lnr", c))]
        self.wait("vector", sem, ctv)
        self.on("vector", lambda v: v.tensor_scalar(out=st[0:1, soff:soff + T1],
                                                    in0=bk[0:1, 64:64 + T1],
                                                    scalar1=1.0 / D, scalar2=None, op0=AL.mult))
        self.on("vector", lambda v: v.tensor_scalar(out=st[0:1, soff + T1:soff + 2 * T1],
                                                    in0=bk[0:1, 64 + T1:64 + 2 * T1],
                                                    scalar1=1.0 / D, scalar2=None, op0=AL.mult))
        self.on("vector", lambda v: v.tensor_tensor(out=st[0:1, soff + 2 * T1:soff + 3 * T1],
                                                    in0=st[0:1, soff:soff + T1],
                                                    in1=st[0:1, soff:soff + T1], op=AL.mult))
        self.on("vector", lambda v: v.tensor_tensor(out=st[0:1, soff + T1:soff + 2 * T1],
                                                    in0=st[0:1, soff + T1:soff + 2 * T1],
                                                    in1=st[0:1, soff + 2 * T1:soff + 3 * T1],
                                                    op=AL.subtract), inc="c_v")
        self.wait("scalar", "c_v", self.cnt["c_v"])
        self.on("scalar", lambda s: s.activation(out=st[0:1, soff + 2 * T1:soff + 3 * T1],
                                                 in_=st[0:1, soff + T1:soff + 2 * T1],
                                                 func=AF.Sqrt, bias=self.sb["cst"][0:1, 0:1],
                                                 scale=1.0), inc="c_s")
        self.wait("vector", "c_s", self.cnt["c_s"])
        self.on("vector", lambda v: v.reciprocal(out=st[0:1, soff + 2 * T1:soff + 3 * T1],
                                                 in_=st[0:1, soff + 2 * T1:soff + 3 * T1]))
        self.on("vector", lambda v: v.tensor_tensor(out=st[0:1, soff + 3 * T1:soff + 4 * T1],
                                                    in0=st[0:1, soff:soff + T1],
                                                    in1=st[0:1, soff + 2 * T1:soff + 3 * T1],
                                                    op=AL.mult), inc="c_v")
        self.wait("tensor", "c_v", self.cnt["c_v"])
        self.on("tensor", lambda t: t.matmul(bk[:, 256:256 + T1], sb["onesT"][0:1, 0:P],
                                             st[0:1, soff + 2 * T1:soff + 3 * T1],
                                             start=True, stop=True))
        self.on("tensor", lambda t: t.matmul(bk[:, 384:384 + T1], sb["onesT"][0:1, 0:P],
                                             st[0:1, soff + 3 * T1:soff + 4 * T1],
                                             start=True, stop=True), inc="c_t")
        self.mark(("lnrb", c), "c_t")

    def _ln0_apply(self, c, g0):
        sb = self.sb
        bank = "ps_w0" if c % 2 == 0 else "ps_w1"
        bk = sb[bank]
        lb = (c % 2) * G * T1
        self.wmark("vector", ("lnb", ("ln0", c)))
        if g0 == 0:
            self.wmark("vector", ("ipw", c - 2, 15))
        for g in range(g0, g0 + 4):
            src = sb["stream"][:, g * T + c * T1:g * T + c * T1 + T1]
            self.on("vector", lambda v, g=g, src=src: v.tensor_tensor(
                out=sb["stage"][:, 512:512 + T1], in0=src, in1=bk[:, 256:256 + T1],
                op=AL.mult))
            self.on("vector", lambda v, g=g: v.tensor_tensor(
                out=sb["ln0b"][:, lb + g * T1:lb + (g + 1) * T1],
                in0=sb["stage"][:, 512:512 + T1], in1=bk[:, 384:384 + T1], op=AL.subtract),
                    inc=("c_v" if g == g0 + 3 else None))
        if g0 + 4 == G:
            self.mark(("ln0", c), "c_v")
            self.mark(("bankfree", bank), "c_v")

    def _inproj_dma(self, c, m):
        sb, s = self.sb, self.scr
        wslot = (m % 3) * G * P
        dsem = ["d_w0", "d_w1", "d_w4"][m % 3]
        if (m % 3) in self.slot_gate:
            self.wmark("sync", self.slot_gate[m % 3])
        self.wmark("sync", ("pd", "inw"))
        self.dma("sync", sb["wst"][:, wslot:wslot + G * P],
                 s["inw"][:, m * G * P:(m + 1) * G * P], dsem)
        self.wstfeed[("ip", c, m)] = self.cnt[dsem]

    def _inproj_tile(self, c, m):
        sb, s = self.sb, self.scr
        lb = (c % 2) * G * T1
        xa0 = (c % 3) * G * (T1 + 3)
        wslot = (m % 3) * G * P
        dsem = ["d_w0", "d_w1", "d_w4"][m % 3]
        self.wait("tensor", dsem, self.wstfeed[("ip", c, m)])
        self.wmark("tensor", ("ln0", c))
        pbi = m % 2
        pb = sb["ps_m0"] if pbi == 0 else sb["ps_m1"]
        if self.pmgate[pbi]:
            self.wmark("tensor", self.pmgate[pbi])
        for k in range(G):
            self.on("tensor", lambda t, k=k, pb=pb, wslot=wslot, lb=lb: t.matmul(
                pb[:, 0:T1], sb["wst"][:, wslot + k * P:wslot + (k + 1) * P],
                sb["ln0b"][:, lb + k * T1:lb + (k + 1) * T1],
                start=(k == 0), stop=(k == G - 1)), inc=("c_t" if k == G - 1 else None))
        self.mark(("ipw", c, m), "c_t")
        self.slot_gate[m % 3] = ("ipw", c, m)
        ctm = self.cnt["c_t"]
        self.wait("scalar", "c_t", ctm)
        if m < G:
            self.on("scalar", lambda sc, m=m, pb=pb: sc.activation(
                out=sb["xab"][:, xa0 + m * (T1 + 3) + 2:xa0 + m * (T1 + 3) + 2 + T1],
                in_=pb[:, 0:T1], func=AF.Copy, bias=0.0, scale=1.0),
                inc="c_s")
            if c > 0:
                pa = ((c - 1) % 3) * G * (T1 + 3)
                self.on("scalar", lambda sc, m=m: sc.copy(
                    out=sb["xab"][:, xa0 + m * (T1 + 3):xa0 + m * (T1 + 3) + 2],
                    in_=sb["xab"][:, pa + m * (T1 + 3) + T1:pa + m * (T1 + 3) + T1 + 2]))
                self.on("scalar", lambda sc, m=m: sc.copy(
                    out=sb["xab"][:, pa + m * (T1 + 3) + T1 + 2:pa + m * (T1 + 3) + T1 + 3],
                    in_=sb["xab"][:, xa0 + m * (T1 + 3) + 2:xa0 + m * (T1 + 3) + 3]),
                        inc="c_s")
                self.mark(("halo", c - 1, m), "c_s")
        else:
            self.wmark("scalar", ("ym", c - 3))
            szv = self._szb(c)
            self.on("scalar", lambda sc, m=m, pb=pb, szv=szv: sc.activation(
                out=szv[:, (m - G) * T1:(m - G + 1) * T1],
                in_=pb[:, 0:T1], func=AF.Silu,
                bias=sb["smallA"][:, 40 + m - G:41 + m - G], scale=1.0),
                inc="c_s")
            if m == 15:
                self.mark(("ipz", c), "c_s")
        self.mark(("ipev", c, m), "c_s")
        self.pmgate[pbi] = ("ipev", c, m)

    def _conv_g(self, c, g):
        sb = self.sb
        xa0 = (c % 3) * G * (T1 + 3)
        xc0 = (c % 2) * G * T1
        par = (g % 2) * 4 * T1
        self.wmark("scalar", ("halo", c, g))
        self.wmark("scalar", ("convg", c, g - 2))
        if g < 2:
            self.wmark("scalar", ("convg", c - 1, g + 6))
        if g == 0:
            self.wmark("scalar", ("scanchunk", c - 2))
            self.wmark("gpsimd", ("scanchunk", c - 2))
        for j in range(4):
            self.on("scalar", lambda sc, g=g, j=j, par=par: sc.activation(
                out=sb["convm"][:, par + j * T1:par + (j + 1) * T1],
                in_=sb["xab"][:, xa0 + g * (T1 + 3) + j:xa0 + g * (T1 + 3) + j + T1],
                func=AF.Copy, bias=0.0, scale=sb["smallA"][:, g * 4 + j:g * 4 + j + 1]),
                    inc=("c_s" if j == 3 else None))
        csg = self.cnt["c_s"]
        self.wait("gpsimd", "c_s", csg)
        self.on("gpsimd", lambda gp, par=par: gp.tensor_tensor(
            out=sb["convm"][:, par:par + T1], in0=sb["convm"][:, par:par + T1],
            in1=sb["convm"][:, par + T1:par + 2 * T1], op=AL.add))
        self.on("gpsimd", lambda gp, par=par: gp.tensor_tensor(
            out=sb["convm"][:, par:par + T1], in0=sb["convm"][:, par:par + T1],
            in1=sb["convm"][:, par + 2 * T1:par + 3 * T1], op=AL.add))
        self.on("gpsimd", lambda gp, g=g, par=par: gp.tensor_tensor(
            out=sb["xcb"][:, xc0 + g * T1:xc0 + (g + 1) * T1], in0=sb["convm"][:, par:par + T1],
            in1=sb["convm"][:, par + 3 * T1:par + 4 * T1], op=AL.add), inc="c_g")
        self.mark(("convg", c, g), "c_g")
        cg = self.cnt["c_g"]
        self.wait("scalar", "c_g", cg)
        self.on("scalar", lambda sc, g=g: sc.activation(
            out=sb["xcb"][:, xc0 + g * T1:xc0 + (g + 1) * T1],
            in_=sb["xcb"][:, xc0 + g * T1:xc0 + (g + 1) * T1], func=AF.Silu,
            bias=sb["smallA"][:, 32 + g:33 + g], scale=1.0))
        if g == 0:
            self.wmark("scalar", ("bmev", c - 1))
        self.on("scalar", lambda sc, g=g: sc.copy(
            out=sb["xcbb"][:, g * T1:(g + 1) * T1],
            in_=sb["xcb"][:, xc0 + g * T1:xc0 + (g + 1) * T1]),
                inc="c_s")
        if g == G - 1:
            self.mark(("xc", c), "c_s")

    def _bmcm(self, c):
        sb, s = self.sb, self.scr
        xc0 = (c % 2) * G * T1
        self.wmark("tensor", ("xc", c))
        self.wmark("tensor", "bcw")
        self.wmark("tensor", ("bmev", c - 1))
        self.wmark("tensor", ("hadd", c - 3))
        for k in range(G):
            self.on("tensor", lambda t, k=k: t.matmul(
                sb["ps_x"][:, 0:2 * NST], sb["xcbb"][:, k * T1:k * T1 + P],
                sb["stageb"][:, k * 64:(k + 1) * 64], start=(k == 0), stop=False))
        self.on("tensor", lambda t: t.matmul(sb["ps_x"][:, 0:2 * NST], sb["onesT"][0:1, 0:P],
                                             sb["bcbrow"][:], start=False, stop=True), inc="c_t")
        ct = self.cnt["c_t"]
        self.wait("scalar", "c_t", ct)
        self.on("scalar", lambda sc: sc.copy(out=sb["bmcm"][:], in_=sb["ps_x"][:, 0:2 * NST]),
                inc="c_s")
        self.mark(("bmev", c), "c_s")
        self.wmark("sync", ("bmev", c))
        base = c * T1 * 2 * NST
        bmv = s["bm"][0:1, base:base + T1 * NST].rearrange("o (t n) -> (o t) n", n=NST)
        cmv = s["bm"][0:1, base + T1 * NST:base + 2 * T1 * NST].rearrange(
            "o (t n) -> (o t) n", n=NST)
        self.dma("sync", bmv, sb["bmcm"][:, 0:NST], "d_bm")
        self.dma("sync", cmv, sb["bmcm"][:, NST:2 * NST], "d_bm")
        self.mark(("bmdram", c), "d_bm")

    def _outproj_dma(self, c, j):
        sb, s = self.sb, self.scr
        wslot = (3 + (j % 3)) * G * P
        dsem = ["d_w2", "d_w3", "d_w5"][j % 3]
        if (3 + j % 3) in self.slot_gate:
            self.wmark("sync", self.slot_gate[3 + j % 3])
        self.wmark("sync", ("pd", "outw"))
        self.dma("sync", sb["wst"][:, wslot:wslot + G * P],
                 s["outw"][:, j * G * P:(j + 1) * G * P], dsem)
        self.wstfeed[("op", c, j)] = self.cnt[dsem]

    def _outproj_tile(self, c, j):
        sb, s = self.sb, self.scr
        y0 = (c % 2) * G * T1
        h0 = (c % 2) * G * T1
        wslot = (3 + (j % 3)) * G * P
        dsem = ["d_w2", "d_w3", "d_w5"][j % 3]
        self.wait("tensor", dsem, self.wstfeed[("op", c, j)])
        self.wmark("tensor", ("ym", c))
        pbi = j % 2
        pb = sb["ps_m0"] if pbi == 0 else sb["ps_m1"]
        if self.pmgate[pbi]:
            self.wmark("tensor", self.pmgate[pbi])
        for k in range(G):
            self.on("tensor", lambda t, k=k, pb=pb, wslot=wslot, y0=y0: t.matmul(
                pb[:, 0:T1], sb["wst"][:, wslot + k * P:wslot + (k + 1) * P],
                sb["ymb"][:, y0 + k * T1:y0 + (k + 1) * T1], start=(k == 0), stop=False))
        self.on("tensor", lambda t, j=j, pb=pb: t.matmul(
            pb[:, 0:T1], sb["brow"][0:1, j * P:(j + 1) * P],
            sb["onesT"][0:1, 0:T1], start=False, stop=True), inc="c_t")
        self.mark(("opw", c, j), "c_t")
        self.slot_gate[3 + j % 3] = ("opw", c, j)
        ct = self.cnt["c_t"]
        self.wait("scalar", "c_t", ct)
        self.wmark("scalar", ("blk1", c - 2))
        self.on("scalar", lambda sc, j=j, pb=pb, h0=h0: sc.activation(
            out=sb["hb"][:, h0 + j * T1:h0 + (j + 1) * T1], in_=pb[:, 0:T1],
            func=AF.Copy, bias=0.0, scale=1.0), inc="c_s")
        self.mark(("opev", c, j), "c_s")
        self.pmgate[pbi] = ("opev", c, j)

    def _hadd(self, c, j):
        sb = self.sb
        h0 = (c % 2) * G * T1
        src = sb["stream"][:, j * T + c * T1:j * T + c * T1 + T1]
        self.wmark("vector", ("lnrb", c))
        self.wmark("vector", ("opev", c, j))
        self.on("vector", lambda v, src=src: v.tensor_tensor(
            out=sb["stage"][:, 0:T1], in0=src, in1=sb["ps_x"][:, 256:256 + T1], op=AL.mult))
        self.on("vector", lambda v: v.tensor_tensor(
            out=sb["stage"][:, 0:T1], in0=sb["stage"][:, 0:T1],
            in1=sb["ps_x"][:, 384:384 + T1], op=AL.subtract))
        self.on("vector", lambda v, j=j, h0=h0: v.tensor_tensor(
            out=sb["hb"][:, h0 + j * T1:h0 + (j + 1) * T1],
            in0=sb["hb"][:, h0 + j * T1:h0 + (j + 1) * T1],
            in1=sb["stage"][:, 0:T1], op=AL.add), inc="c_v")
        if j == G - 1:
            self.mark(("hadd", c), "c_v")
            self.mark(("h", c), "c_v")

    def _lnh_apply(self, c, g0):
        sb = self.sb
        bank = "ps_w0" if c % 2 == 0 else "ps_w1"
        bk = sb[bank]
        h0 = (c % 2) * G * T1
        tg0 = c * T1
        self.wmark("vector", ("lnb", ("lnh", c)))
        for g in range(g0, g0 + 2):
            hsrc = sb["hb"][:, h0 + g * T1:h0 + (g + 1) * T1]
            self.on("vector", lambda v, g=g, hsrc=hsrc: v.tensor_tensor(
                out=sb["stage"][:, 256:256 + T1], in0=hsrc, in1=bk[:, 256:256 + T1],
                op=AL.mult))
            self.on("vector", lambda v, g=g: v.tensor_tensor(
                out=sb["stage"][:, 256:256 + T1], in0=sb["stage"][:, 256:256 + T1],
                in1=bk[:, 384:384 + T1], op=AL.subtract))
            self.on("vector", lambda v, g=g: v.tensor_scalar(
                out=sb["stage"][:, 256:256 + T1], in0=sb["stage"][:, 256:256 + T1],
                scalar1=sb["smallB"][:, g:g + 1], scalar2=sb["smallB"][:, 8 + g:9 + g],
                op0=AL.mult, op1=AL.add))
            self.on("vector", lambda v, g=g, tg0=tg0: v.tensor_tensor(
                out=sb["stream"][:, g * T + tg0:g * T + tg0 + T1],
                in0=sb["stream"][:, g * T + tg0:g * T + tg0 + T1],
                in1=sb["stage"][:, 256:256 + T1], op=AL.add),
                    inc=("c_v" if g == g0 + 1 else None))
        if g0 + 2 == G:
            self.mark(("blk1", c), "c_v")
            self.mark(("bankfree", bank), "c_v")
            self.sqgate = ("blk1", c)

    def _pf_lnf(self, g):
        sb = self.sb
        if g < 4:
            return sb["xcb"][:, g * 512:(g + 1) * 512]
        return sb["xab"][:, (g - 4) * 512:(g - 3) * 512]

    def _pf_h2(self, g):
        sb = self.sb
        if g < 2:
            return sb["xcbb"][:, g * 512:(g + 1) * 512]
        if g < 6:
            return sb["ln0b"][:, (g - 2) * 512:(g - 1) * 512]
        return sb["yb"][:].bitcast(BF16)[:, (g - 6) * 512:(g - 5) * 512]

    def _pf_y1(self, ff):
        sb = self.sb
        if ff < 8:
            return sb["hb"][:].bitcast(BF16)[:, ff * 512:(ff + 1) * 512]
        if ff < 20:
            return sb["ln0f"][:].bitcast(BF16)[:, (ff - 8) * 512:(ff - 7) * 512]
        if ff < 28:
            return sb["szb"][:].bitcast(BF16)[:, (ff - 20) * 512:(ff - 19) * 512]
        return sb["ymb"][:].bitcast(BF16)[:, (ff - 28) * 512:(ff - 27) * 512]

    def _lnstats512(self, srcf, tag, src_gate=None):
        sb = self.sb
        w0, w1 = sb["ps_w0"], sb["ps_w1"]
        NT = 512
        self.wmark("tensor", ("bankfree", "ps_w0"))
        self.wmark("tensor", ("bankfree", "ps_w1"))
        if src_gate is not None:
            self.wmark("tensor", src_gate)
        for g in range(G):
            self.on("tensor", lambda t, g=g: t.matmul(w0[0:1, 0:NT], sb["onesT"][:, 0:1],
                                                      srcf(g), start=(g == 0), stop=(g == G - 1)),
                    inc=("c_t" if g == G - 1 else None))
        self.mark(("ln5sum", tag), "c_t")
        if src_gate is not None:
            self.wmark("scalar", src_gate)
        for g in range(G):
            par = (g % 2) * NT
            if self.sqgate is not None:
                self.wmark("scalar", self.sqgate)
            self.wmark("scalar", ("ln5mm", tag, g - 2))
            self.on("scalar", lambda sc, g=g, par=par: sc.activation(
                out=sb["sq"][:, par:par + NT], in_=srcf(g), func=AF.Square,
                bias=0.0, scale=1.0), inc="c_s")
            css = self.cnt["c_s"]
            self.wait("tensor", "c_s", css)
            self.on("tensor", lambda t, g=g, par=par: t.matmul(
                w1[0:1, 0:NT], sb["onesT"][:, 0:1], sb["sq"][:, par:par + NT],
                start=(g == 0), stop=(g == G - 1), skip_group_check=True),
                    inc="c_t")
            self.mark(("ln5mm", tag, g), "c_t")
        self.sqgate = ("ln5mm", tag, G - 1)
        ct0 = self.cnt["c_t"]
        st = sb["rbuf"]
        self.wait("vector", "c_t", ct0)
        self.on("vector", lambda v: v.tensor_scalar(out=st[0:1, 0:NT], in0=w0[0:1, 0:NT],
                                                    scalar1=1.0 / D, scalar2=None, op0=AL.mult))
        self.on("vector", lambda v: v.tensor_scalar(out=st[0:1, NT:2 * NT], in0=w1[0:1, 0:NT],
                                                    scalar1=1.0 / D, scalar2=None, op0=AL.mult))
        self.on("vector", lambda v: v.tensor_tensor(out=st[0:1, 2 * NT:3 * NT], in0=st[0:1, 0:NT],
                                                    in1=st[0:1, 0:NT], op=AL.mult))
        self.on("vector", lambda v: v.tensor_tensor(out=st[0:1, NT:2 * NT], in0=st[0:1, NT:2 * NT],
                                                    in1=st[0:1, 2 * NT:3 * NT], op=AL.subtract),
                inc="c_v")
        self.wait("scalar", "c_v", self.cnt["c_v"])
        self.on("scalar", lambda sc: sc.activation(out=st[0:1, 2 * NT:3 * NT],
                                                   in_=st[0:1, NT:2 * NT], func=AF.Sqrt,
                                                   bias=self.sb["cst"][0:1, 0:1], scale=1.0),
                inc="c_s")
        self.wait("vector", "c_s", self.cnt["c_s"])
        self.on("vector", lambda v: v.reciprocal(out=st[0:1, 2 * NT:3 * NT],
                                                 in_=st[0:1, 2 * NT:3 * NT]))
        self.on("vector", lambda v: v.tensor_tensor(out=st[0:1, 3 * NT:4 * NT], in0=st[0:1, 0:NT],
                                                    in1=st[0:1, 2 * NT:3 * NT], op=AL.mult),
                inc="c_v")
        self.wait("tensor", "c_v", self.cnt["c_v"])
        self.on("tensor", lambda t: t.matmul(w0[:, 0:NT], sb["onesT"][0:1, 0:P],
                                             st[0:1, 2 * NT:3 * NT], start=True, stop=True))
        self.on("tensor", lambda t: t.matmul(w1[:, 0:NT], sb["onesT"][0:1, 0:P],
                                             st[0:1, 3 * NT:4 * NT], start=True, stop=True),
                inc="c_t")
        return self.cnt["c_t"]

    def _dequant(self, qsrc, ssrc, dst, kchunks, mchunks):
        sb = self.sb
        qint = sb["szb"][:].bitcast(I32)
        dst4 = dst.rearrange("p (m kk q) -> p m kk q", kk=kchunks, q=P)
        ncols = mchunks * P
        for k in range(kchunks):
            for c0 in range(0, ncols, 1024):
                w = min(1024, ncols - c0)
                self.wait("sync", "d_cs", self.cnt["d_cs"])
                self.wmark("sync", ("dq_used",))
                self.dma("sync", sb["stage"][:, 0:w],
                         ssrc[k:k + 1, c0:c0 + w].broadcast_to([P, w]), "d_c")
                self.dma("sync", qint[:, 0:w], qsrc[k * P:(k + 1) * P, c0:c0 + w], "d_c")
                dc = self.cnt["d_c"]
                self.wait("scalar", "d_c", dc)
                self.wmark("scalar", ("dq_used",))
                self.on("scalar", lambda sc, w=w: sc.activation(
                    out=sb["xcb"][:, 0:w], in_=qint[:, 0:w], func=AF.Copy, bias=0.0, scale=1.0),
                    inc="c_s")
                cs = self.cnt["c_s"]
                self.wait("vector", "c_s", cs)
                self.wait("vector", "d_c", dc)
                self.wait("vector", "d_cs", self.cnt["d_cs"])
                self.on("vector", lambda v, w=w: v.tensor_scalar(
                    out=sb["convm"][:, 0:w], in0=sb["stage"][:, 0:w], scalar1=-128.0,
                    scalar2=None, op0=AL.mult))
                self.on("vector", lambda v, w=w: v.tensor_tensor(
                    out=sb["xcb"][:, 0:w], in0=sb["xcb"][:, 0:w], in1=sb["stage"][:, 0:w],
                    op=AL.mult))
                self.on("vector", lambda v, w=w: v.tensor_tensor(
                    out=sb["stageb"][:, 0:w], in0=sb["xcb"][:, 0:w], in1=sb["convm"][:, 0:w],
                    op=AL.add), inc="c_v")
                self.mark(("dq_used",), "c_v")
                self.wait("sync", "c_v", self.cnt["c_v"])
                m0, nm = c0 // P, w // P
                self.dma("sync", dst4[:, m0:m0 + nm, k, :],
                         sb["stageb"][:, 0:w].rearrange("p (m q) -> p m q", q=P), "d_cs")

    def ffn0(self):
        sb, s = self.sb, self.scr
        self.wmark("sync", ("blk1", NC1 - 1))
        self.wmark("scalar", ("blk1", NC1 - 1))
        self.wmark("vector", ("blk1", NC1 - 1))
        self.wmark("tensor", ("blk1", NC1 - 1))
        self.wmark("gpsimd", ("blk1", NC1 - 1))
        self.wmark("sync", ("ym", NC1 - 1))
        self.wmark("scalar", ("ym", NC1 - 1))
        self.wmark("vector", ("ym", NC1 - 1))
        self.on("vector", lambda v: v.tensor_scalar(
            out=sb["smallB"][:, 56:64], in0=sb["smallB"][:, 32:40], scalar1=0.5, scalar2=None,
            op0=AL.mult), inc="c_v")
        self.wmark("sync", ("pd", "fc1"))
        self.wmark("sync", ("pd", "fc2"))
        NT = 512
        NCF = T // NT
        for c in range(NCF):
            t0 = c * NT
            src = lambda g: sb["stream"][:, g * T + t0:g * T + t0 + NT]
            ctb = self._lnstats512(src, ("f0a", c))
            self.wait("vector", "c_t", ctb)
            self.wmark("vector", ("f2rd", c - 1, G - 1))
            for g in range(G):
                self.on("vector", lambda v, g=g: v.tensor_tensor(
                    out=self._pf_lnf(g), in0=src(g), in1=sb["ps_w0"][:, 0:NT], op=AL.mult))
                self.on("vector", lambda v, g=g: v.tensor_tensor(
                    out=self._pf_lnf(g), in0=self._pf_lnf(g), in1=sb["ps_w1"][:, 0:NT],
                    op=AL.subtract), inc=("c_v" if g == G - 1 else None))
            self.mark(("ln1", c), "c_v")
            self.mark(("bankfree", "ps_w0"), "c_v")
            self.mark(("bankfree", "ps_w1"), "c_v")
            ctb2 = self._lnstats512(self._pf_lnf, ("f0b", c), src_gate=("ln1", c))
            self.wait("vector", "c_t", ctb2)
            self.wmark("vector", ("f1rd", c - 1, GF - 1))
            for g in range(G):
                self.on("vector", lambda v, g=g: v.tensor_tensor(
                    out=sb["convm"][:, 0:NT], in0=self._pf_lnf(g), in1=sb["ps_w0"][:, 0:NT],
                    op=AL.mult))
                self.on("vector", lambda v, g=g: v.tensor_tensor(
                    out=sb["convm"][:, 0:NT], in0=sb["convm"][:, 0:NT],
                    in1=sb["ps_w1"][:, 0:NT], op=AL.subtract))
                self.on("vector", lambda v, g=g: v.tensor_scalar(
                    out=self._pf_h2(g), in0=sb["convm"][:, 0:NT],
                    scalar1=sb["smallB"][:, 16 + g:17 + g], scalar2=sb["smallB"][:, 24 + g:25 + g],
                    op0=AL.mult, op1=AL.add), inc=("c_v" if g == G - 1 else None))
            self.mark(("h2", c), "c_v")
            self.mark(("bankfree", "ps_w0"), "c_v")
            self.mark(("bankfree", "ps_w1"), "c_v")
            for ff in range(GF):
                wslot = (ff % 2) * G * P
                dsem = "d_w0" if ff % 2 == 0 else "d_w1"
                if (ff % 2) in self.slot_gate:
                    self.wmark("sync", self.slot_gate[ff % 2])
                self.dma("sync", sb["wst"][:, wslot:wslot + G * P],
                         s["fc1"][:, ff * G * P:(ff + 1) * G * P], dsem)
                self.wait("tensor", dsem, self.cnt[dsem])
                self.wmark("tensor", ("h2", c))
                pbi = ff % 2
                pb = sb["ps_m0"] if pbi == 0 else sb["ps_m1"]
                if self.pmgate[pbi]:
                    self.wmark("tensor", self.pmgate[pbi])
                self.wmark("tensor", ("f1rd", c - 1, ff))
                for k in range(G):
                    self.on("tensor", lambda t, k=k, pb=pb, wslot=wslot: t.matmul(
                        pb[:, 0:NT], sb["wst"][:, wslot + k * P:wslot + (k + 1) * P],
                        self._pf_h2(k), start=(k == 0), stop=(k == G - 1)),
                            inc=("c_t" if k == G - 1 else None))
                self.mark(("f1w", c, ff), "c_t")
                self.slot_gate[ff % 2] = ("f1w", c, ff)
                ct = self.cnt["c_t"]
                self.wait("scalar", "c_t", ct)
                self.on("scalar", lambda sc, ff=ff, pb=pb: sc.activation(
                    out=self._pf_y1(ff), in_=pb[:, 0:NT], func=AF.Silu,
                    bias=sb["smallA"][:, 64 + ff:65 + ff], scale=1.0), inc="c_s")
                self.mark(("f1e", c, ff), "c_s")
                self.pmgate[pbi] = ("f1e", c, ff)
                self.pump_step(2)
            for j in range(G):
                pbi = j % 2
                pb = sb["ps_m0"] if pbi == 0 else sb["ps_m1"]
                if self.pmgate[pbi]:
                    self.wmark("tensor", self.pmgate[pbi])
                self.wmark("tensor", ("f1e", c, GF - 1))
                for piece in range(4):
                    wslot = (2 + (piece % 2)) * G * P
                    dsem = "d_w2" if piece % 2 == 0 else "d_w3"
                    if (2 + piece % 2) in self.slot_gate:
                        self.wmark("sync", self.slot_gate[2 + piece % 2])
                    self.dma("sync", sb["wst"][:, wslot:wslot + G * P],
                             s["fc2"][:, (j * GF + piece * 8) * P:(j * GF + piece * 8 + 8) * P],
                             dsem)
                    self.wait("tensor", dsem, self.cnt[dsem])
                    for k8 in range(G):
                        kk = piece * 8 + k8
                        self.on("tensor", lambda t, k8=k8, kk=kk, pb=pb, wslot=wslot: t.matmul(
                            pb[:, 0:NT], sb["wst"][:, wslot + k8 * P:wslot + (k8 + 1) * P],
                            self._pf_y1(kk), start=(kk == 0), stop=(kk == GF - 1)),
                            inc=("c_t" if k8 == G - 1 else None))
                    self.mark(("f2w", c, j, piece), "c_t")
                    self.slot_gate[2 + piece % 2] = ("f2w", c, j, piece)
                    self.pump_step(2)
                self.on("tensor", lambda t: t.nop(), inc="c_t")
                ct = self.cnt["c_t"]
                if j == G - 1:
                    for ff in range(GF):
                        self.mark(("f1rd", c, ff), "c_t")
                self.wait("scalar", "c_t", ct)
                self.wmark("scalar", ("f2v", c, j - 2))
                par = (j % 2) * NT
                self.on("scalar", lambda sc, j=j, pb=pb, par=par: sc.activation(
                    out=sb["sq"][:, par:par + NT], in_=pb[:, 0:NT], func=AF.Copy,
                    bias=0.0, scale=sb["cst"][:, 1:2]), inc="c_s")
                self.mark(("f2e", c, j), "c_s")
                self.pmgate[pbi] = ("f2e", c, j)
                cs = self.cnt["c_s"]
                self.wait("vector", "c_s", cs)
                self.on("vector", lambda v, j=j, par=par: v.tensor_tensor(
                    out=sb["sq"][:, par:par + NT], in0=sb["sq"][:, par:par + NT],
                    in1=self._pf_lnf(j), op=AL.add))
                self.on("vector", lambda v, j=j, t0=t0, par=par: v.scalar_tensor_tensor(
                    out=sb["stream"][:, j * T + t0:j * T + t0 + NT],
                    in0=sb["stream"][:, j * T + t0:j * T + t0 + NT],
                    scalar=sb["smallB"][:, 56 + j:57 + j], in1=sb["sq"][:, par:par + NT],
                    op0=AL.add, op1=AL.add), inc="c_v")
                self.mark(("f2v", c, j), "c_v")
                if j == G - 1:
                    self.mark(("f2rd", c, G - 1), "c_v")
            self.sqgate = ("f2v", c, G - 1)
            self.mark(("f0done", c), "c_v")
        self.mark(("ffn0_done",), "c_v")


    def _pool_lnx(self, g):
        sb = self.sb
        if g < 2:
            return sb["bxb"][:, g * T:(g + 1) * T]
        if g < 5:
            return sb["xab"][:].bitcast(BF16)[:, (g - 2) * T:(g - 1) * T]
        return sb["ln0f"][:].bitcast(BF16)[:, (g - 5) * T:(g - 4) * T]

    def _pool_sg(self, g):
        sb = self.sb
        half = 1024
        if g < 2:
            return sb["hb"][:, g * half:(g + 1) * half]
        if g == 2:
            return sb["sq"][:, 0:half]
        if g < 5:
            return sb["szb"][:, (g - 3) * half:(g - 2) * half]
        if g < 7:
            return sb["xcb"][:, (g - 5) * half:(g - 4) * half]
        return sb["sgx"][:, 0:half]

    def _pool_o(self, h):
        sb = self.sb
        if h == 0:
            return sb["xcbb"][:, 0:1024]
        if h == 1:
            return sb["cmb"][:].bitcast(BF16)[:, 0:1024]
        if h < 4:
            return sb["yb"][:].bitcast(BF16)[:, (h - 2) * 1024:(h - 1) * 1024]
        if h < 6:
            return sb["ln0b"][:, (h - 4) * 1024:(h - 3) * 1024]
        if h == 6:
            return sb["psc"][:].bitcast(BF16)[:, 0:1024]
        return sb["ptmp"][:].bitcast(BF16)[:, 0:1024]

    def _pool_S(self, h):
        sb = self.sb
        if h < 4:
            return sb["bbb"][:, h * P:(h + 1) * P]
        if h < 6:
            return sb["sprime"][:, (h - 4) * P:(h - 3) * P]
        return sb["amisc"][:, (h - 6) * P:(h - 5) * P]

    def l1_attn(self):
        sb, s, i = self.sb, self.scr, self.i
        arena = sb["pin"][:].bitcast(BF16)
        kbuf = sb["convm"]
        csb = sb["rbuf"]
        rcs = sb["ymb"][:].bitcast(F32)
        qb = sb["stage"]
        vb = sb["stageb"]
        SC = 1024
        for e in ["sync", "scalar", "vector", "tensor", "gpsimd"]:
            self.wmark(e, ("ffn0_done",))
        self.wmark("sync", ("pd", "wg2"))
        if self.stage >= 4:
            for e in ["sync", "scalar", "vector", "tensor"]:
                self.wmark(e, ("pd", "w2"))
        self.dma("sync", sb["identf"][:], i["trimask"], "d_misc")
        nm = self.cnt["d_misc"]
        self.wait("vector", "d_misc", nm)
        for c in range(NC1):
            t0 = c * T1
            src = lambda g: sb["stream"][:, g * T + t0:g * T + t0 + T1]
            bank = "ps_w0" if c % 2 == 0 else "ps_w1"
            bk = sb[bank]
            ctb = self._lnstats(src, bank, ("l1ln", c))
            self.wait("vector", "c_t", ctb)
            for g in range(G):
                self.on("vector", lambda v, g=g: v.tensor_tensor(
                    out=kbuf[:, 0:T1], in0=src(g),
                    in1=bk[:, 256:256 + T1], op=AL.mult))
                self.on("vector", lambda v, g=g, t0=t0: v.tensor_tensor(
                    out=self._pool_lnx(g)[:, t0:t0 + T1], in0=kbuf[:, 0:T1],
                    in1=bk[:, 384:384 + T1], op=AL.subtract),
                        inc=("c_v" if g == G - 1 else None))
            self.mark(("lnx", c), "c_v")
            self.mark(("bankfree", bank), "c_v")
        self.mark(("lnx_all",), "c_v")

        HT = T // 2
        NCH = HT // TC
        for half in range(2):
            th0 = half * HT
            for e in ["sync", "scalar", "vector", "tensor"]:
                self.wmark(e, ("l1half", half - 1))
            self.wmark("scalar", ("kvmm", half - 1, H - 1, NCH - 1))
            dsem = "d_w2"
            if 2 in self.slot_gate:
                self.wmark("sync", self.slot_gate[2])
            self.dma("sync", sb["wst"][:, 2 * G * P:3 * G * P], s["wg1"][:, 0:G * P], dsem)
            self.wait("tensor", dsem, self.cnt[dsem])
            self.wmark("tensor", ("lnx_all",))
            for tt in range(2):
                pb = sb["ps_m0"] if tt == 0 else sb["ps_m1"]
                if self.pmgate[tt]:
                    self.wmark("tensor", self.pmgate[tt])
                for k in range(G):
                    self.on("tensor", lambda t, k=k, tt=tt, pb=pb: t.matmul(
                        pb[:, 0:512], sb["wst"][:, 2 * G * P + k * P:2 * G * P + (k + 1) * P],
                        self._pool_lnx(k)[:, th0 + tt * 512:th0 + (tt + 1) * 512],
                        start=(k == 0), stop=(k == G - 1)), inc=("c_t" if k == G - 1 else None))
                ct = self.cnt["c_t"]
                self.mark(("g1w", half, tt), "c_t")
                self.slot_gate[2] = ("g1w", half, tt)
                self.wait("scalar", "c_t", ct)
                self.on("scalar", lambda sc, tt=tt, pb=pb: sc.copy(
                    out=arena[:, tt * 512:(tt + 1) * 512], in_=pb[:, 0:512]), inc="c_s")
                self.mark(("g1e", half, tt), "c_s")
                self.pmgate[tt] = ("g1e", half, tt)
            if 3 in self.slot_gate:
                self.wmark("sync", self.slot_gate[3])
            self.dma("sync", sb["wst"][:, 3 * G * P:4 * G * P], s["wg2"][:, 0:G * P], "d_w3")
            self.wait("tensor", "d_w3", self.cnt["d_w3"])
            self.wmark("tensor", ("g1e", half, 1))
            for j in range(G):
                for tt in range(2):
                    pbi = (j * 2 + tt) % 2
                    pb = sb["ps_m0"] if pbi == 0 else sb["ps_m1"]
                    if self.pmgate[pbi]:
                        self.wmark("tensor", self.pmgate[pbi])
                    self.on("tensor", lambda t, j=j, tt=tt, pb=pb: t.matmul(
                        pb[:, 0:512], sb["wst"][:, 3 * G * P + j * P:3 * G * P + (j + 1) * P],
                        arena[:, tt * 512:(tt + 1) * 512], start=True, stop=True), inc="c_t")
                    ct = self.cnt["c_t"]
                    self.mark(("g2w", half, j, tt), "c_t")
                    self.slot_gate[3] = ("g2w", half, j, tt)
                    self.wait("scalar", "c_t", ct)
                    self.on("scalar", lambda sc, j=j, tt=tt, pb=pb: sc.activation(
                        out=self._pool_sg(j)[:, tt * 512:(tt + 1) * 512], in_=pb[:, 0:512],
                        func=AF.Silu, bias=0.0, scale=1.0), inc="c_s")
                    self.mark(("sg", half, j, tt), "c_s")
                    self.pmgate[pbi] = ("sg", half, j, tt)
            self.mark(("sg_done", half), "c_s")
            self.mark(("g2w_all", half), "c_t")

            for h in range(H):
                self._l1_proj(s["wk"], h, kbuf, half, tag=("kp", half, h), silu=False,
                              bf16out=False, gate=("chunkdone", half, h - 1, NCH - 1),
                              gate_eng="c_v")
                cvk = self.marks[("kp", half, h)][1]
                self.wait("vector", "c_s", cvk)
                self.on("vector", lambda v: v.tensor_reduce(
                    out=csb[:, SC:SC + 1], in_=kbuf[:, 0:HT], axis=AX.X, op=AL.max), inc="c_v")
                self.on("vector", lambda v: v.tensor_scalar(
                    out=csb[:, SC + 1:SC + 2], in0=csb[:, SC:SC + 1], scalar1=-1.0, scalar2=None,
                    op0=AL.mult), inc="c_v")
                cv = self.cnt["c_v"]
                self.wait("scalar", "c_v", cv)
                if half == 0:
                    self.on("scalar", lambda sc, h=h: sc.copy(
                        out=csb[:, SC + 8 + h:SC + 9 + h], in_=csb[:, SC:SC + 1]), inc="c_s")
                self.on("scalar", lambda sc: sc.activation(
                    out=kbuf[:, 0:HT], in_=kbuf[:, 0:HT], func=AF.Exp,
                    bias=csb[:, SC + 1:SC + 2], scale=1.0), inc="c_s")
                cs0 = self.cnt["c_s"]
                self.wait("vector", "c_s", cs0)
                self.on("vector", lambda v: v.tensor_tensor_scan(
                    out=csb[:, 0:HT], data0=kbuf[:, 0:HT], data1=kbuf[:, 0:HT],
                    initial=0.0, op0=AL.add, op1=AL.bypass), inc="c_v")
                if half == 1:
                    self.on("vector", lambda v, h=h: v.tensor_tensor(
                        out=csb[:, SC + 2:SC + 3], in0=csb[:, SC + 8 + h:SC + 9 + h],
                        in1=csb[:, SC:SC + 1], op=AL.subtract), inc="c_v")
                    cv1 = self.cnt["c_v"]
                    self.wait("scalar", "c_v", cv1)
                    self.on("scalar", lambda sc: sc.activation(
                        out=csb[:, SC + 2:SC + 3], in_=csb[:, SC + 2:SC + 3], func=AF.Exp,
                        bias=0.0, scale=1.0), inc="c_s")
                    cs1 = self.cnt["c_s"]
                    self.wait("vector", "c_s", cs1)
                    self.on("vector", lambda v, h=h: v.tensor_tensor(
                        out=csb[:, SC + 3:SC + 4], in0=csb[:, SC + 16 + h:SC + 17 + h],
                        in1=csb[:, SC + 2:SC + 3], op=AL.mult), inc="c_v")
                    self.on("vector", lambda v: v.tensor_scalar(
                        out=csb[:, 0:HT], in0=csb[:, 0:HT],
                        scalar1=csb[:, SC + 3:SC + 4], scalar2=None, op0=AL.add), inc="c_v")
                else:
                    self.on("vector", lambda v, h=h: v.tensor_copy(
                        out=csb[:, SC + 16 + h:SC + 17 + h], in_=csb[:, HT - 1:HT]), inc="c_v")
                self.on("vector", lambda v: v.reciprocal(out=rcs[:, 0:HT], in_=csb[:, 0:HT]),
                        inc="c_v")
                self.mark(("csr", half, h), "c_v")
                self._l1_proj(s["wq"], h, qb, half, tag=("qp", half, h), silu=True,
                              bf16out=False, gate=("chunkdone", half, h - 1, NCH - 1),
                              gate_eng="c_v")
                self._l1_proj(s["wv"], h, vb, half, tag=("vp", half, h), silu=False,
                              bf16out=True, gate=("tmm", half, h - 1, NCH - 1),
                              gate_eng="c_t")
                for ci in range(NCH):
                    self._l1_chunk(half, h, ci)
                self.mark(("head", half, h), "c_v")
                o2scr = sb["rbuf"][:, 1184:1696]
                for tt in range(2):
                    if tt == 0:
                        self.wmark("scalar", ("osq", half, h - 1, 1))
                    else:
                        self.wmark("scalar", ("osq", half, h, 0))
                    self.wmark("scalar", ("oev", half, h, NCH - 1))
                    self.on("scalar", lambda sc, h=h, tt=tt, o2scr=o2scr: sc.activation(
                        out=o2scr, in_=self._pool_o(h)[:, tt * 512:(tt + 1) * 512],
                        func=AF.Square, bias=0.0, scale=1.0), inc="c_s")
                    cso = self.cnt["c_s"]
                    self.wait("tensor", "c_s", cso)
                    cbb = "ps_cb0" if tt == 0 else "ps_cb1"
                    if h == 0:
                        self.wmark("tensor", ("cbfree", half, tt))
                    self.on("tensor", lambda t, h=h, cbb=cbb, o2scr=o2scr: t.matmul(
                        sb[cbb][0:1, 0:512], sb["onesT"][:, 0:1], o2scr,
                        start=(h == 0), stop=(h == H - 1), skip_group_check=True), inc="c_t")
                    self.mark(("osq", half, h, tt), "c_t")
            self.wmark("vector", ("osq", half, H - 1, 0))
            self.wmark("vector", ("osq", half, H - 1, 1))
            st = sb["strip"]
            self.on("vector", lambda v: v.tensor_scalar(
                out=st[0:1, 0:HT] if False else sb["strip"][0:1, 0:512],
                in0=sb["ps_cb0"][0:1, 0:512], scalar1=1.0 / D, scalar2=None, op0=AL.mult))
            self.on("vector", lambda v: v.tensor_scalar(
                out=sb["rbuf"][0:1, 0:512], in0=sb["ps_cb1"][0:1, 0:512],
                scalar1=1.0 / D, scalar2=None, op0=AL.mult), inc="c_v")
            cv = self.cnt["c_v"]
            self.wait("scalar", "c_v", cv)
            self.on("scalar", lambda sc: sc.activation(
                out=sb["strip"][0:1, 0:512], in_=sb["strip"][0:1, 0:512], func=AF.Sqrt,
                bias=sb["cst"][0:1, 0:1], scale=1.0), inc="c_s")
            self.on("scalar", lambda sc: sc.activation(
                out=sb["rbuf"][0:1, 0:512], in_=sb["rbuf"][0:1, 0:512], func=AF.Sqrt,
                bias=sb["cst"][0:1, 0:1], scale=1.0), inc="c_s")
            cs2 = self.cnt["c_s"]
            self.wait("vector", "c_s", cs2)
            self.on("vector", lambda v: v.reciprocal(out=sb["strip"][0:1, 0:512],
                                                     in_=sb["strip"][0:1, 0:512]))
            self.on("vector", lambda v: v.reciprocal(out=sb["rbuf"][0:1, 0:512],
                                                     in_=sb["rbuf"][0:1, 0:512]), inc="c_v")
            cv2 = self.cnt["c_v"]
            self.wait("tensor", "c_v", cv2)
            self.on("tensor", lambda t: t.matmul(sb["ps_cb0"][:, 0:512], sb["onesT"][0:1, 0:P],
                                                 sb["strip"][0:1, 0:512], start=True, stop=True))
            self.on("tensor", lambda t: t.matmul(sb["ps_cb1"][:, 0:512], sb["onesT"][0:1, 0:P],
                                                 sb["rbuf"][0:1, 0:512], start=True, stop=True),
                    inc="c_t")
            ctb = self.cnt["c_t"]
            self.mark(("rbb", half), "c_t")
            for j in range(G):
                dsemo = "d_w2" if j % 2 == 0 else "d_w3"
                if (2 + j % 2) in self.slot_gate:
                    self.wmark("sync", self.slot_gate[2 + j % 2])
                self.dma("sync", sb["wst"][:, (2 + j % 2) * G * P:(3 + j % 2) * G * P],
                         s["wo"][:, j * G * P:(j + 1) * G * P], dsemo)
                self.wait("tensor", dsemo, self.cnt[dsemo])
                for tt in range(2):
                    pbi = tt
                    pb = sb["ps_m0"] if pbi == 0 else sb["ps_m1"]
                    if self.pmgate[pbi]:
                        self.wmark("tensor", self.pmgate[pbi])
                    for h in range(H):
                        if j == 0:
                            self.wmark("vector", ("rbb", half))
                            self.wmark("vector", ("head", half, h))
                            self.wmark("vector", ("sg_done", half))
                            cbb = sb["ps_cb0"] if tt == 0 else sb["ps_cb1"]
                            self.on("vector", lambda v, h=h, tt=tt: v.tensor_tensor(
                                out=kbuf[:, 0:512], in0=self._pool_o(h)[:, tt * 512:(tt + 1) * 512],
                                in1=self._pool_sg(h)[:, tt * 512:(tt + 1) * 512], op=AL.mult))
                            self.on("vector", lambda v, h=h, tt=tt, cbb=cbb: v.tensor_tensor(
                                out=kbuf[:, 0:512], in0=kbuf[:, 0:512], in1=cbb[:, 0:512],
                                op=AL.mult))
                            self.on("vector", lambda v, h=h, tt=tt: v.tensor_scalar(
                                out=self._pool_o(h)[:, tt * 512:(tt + 1) * 512],
                                in0=kbuf[:, 0:512],
                                scalar1=sb["smallB"][:, 40 + h:41 + h], scalar2=None,
                                op0=AL.mult), inc="c_v")
                            self.mark(("om", half, h, tt), "c_v")
                        self.wmark("tensor", ("om", half, h, tt))
                        self.on("tensor", lambda t, j=j, h=h, tt=tt, pb=pb: t.matmul(
                            pb[:, 0:512],
                            sb["wst"][:, (2 + j % 2) * G * P + h * P:(2 + j % 2) * G * P + (h + 1) * P],
                            self._pool_o(h)[:, tt * 512:(tt + 1) * 512],
                            start=(h == 0), stop=(h == H - 1)),
                            inc=("c_t" if h == H - 1 else None))
                    self.mark(("wow", half, j, tt), "c_t")
                    self.slot_gate[2 + j % 2] = ("wow", half, j, tt)
                    ct = self.cnt["c_t"]
                    self.wait("vector", "c_t", ct)
                    self.on("vector", lambda v, j=j, tt=tt, pb=pb: v.scalar_tensor_tensor(
                        out=sb["stream"][:, j * T + th0 + tt * 512:j * T + th0 + (tt + 1) * 512],
                        in0=sb["stream"][:, j * T + th0 + tt * 512:j * T + th0 + (tt + 1) * 512],
                        scalar=0.0, in1=pb[:, 0:512], op0=AL.add, op1=AL.add), inc="c_v")
                    self.mark(("wost", half, j, tt), "c_v")
                    self.pmgate[pbi] = ("wost", half, j, tt)
            self.mark(("cbfree", half + 1, 0), "c_v")
            self.mark(("cbfree", half + 1, 1), "c_v")
            self.mark(("l1half", half), "c_v")
        self.mark(("l1_done",), "c_v")

    def _l1_proj(self, wsrc, h, dstbuf, half, tag, silu, bf16out, gate=None, gate_eng=None):
        sb = self.sb
        th0 = half * (T // 2)
        dsem = "d_w0" if h % 2 == 0 else "d_w1"
        wslot = (h % 2) * G * P
        if (h % 2) in self.slot_gate:
            self.wmark("sync", self.slot_gate[h % 2])
        self.dma("sync", sb["wst"][:, wslot:wslot + G * P],
                 wsrc[:, h * G * P:(h + 1) * G * P], dsem)
        self.wait("tensor", dsem, self.cnt[dsem])
        self.wmark("tensor", ("lnx_all",))
        for tt in range(2):
            pbi = tt
            pb = sb["ps_m0"] if pbi == 0 else sb["ps_m1"]
            if self.pmgate[pbi]:
                self.wmark("tensor", self.pmgate[pbi])
            for k in range(G):
                self.on("tensor", lambda t, k=k, tt=tt, pb=pb, wslot=wslot: t.matmul(
                    pb[:, 0:512], sb["wst"][:, wslot + k * P:wslot + (k + 1) * P],
                    self._pool_lnx(k)[:, th0 + tt * 512:th0 + (tt + 1) * 512],
                    start=(k == 0), stop=(k == G - 1)), inc=("c_t" if k == G - 1 else None))
            self.mark(tag + (tt, "w"), "c_t")
            self.slot_gate[h % 2] = tag + (tt, "w")
            ct = self.cnt["c_t"]
            self.wait("scalar", "c_t", ct)
            if gate is not None and tt == 0:
                self.wmark("scalar", gate)
            fn = AF.Silu if silu else AF.Copy
            self.on("scalar", lambda sc, tt=tt, pb=pb, fn=fn: sc.activation(
                out=dstbuf[:, tt * 512:(tt + 1) * 512], in_=pb[:, 0:512], func=fn,
                bias=0.0, scale=1.0), inc="c_s")
            self.mark(tag + (tt, "e"), "c_s")
            self.pmgate[pbi] = tag + (tt, "e")
        self.mark(tag, "c_s")
        self.pump_step(2)

    def _l1_chunk(self, half, h, ci):
        sb = self.sb
        arena = sb["pin"][:].bitcast(BF16)
        kbuf = sb["convm"]
        csb = sb["rbuf"]
        rcs = sb["ymb"][:].bitcast(F32)
        qb = sb["stage"]
        vb = sb["stageb"]
        SC = 1024
        Sh = self._pool_S(h)
        t0 = ci * TC
        NCH = (T // 2) // TC
        first = (half == 0 and ci == 0)
        prevk = ((half, h, ci - 1) if ci > 0 else
                 ((half, h - 1, NCH - 1) if h > 0 else
                  ((half - 1, H - 1, NCH - 1) if half > 0 else None)))
        A_Sb = arena[:, 0:P]
        A_qn = arena[:, P:2 * P]
        A_qi = arena[:, 2 * P:3 * P]
        A_eb = arena[:, 3 * P:4 * P]
        A_Pb = arena[:, 4 * P:5 * P]
        A_KeT = arena[:, 5 * P:6 * P]
        A_VT = arena[:, 6 * P:7 * P]
        t1f = csb[:, SC + 32:SC + 32 + P]

        def pk(name):
            return None if prevk is None else (name,) + prevk

        self.wmark("vector", ("csr", half, h))
        self.wmark("vector", ("qp", half, h))
        if prevk is not None:
            self.wmark("vector", pk("pmm"))
            self.wmark("vector", pk("tmm"))
            self.wmark("vector", pk("omm"))
        if h == 0 and ci == 0:
            self.wmark("vector", ("g2w_all", half))
        self.on("vector", lambda v: v.tensor_tensor(
            out=t1f, in0=qb[:, t0:t0 + TC], in1=rcs[:, t0:t0 + TC], op=AL.mult), inc="c_v")
        self.on("vector", lambda v: v.tensor_copy(out=A_qn, in_=t1f), inc="c_v")
        if not first:
            if ci == 0:
                scal = csb[:, SC + 3:SC + 4] if half == 1 else csb[:, SC + 16 + h:SC + 17 + h]
            else:
                scal = csb[:, t0 - 1:t0]
            self.on("vector", lambda v, scal=scal: v.tensor_scalar(
                out=A_qi, in0=t1f, scalar1=scal, scalar2=None, op0=AL.mult), inc="c_v")
            self.on("vector", lambda v: v.tensor_copy(out=A_Sb, in_=Sh), inc="c_v")
        self.on("vector", lambda v: v.tensor_copy(out=A_eb, in_=kbuf[:, t0:t0 + TC]), inc="c_v")
        cv_prep = self.cnt["c_v"]
        self.mark(("prep", half, h, ci), "c_v")

        self.wait("tensor", "c_v", cv_prep)
        if prevk is not None:
            self.wmark("tensor", pk("pevac"))
            self.wmark("tensor", pk("supd"))
        self.on("tensor", lambda t: t.matmul(sb["ps_x"][:, 0:P], A_eb, A_qn,
                                             start=True, stop=True), inc="c_t")
        ct_p = self.cnt["c_t"]
        self.mark(("pmm", half, h, ci), "c_t")
        psT = sb["ps_a"][:].bitcast(BF16)
        if prevk is not None:
            self.wmark("tensor", pk("tevac"))
        self.wmark("tensor", ("vp", half, h))
        self.on("tensor", lambda t: t.transpose(psT[:, 0:P], A_eb, sb["identb"][:]))
        self.on("tensor", lambda t, t0=t0: t.transpose(psT[:, P:2 * P], vb[:, t0:t0 + TC],
                                                       sb["identb"][:]), inc="c_t")
        ct_t = self.cnt["c_t"]
        self.mark(("tmm", half, h, ci), "c_t")
        self.wait("scalar", "c_t", ct_t)
        if prevk is not None:
            self.wmark("scalar", pk("kvmm"))
            self.wmark("scalar", pk("omm"))
        self.on("scalar", lambda sc: sc.copy(out=A_KeT, in_=psT[:, 0:P]))
        self.on("scalar", lambda sc: sc.copy(out=A_VT, in_=psT[:, P:2 * P]), inc="c_s")
        cs_t = self.cnt["c_s"]
        self.mark(("tevac", half, h, ci), "c_s")
        self.wait("vector", "c_t", ct_p)
        if prevk is not None:
            self.wmark("vector", pk("omm"))
        self.on("vector", lambda v: v.tensor_tensor(out=A_Pb, in0=sb["ps_x"][:, 0:P],
                                                    in1=sb["identf"][:], op=AL.mult), inc="c_v")
        cv_pm = self.cnt["c_v"]
        self.mark(("pevac", half, h, ci), "c_v")

        ob = "ps_w0" if ci % 2 == 0 else "ps_w1"
        self.wmark("tensor", ("bankfree", ob))
        self.wait("tensor", "c_v", cv_pm)
        self.wait("tensor", "c_s", cs_t)
        if not first:
            self.on("tensor", lambda t, ob=ob: t.matmul(sb[ob][:, 0:P], A_Sb, A_qi,
                                                        start=True, stop=False))
            self.on("tensor", lambda t, ob=ob: t.matmul(sb[ob][:, 0:P], A_VT, A_Pb,
                                                        start=False, stop=True), inc="c_t")
        else:
            self.on("tensor", lambda t, ob=ob: t.matmul(sb[ob][:, 0:P], A_VT, A_Pb,
                                                        start=True, stop=True), inc="c_t")
        ct_o = self.cnt["c_t"]
        self.mark(("omm", half, h, ci), "c_t")
        self.on("tensor", lambda t: t.matmul(sb["ps_x"][:, P:2 * P], A_KeT, A_VT,
                                             start=True, stop=True), inc="c_t")
        ct_kv = self.cnt["c_t"]
        self.mark(("kvmm", half, h, ci), "c_t")

        self.wait("scalar", "c_t", ct_o)
        self.on("scalar", lambda sc, ob=ob, t0=t0, h=h: sc.activation(
            out=self._pool_o(h)[:, t0:t0 + TC], in_=sb[ob][:, 0:P], func=AF.Copy,
            bias=0.0, scale=sb["cst"][:, 2:3]), inc="c_s")
        self.mark(("oev", half, h, ci), "c_s")
        self.mark(("bankfree", ob), "c_s")

        self.wait("vector", "c_t", ct_kv)
        if first:
            scal2 = 0.0
        elif ci == 0:
            scal2 = csb[:, SC + 3:SC + 4] if half == 1 else csb[:, SC + 16 + h:SC + 17 + h]
        else:
            scal2 = csb[:, t0 - 1:t0]
        self.on("vector", lambda v, scal2=scal2, Sh=Sh: v.scalar_tensor_tensor(
            out=Sh, in0=Sh, scalar=scal2, in1=sb["ps_x"][:, P:2 * P],
            op0=AL.mult, op1=AL.add), inc="c_v")
        self.on("vector", lambda v, t0=t0: v.reciprocal(
            out=csb[:, SC + 4:SC + 5], in_=csb[:, t0 + TC - 1:t0 + TC]), inc="c_v")
        self.on("vector", lambda v, Sh=Sh: v.tensor_scalar(
            out=Sh, in0=Sh, scalar1=csb[:, SC + 4:SC + 5], scalar2=None, op0=AL.mult),
                inc="c_v")
        self.mark(("supd", half, h, ci), "c_v")
        self.mark(("chunkdone", half, h, ci), "c_v")
        self.pump_step(2)

    def _ternarize(self, wsrc, dst, kchunks, mchunks):
        sb = self.sb
        ncols = mchunks * P
        npieces_per_k = (ncols + 1023) // 1024
        acc = sb["rbuf"]
        nacc = 0
        for k in range(kchunks):
            for c0 in range(0, ncols, 1024):
                w = min(1024, ncols - c0)
                self.wmark("sync", ("tz_used",))
                self.dma("sync", sb["stage"][:, 0:w], wsrc[k * P:(k + 1) * P, c0:c0 + w], "d_c")
                dc = self.cnt["d_c"]
                self.wait("scalar", "d_c", dc)
                self.wmark("scalar", ("tz2_used",))
                self.on("scalar", lambda sc, w=w, nacc=nacc: sc.activation(
                    out=sb["convm"][:, 0:w], in_=sb["stage"][:, 0:w], func=AF.Abs,
                    bias=0.0, scale=1.0, accum_out=acc[:, 1024 + nacc:1025 + nacc]), inc="c_s")
                self.mark(("tz_used",), "c_s")
                nacc += 1
        css = self.cnt["c_s"]
        self.wait("vector", "c_s", css)
        self.on("vector", lambda v, nacc=nacc: v.tensor_reduce(
            out=acc[:, 1024 + nacc:1025 + nacc], in_=acc[:, 1024:1024 + nacc],
            axis=AX.X, op=AL.add), inc="c_v")
        cv = self.cnt["c_v"]
        self.wait("tensor", "c_v", cv)
        self.wmark("tensor", ("tzps",))
        self.on("tensor", lambda t, nacc=nacc: t.matmul(
            sb["ps_x"][0:1, 0:1], acc[:, 1024 + nacc:1025 + nacc], sb["onesT"][:, 0:1],
            start=True, stop=True), inc="c_t")
        ct = self.cnt["c_t"]
        self.wait("vector", "c_t", ct)
        self.on("vector", lambda v: v.tensor_scalar(
            out=acc[0:1, 1023:1024], in0=sb["ps_x"][0:1, 0:1],
            scalar1=1.0 / (kchunks * P * ncols), scalar2=1e-8, op0=AL.mult, op1=AL.max),
                inc="c_v")
        self.on("vector", lambda v: v.reciprocal(out=acc[0:1, 1023:1024],
                                                 in_=acc[0:1, 1023:1024]), inc="c_v")
        cv2 = self.cnt["c_v"]
        self.wait("tensor", "c_v", cv2)
        self.on("tensor", lambda t: t.matmul(sb["ps_x"][:, 4:5], sb["onesT"][0:1, 0:P],
                                             acc[0:1, 1023:1024], start=True, stop=True),
                inc="c_t")
        ct2 = self.cnt["c_t"]
        self.mark(("tzps",), "c_t")
        self.wait("vector", "c_t", ct2)
        self.on("vector", lambda v: v.tensor_copy(out=acc[:, 1022:1023], in_=sb["ps_x"][:, 4:5]),
                inc="c_v")
        self.mark(("tz_recip",), "c_v")
        recip = acc[:, 1022:1023]
        dst4 = dst.rearrange("p (m kk q) -> p m kk q", kk=kchunks, q=P)
        qflt = sb["szb"]
        self.wait("sync", "c_s", css)
        for k in range(kchunks):
            for c0 in range(0, ncols, 1024):
                w = min(1024, ncols - c0)
                self.wait("sync", "d_cs", self.cnt["d_cs"])
                self.wmark("sync", ("tz2_used",))
                self.dma("sync", sb["stage"][:, 0:w], wsrc[k * P:(k + 1) * P, c0:c0 + w], "d_c")
                dc = self.cnt["d_c"]
                self.wait("scalar", "d_c", dc)
                self.wmark("scalar", ("tz_recip",))
                self.wmark("scalar", ("tz2v",))
                self.on("scalar", lambda sc, w=w, recip=recip: sc.activation(
                    out=sb["convm"][:, 0:w], in_=sb["stage"][:, 0:w], func=AF.Abs,
                    bias=0.0, scale=recip))
                self.on("scalar", lambda sc, w=w: sc.activation(
                    out=qflt[:, 0:w], in_=sb["stage"][:, 0:w], func=AF.Sign,
                    bias=0.0, scale=1.0), inc="c_s")
                cs2v = self.cnt["c_s"]
                self.mark(("tz2_used",), "c_s")
                self.wait("vector", "c_s", cs2v)
                self.wait("vector", "d_cs", self.cnt["d_cs"])
                self.on("vector", lambda v, w=w: v.tensor_scalar(
                    out=sb["convm"][:, 0:w], in0=sb["convm"][:, 0:w],
                    scalar1=0.5, scalar2=None, op0=AL.is_gt))
                self.on("vector", lambda v, w=w: v.tensor_tensor(
                    out=sb["stageb"][:, 0:w], in0=sb["convm"][:, 0:w], in1=qflt[:, 0:w],
                    op=AL.mult), inc="c_v")
                self.mark(("tz2v",), "c_v")
                self.wait("sync", "c_v", self.cnt["c_v"])
                m0, nm = c0 // P, w // P
                self.dma("sync", dst4[:, m0:m0 + nm, k, :],
                         sb["stageb"][:, 0:w].rearrange("p (m q) -> p m q", q=P), "d_cs")

    def ffn1(self):
        sb, s = self.sb, self.scr
        self.pump_drain()
        for e in ["sync", "scalar", "vector", "tensor", "gpsimd"]:
            self.wmark(e, ("l1_done",))
        self.wmark("sync", ("pd", "w1"))
        self.wmark("sync", ("pd", "w2"))
        NT = 512
        NCF = T // NT
        for c in range(NCF):
            t0 = c * NT
            src = lambda g: sb["stream"][:, g * T + t0:g * T + t0 + NT]
            ctb = self._lnstats512(src, ("t3", c))
            self.wait("vector", "c_t", ctb)
            self.wmark("vector", ("g1rd", c - 1, GF - 1))
            for g in range(G):
                self.on("vector", lambda v, g=g: v.tensor_tensor(
                    out=sb["convm"][:, 0:NT], in0=src(g), in1=sb["ps_w0"][:, 0:NT], op=AL.mult))
                self.on("vector", lambda v, g=g: v.tensor_tensor(
                    out=self._pf_h2(g), in0=sb["convm"][:, 0:NT], in1=sb["ps_w1"][:, 0:NT],
                    op=AL.subtract), inc=("c_v" if g == G - 1 else None))
            self.mark(("xln", c), "c_v")
            self.mark(("bankfree", "ps_w0"), "c_v")
            self.mark(("bankfree", "ps_w1"), "c_v")
            for ff in range(GF):
                wslot = (ff % 2) * G * P
                dsem = "d_w0" if ff % 2 == 0 else "d_w1"
                if (ff % 2) in self.slot_gate:
                    self.wmark("sync", self.slot_gate[ff % 2])
                self.dma("sync", sb["wst"][:, wslot:wslot + G * P],
                         s["w1"][:, ff * G * P:(ff + 1) * G * P], dsem)
                self.wait("tensor", dsem, self.cnt[dsem])
                self.wmark("tensor", ("xln", c))
                pbi = ff % 2
                pb = sb["ps_m0"] if pbi == 0 else sb["ps_m1"]
                if self.pmgate[pbi]:
                    self.wmark("tensor", self.pmgate[pbi])
                self.wmark("tensor", ("g1rd", c - 1, ff))
                for k in range(G):
                    self.on("tensor", lambda t, k=k, pb=pb, wslot=wslot: t.matmul(
                        pb[:, 0:NT], sb["wst"][:, wslot + k * P:wslot + (k + 1) * P],
                        self._pf_h2(k), start=(k == 0), stop=(k == G - 1)),
                            inc=("c_t" if k == G - 1 else None))
                self.mark(("t1w", c, ff), "c_t")
                self.slot_gate[ff % 2] = ("t1w", c, ff)
                ct = self.cnt["c_t"]
                self.wait("scalar", "c_t", ct)
                self.on("scalar", lambda sc, ff=ff, pb=pb: sc.activation(
                    out=self._pf_y1(ff), in_=pb[:, 0:NT], func=AF.Gelu,
                    bias=sb["smallA"][:, 96 + ff:97 + ff], scale=1.0), inc="c_s")
                self.mark(("t1e", c, ff), "c_s")
                self.pmgate[pbi] = ("t1e", c, ff)
            for j in range(G):
                pbi = j % 2
                pb = sb["ps_m0"] if pbi == 0 else sb["ps_m1"]
                if self.pmgate[pbi]:
                    self.wmark("tensor", self.pmgate[pbi])
                self.wmark("tensor", ("t1e", c, GF - 1))
                for piece in range(4):
                    wslot = (2 + (piece % 2)) * G * P
                    dsem = "d_w2" if piece % 2 == 0 else "d_w3"
                    if (2 + piece % 2) in self.slot_gate:
                        self.wmark("sync", self.slot_gate[2 + piece % 2])
                    self.dma("sync", sb["wst"][:, wslot:wslot + G * P],
                             s["w2"][:, (j * GF + piece * 8) * P:(j * GF + piece * 8 + 8) * P],
                             dsem)
                    self.wait("tensor", dsem, self.cnt[dsem])
                    for k8 in range(G):
                        kk = piece * 8 + k8
                        self.on("tensor", lambda t, k8=k8, kk=kk, pb=pb, wslot=wslot: t.matmul(
                            pb[:, 0:NT], sb["wst"][:, wslot + k8 * P:wslot + (k8 + 1) * P],
                            self._pf_y1(kk), start=(kk == 0), stop=(kk == GF - 1)),
                            inc=("c_t" if k8 == G - 1 else None))
                    self.mark(("t2w", c, j, piece), "c_t")
                    self.slot_gate[2 + piece % 2] = ("t2w", c, j, piece)
                self.on("tensor", lambda t: t.nop(), inc="c_t")
                ct = self.cnt["c_t"]
                if j == G - 1:
                    for ff in range(GF):
                        self.mark(("g1rd", c, ff), "c_t")
                self.wait("vector", "c_t", ct)
                self.on("vector", lambda v, j=j, t0=t0, pb=pb: v.scalar_tensor_tensor(
                    out=sb["stream"][:, j * T + t0:j * T + t0 + NT],
                    in0=sb["stream"][:, j * T + t0:j * T + t0 + NT],
                    scalar=sb["smallB"][:, 48 + j:49 + j],
                    in1=pb[:, 0:NT], op0=AL.add, op1=AL.add), inc="c_v")
                self.mark(("t2e", c, j), "c_v")
                self.pmgate[pbi] = ("t2e", c, j)
            self.mark(("f1done", c), "c_v")
        self.mark(("ffn1_done",), "c_v")


    def epilogue(self):
        sb = self.sb
        key = {1: ("blk1", NC1 - 1), 2: ("ffn0_done",), 3: ("l1_done",), 4: ("ffn1_done",)}[self.stage]
        sem, val = self.marks[key]
        self.wait("sync", sem, val)
        for g in range(G):
            self.dma("sync", self.y_out[g * P:(g + 1) * P, :], sb["stream"][:, g * T:(g + 1) * T],
                     "d_out")
        self.wait("sync", "d_out", self.cnt["d_out"])

    def build(self):
        blk, me = self.block, self

        @blk.sync
        def _(sync):
            me._reset("sync", sync)
            me.walk()

        @blk.vector
        def _(v):
            me._reset("vector", v)
            me.walk()

        @blk.scalar
        def _(s):
            me._reset("scalar", s)
            me.walk()

        @blk.tensor
        def _(t):
            me._reset("tensor", t)
            me.walk()

        @blk.gpsimd
        def _(g):
            me._reset("gpsimd", g)
            me.walk()

        for cm in reversed(self._ctx):
            cm.__exit__(None, None, None)
        self.nc.finalize()
        return self.nc


def _prep_weights(inp, stage):
    f = np.ascontiguousarray
    a = lambda k: np.asarray(inp[k], np.float32)
    d = {}
    d["alog_r"] = f(a("L0_attn_A_log").reshape(G, P, NST).transpose(1, 0, 2).reshape(P, GN))
    d["inwT"] = f(a("L0_attn_in_w").T)
    d["brow_l0"] = f(a("L0_attn_out_b").reshape(1, 1024))
    d["inb_xa_r"] = f(a("L0_attn_in_b")[:D].reshape(G, P).T)
    d["inb_z_r"] = f(a("L0_attn_in_b")[D:].reshape(G, P).T)
    d["convw_r"] = f(a("L0_attn_conv_w").reshape(D, 4).reshape(G, P, 4)
                     .transpose(1, 0, 2).reshape(P, 4 * G))
    d["convb_r"] = f(a("L0_attn_conv_b").reshape(G, P).T)
    d["bcwT"] = f(np.concatenate([a("L0_attn_B_w").T, a("L0_attn_C_w").T], axis=1))
    d["bcb"] = f(np.concatenate([a("L0_attn_B_b"), a("L0_attn_C_b")]).reshape(1, 2 * NST))
    d["outwT"] = f(a("L0_attn_out_w").T)
    d["lnw_r"] = f(a("L0_attn_ln_w").reshape(G, P).T)
    d["lnb_r"] = f(a("L0_attn_ln_b").reshape(G, P).T)
    d["ones"] = np.ones((P, 512), np.float32)
    d["identity"] = np.eye(P, dtype=np.float32)
    if stage >= 2:
        d["ffg_r"] = f(a("L0_ffn_ln_gamma").reshape(G, P).T)
        d["ffb_r"] = f(a("L0_ffn_ln_beta").reshape(G, P).T)
        d["fc1q"] = f(np.asarray(inp["L0_ffn_fc1_q"], np.int32))
        d["fc1s"] = f(a("L0_ffn_fc1_scales"))
        d["fc1b_r"] = f(a("L0_ffn_fc1_bias").reshape(GF, P).T)
        d["fc2q"] = f(np.asarray(inp["L0_ffn_fc2_q"], np.int32))
        d["fc2s"] = f(a("L0_ffn_fc2_scales"))
        d["fc2b_r"] = f(a("L0_ffn_fc2_bias").reshape(G, P).T)
    if stage >= 3:
        d["wqT"] = f(a("L1_attn_Wq").T)
        d["wkT"] = f(a("L1_attn_Wk").T)
        d["wvT"] = f(a("L1_attn_Wv").T)
        d["wg1T"] = f(a("L1_attn_Wg1").T)
        d["wg2T"] = f(a("L1_attn_Wg2").T)
        d["gnorm_r"] = f(a("L1_attn_g_norm_weight").reshape(H, P).T)
        d["woT"] = f(a("L1_attn_Wo").T)
        d["trimask"] = np.triu(np.ones((P, P), np.float32))
    if stage >= 4:
        d["w1T"] = f(a("L1_ffn_W1").T)
        d["w1b_r"] = f(a("L1_ffn_b1").reshape(GF, P).T)
        d["w2T"] = f(a("L1_ffn_W2").T)
        d["w2b_r"] = f(a("L1_ffn_b2").reshape(G, P).T)
    return d


_CACHE = {}


def run_stage(inputs, stage, cores=None):
    if stage not in _CACHE:
        _CACHE[stage] = Bld(stage=stage).build()
    nc = _CACHE[stage]
    w = _prep_weights(inputs, stage)
    x = np.asarray(inputs["L0_attn_x"], np.float32)
    nb = x.shape[0] if cores is None else cores
    in_maps = []
    for bi in range(nb):
        m = dict(w)
        m["x_cm"] = np.ascontiguousarray(x[bi].T)
        in_maps.append(m)
    res = run_bass_kernel_spmd(nc, in_maps, core_ids=list(range(nb)))
    return np.stack([np.ascontiguousarray(np.asarray(r["y_cm"]).T) for r in res.results], axis=0)


def kernel(**inputs):
    return run_stage(inputs, stage=4)


# version 12
# speedup vs baseline: 1.0491x; 1.0491x over previous
import numpy as np
import ml_dtypes
import concourse.bacc as bacc
import concourse.mybir as mybir
from concourse.bass_utils import run_bass_kernel_spmd
from concourse.dve_spec import Spec, lower, Src0, Src1, C0, C1, maxx, minn, AluOp, scan
from concourse.dve_op import DveOp
from concourse.dve_table_gen import dve_ver_for
import concourse.dve_ops as dve_ops

F32 = mybir.dt.float32
BF16 = mybir.dt.bfloat16
I32 = mybir.dt.int32
AL = mybir.AluOpType
AF = mybir.ActivationFunctionType
AX = mybir.AxisListType

P = 128
D = 1024
T = 2048
G = D // P
NST = 32
T1 = 128
NC1 = T // T1
T2 = 8
NSC = T1 // T2
GN = G * NST
RW = GN + 1
FF = 4096
GF = FF // P
H = 8
TC = 128
EPS = 1e-5

_VER = dve_ver_for("TRN2")


def _register_ops():
    def mk(name, spec, reference):
        for o in dve_ops.OPS:
            if o.name == name:
                return o
        op = DveOp(name, spec, subdim=False,
                   uops_sha={_VER: lower(spec, ver=_VER, perf_en=False, name=name).sha(_VER)},
                   reference=reference)
        dve_ops.OPS.append(op)
        dve_ops._SUB_OPCODE_FOR_NAME[name] = dve_ops._CUSTOM_DVE_ROW_BASE + len(dve_ops.OPS) - 1
        dve_ops.CUSTOM_DVE_SPECS[name] = spec
        dve_ops.CUSTOM_DVE_REFS[name] = reference
        return op

    def _ref_mulscan(in0, in1, s0, s1, imm2):
        a = np.asarray(in0, np.float32) * np.asarray(in1, np.float32)
        f = a.reshape(a.shape[0], -1)
        return np.cumsum(f, axis=-1).reshape(np.asarray(in0).shape)

    addclip = mk("MK_ADDCLIP", Spec(body=minn(maxx(Src0 + Src1, C0), C1)),
                 lambda in0, in1, s0, s1, imm2: np.minimum(np.maximum(in0 + in1, s0), s1))
    mulscan = mk("MK_MULSCAN", Spec(body=scan(AluOp.ADD, Src0 * Src1)), _ref_mulscan)
    return addclip, mulscan


ADDCLIP, MULSCAN = _register_ops()


class Bld:
    def __init__(self, stage=1):
        self.stage = stage
        self.nc = bacc.Bacc("TRN2", target_bir_lowering=False, detect_race_conditions=False)
        self._ctx = []
        self._declare()

    def _enter(self, cm):
        v = cm.__enter__()
        self._ctx.append(cm)
        return v

    def _declare(self):
        nc = self.nc

        def din(name, shape, dt=F32):
            return nc.dram_tensor(name, list(shape), dt, kind="ExternalInput").ap()

        def dscr(name, shape, dt=BF16):
            return nc.dram_tensor(name, list(shape), dt).ap()

        i = self.i = {}
        i["x"] = din("x_cm", (D, T))
        i["alog"] = din("alog_r", (P, GN))
        i["inwT"] = din("inwT", (D, 2 * D))
        i["brow_l0"] = din("brow_l0", (1, 1024))
        i["inb_xa"] = din("inb_xa_r", (P, G))
        i["inb_z"] = din("inb_z_r", (P, G))
        i["convw"] = din("convw_r", (P, 4 * G))
        i["convb"] = din("convb_r", (P, G))
        i["bcwT"] = din("bcwT", (D, 2 * NST))
        i["bcb"] = din("bcb", (1, 2 * NST))
        i["outwT"] = din("outwT", (D, D))
        i["lnw"] = din("lnw_r", (P, G))
        i["lnb"] = din("lnb_r", (P, G))
        i["ones"] = din("ones", (P, 512))
        i["identity"] = din("identity", (P, P))
        if self.stage >= 2:
            i["ffg"] = din("ffg_r", (P, G))
            i["ffb"] = din("ffb_r", (P, G))
            i["fc1q"] = din("fc1q", (D, FF), I32)
            i["fc1s"] = din("fc1s", (8, FF))
            i["fc1b"] = din("fc1b_r", (P, GF))
            i["fc2q"] = din("fc2q", (FF, D), I32)
            i["fc2s"] = din("fc2s", (32, D))
            i["fc2b"] = din("fc2b_r", (P, G))
        if self.stage >= 3:
            i["wqT"] = din("wqT", (D, D))
            i["wkT"] = din("wkT", (D, D))
            i["wvT"] = din("wvT", (D, D))
            i["wg1T"] = din("wg1T", (D, P))
            i["wg2T"] = din("wg2T", (P, D))
            i["gnorm"] = din("gnorm_r", (P, H))
            i["woT"] = din("woT", (D, D))
            i["trimask"] = din("trimask", (P, P))
        if self.stage >= 4:
            i["w1T"] = din("w1T", (D, FF))
            i["w1b"] = din("w1b_r", (P, GF))
            i["w2T"] = din("w2T", (FF, D))
            i["w2b"] = din("w2b_r", (P, G))

        self.y_out = nc.dram_tensor("y_cm", [D, T], F32, kind="ExternalOutput").ap()

        s = self.scr = {}
        s["inw"] = dscr("s_inw", (P, 16 * G * P))
        s["outw"] = dscr("s_outw", (P, G * G * P))
        s["bm"] = dscr("s_bm", (1, T * NST), F32)
        s["cm"] = dscr("s_cm", (1, T * NST), F32)
        if self.stage >= 2:
            s["fc1"] = dscr("s_fc1", (P, GF * G * P))
            s["fc2"] = dscr("s_fc2", (P, G * GF * P))
        if self.stage >= 3:
            s["wq"] = dscr("s_wq", (P, G * G * P))
            s["wk"] = dscr("s_wk", (P, G * G * P))
            s["wv"] = dscr("s_wv", (P, G * G * P))
            s["wo"] = dscr("s_wo", (P, G * G * P))
            s["wg1"] = dscr("s_wg1", (P, G * P))
            s["wg2"] = dscr("s_wg2", (P, G * P))
        if self.stage >= 4:
            s["w1"] = dscr("s_w1", (P, GF * G * P))
            s["w2"] = dscr("s_w2", (P, G * GF * P))

        sb = self.sb = {}

        def sbuf(name, shape, dt=F32):
            sb[name] = self._enter(nc.sbuf_tensor(name, list(shape), dt))

        def psum(name, shape, dt=F32):
            sb[name] = self._enter(nc.psum_tensor(name, list(shape), dt))

        sbuf("stream", (P, G * T))
        sbuf("xab", (P, 3 * G * (T1 + 3)))
        sbuf("szb", (P, 2 * G * T1))
        sbuf("xcb", (P, 2 * G * T1))
        sbuf("xcbb", (P, 2 * G * T1), BF16)
        sbuf("ln0b", (P, 2 * G * T1), BF16)
        sbuf("ln0f", (P, 3 * G * T1))
        sbuf("bbb", (P, 2 * T2 * NST))
        sbuf("bxb", (P, 2 * T2 * GN), BF16)
        sbuf("rbuf", (P, T2 * RW))
        sbuf("sprime", (P, GN))
        sbuf("yb", (P, G * T1))
        sbuf("ymb", (P, 2 * G * T1), BF16)
        sbuf("hb", (P, 2 * G * T1))
        sbuf("sq", (P, G * T1))
        sbuf("cstrip", (1, 2 * T2 * NST))
        sbuf("bmcm", (P, 2 * NST))
        sbuf("wst", (P, 4 * G * P), BF16)
        sbuf("stage", (P, 1024))
        sbuf("stageb", (P, 1024), BF16)
        sbuf("onesT", (P, 512))
        sbuf("identf", (P, P))
        sbuf("identb", (P, P), BF16)
        sbuf("amisc", (P, GN))
        sbuf("smallA", (P, 128))
        sbuf("smallB", (P, 64))
        sbuf("bcbrow", (1, 2 * NST))
        sbuf("strip", (1, 1536))
        sbuf("convm", (P, 2 * 4 * T1))
        sbuf("cst", (P, 4))
        sbuf("brow", (1, 3072))
        sbuf("sgx", (P, 1024))
        sbuf("pin", (P, 512))
        sbuf("psc", (P, 512))
        sbuf("ptmp", (P, 512))
        sbuf("pacc", (P, 272))

        psum("ps_a", (P, 512))
        psum("ps_cb0", (P, 512))
        psum("ps_cb1", (P, 512))
        psum("ps_w0", (P, 512))
        psum("ps_w1", (P, 512))
        psum("ps_m0", (P, 512))
        psum("ps_m1", (P, 512))
        psum("ps_x", (P, 512))

        self.semh = {}
        for nm in ["c_v", "c_s", "c_t", "c_g", "d_in", "d_c", "d_cs", "d_w0", "d_w1",
                   "d_w2", "d_w3", "d_b0", "d_b1", "d_bm", "d_out", "d_misc",
                   "d_pin", "d_po"]:
            self.semh[nm] = self._enter(nc.semaphore(nm))
        self.block = self._enter(nc.Block())
        self._pump_tasks()

    def _mk_cast(self, src, dst4, k, c0, w, endmark=None):
        sb = self.sb

        def dma(me, j):
            sl = (j % 2) * 256
            me.dma("scalar", sb["pin"][:, sl:sl + w], src[k * P:(k + 1) * P, c0:c0 + w],
                   "d_pin")

        def cmp(me, j, pc):
            sl = (j % 2) * 256
            slb = (j % 2) * 512
            me.wait("scalar", "d_pin", pc["din"])
            me.on("scalar", lambda sc: sc.activation(
                out=sb["psc"].bitcast(BF16)[:, slb:slb + w], in_=sb["pin"][:, sl:sl + w],
                func=AF.Copy, bias=0.0, scale=1.0), inc="c_s")
            me.mark(("ppc", j), "c_s")

        def fin(me, j, pc):
            slb = (j % 2) * 512
            m0, nm = c0 // P, w // P
            me.dma("scalar", dst4[:, m0:m0 + nm, k, :],
                   sb["psc"].bitcast(BF16)[:, slb:slb + w].rearrange("p (m q) -> p m q", q=P),
                   "d_po")
            pc["dout"] = me.cnt["d_po"]
            if endmark is not None:
                me.mark(("pd", endmark), "d_po")
        return {"dma": dma, "cmp": cmp, "fin": fin}

    def _mk_dq(self, qsrc, ssrc, dst4, k, c0, w, eng, endmark=None):
        sb = self.sb
        eng = "vector"
        csem = "c_v"

        def dma(me, j):
            sl = (j % 2) * 256
            me.dma("scalar", sb["pin"].bitcast(I32)[:, sl:sl + w],
                   qsrc[k * P:(k + 1) * P, c0:c0 + w], "d_pin")
            me.dma("scalar", sb["psc"][:, sl:sl + w],
                   ssrc[k:k + 1, c0:c0 + w].broadcast_to([P, w]), "d_pin")

        def cmp(me, j, pc):
            sl = (j % 2) * 256
            slb = (j % 2) * 512
            me.wait("scalar", "d_pin", pc["din"])
            me.on("scalar", lambda sc: sc.activation(
                out=sb["ptmp"][:, sl:sl + w], in_=sb["pin"].bitcast(I32)[:, sl:sl + w],
                func=AF.Copy, bias=-128.0, scale=1.0), inc="c_s")
            cs = me.cnt["c_s"]
            me.wait(eng, "c_s", cs)
            me.wait(eng, "d_pin", pc["din"])
            me.on(eng, lambda e: e.tensor_tensor(
                out=sb["pin"].bitcast(BF16)[:, slb:slb + w], in0=sb["ptmp"][:, sl:sl + w],
                in1=sb["psc"][:, sl:sl + w], op=AL.mult), inc=csem)
            me.mark(("ppc", j), csem)
            pc["cval"] = me.cnt[csem]

        def fin(me, j, pc):
            slb = (j % 2) * 512
            me.wait("scalar", csem, pc["cval"])
            m0, nm = c0 // P, w // P
            me.dma("scalar", dst4[:, m0:m0 + nm, k, :],
                   sb["pin"].bitcast(BF16)[:, slb:slb + w].rearrange("p (m q) -> p m q", q=P),
                   "d_po")
            pc["dout"] = me.cnt["d_po"]
            if endmark is not None:
                me.mark(("pd", endmark), "d_po")
        return {"dma": dma, "cmp": cmp, "fin": fin}

    def _mk_t1(self, src, k, c0, w, aidx, wn, last=False):
        sb = self.sb

        def dma(me, j):
            sl = (j % 2) * 256
            me.dma("scalar", sb["pin"][:, sl:sl + w], src[k * P:(k + 1) * P, c0:c0 + w],
                   "d_pin")

        def cmp(me, j, pc):
            sl = (j % 2) * 256
            me.wait("scalar", "d_pin", pc["din"])
            me.on("scalar", lambda sc: sc.activation(
                out=sb["ptmp"][:, sl:sl + w], in_=sb["pin"][:, sl:sl + w],
                func=AF.Abs, bias=0.0, scale=1.0,
                accum_out=sb["pacc"][:, aidx:aidx + 1]), inc="c_s")
            me.mark(("ppc", j), "c_s")
            if last:
                me.mark(("t1done", wn), "c_s")
        return {"dma": dma, "cmp": cmp, "fin": None}

    def _mk_treduce(self, wn, a0, nacc, nelem):
        sb = self.sb
        cA, cB = 300 + wn, 304 + 4 * wn

        def cmp(me, j, pc):
            me.wmark("vector", ("t1done", wn))
            me.on("vector", lambda v: v.tensor_reduce(
                out=sb["pacc"][:, 260 + wn:261 + wn], in_=sb["pacc"][:, a0:a0 + nacc],
                axis=AX.X, op=AL.add), inc="c_v")
            cv = me.cnt["c_v"]
            me.wait("tensor", "c_v", cv)
            me.wmark("tensor", ("blk1", NC1 - 1))
            me.on("tensor", lambda t: t.matmul(
                sb["ps_x"][0:1, cA:cA + 1], sb["pacc"][:, 260 + wn:261 + wn],
                sb["onesT"][:, 0:1], start=True, stop=True, skip_group_check=True),
                inc="c_t")
            ct = me.cnt["c_t"]
            me.wait("vector", "c_t", ct)
            me.on("vector", lambda v: v.tensor_scalar(
                out=sb["pacc"][0:1, 264 + wn:265 + wn], in0=sb["ps_x"][0:1, cA:cA + 1],
                scalar1=1.0 / nelem, scalar2=1e-8, op0=AL.mult, op1=AL.max), inc="c_v")
            me.on("vector", lambda v: v.reciprocal(
                out=sb["pacc"][0:1, 264 + wn:265 + wn],
                in_=sb["pacc"][0:1, 264 + wn:265 + wn]), inc="c_v")
            cv2 = me.cnt["c_v"]
            me.wait("tensor", "c_v", cv2)
            me.on("tensor", lambda t: t.matmul(
                sb["ps_x"][:, cB:cB + 1], sb["onesT"][0:1, 0:P],
                sb["pacc"][0:1, 264 + wn:265 + wn], start=True, stop=True,
                skip_group_check=True), inc="c_t")
            ct2 = me.cnt["c_t"]
            me.wait("vector", "c_t", ct2)
            me.on("vector", lambda v: v.tensor_copy(
                out=sb["pacc"][:, 256 + wn:257 + wn], in_=sb["ps_x"][:, cB:cB + 1]),
                inc="c_v")
            me.mark(("trec", wn), "c_v")
            me.mark(("ppc", j), "c_v")
        return {"dma": None, "cmp": cmp, "fin": None,
                "hold": ("blk1", NC1 - 1)}

    def _mk_t2(self, src, dst4, k, c0, w, wn, eng, endmark=None):
        sb = self.sb
        eng = "vector"
        csem = "c_v"

        def dma(me, j):
            sl = (j % 2) * 256
            me.dma("scalar", sb["pin"][:, sl:sl + w], src[k * P:(k + 1) * P, c0:c0 + w],
                   "d_pin")

        def cmp(me, j, pc):
            sl = (j % 2) * 256
            slb = (j % 2) * 512
            me.wait("scalar", "d_pin", pc["din"])
            me.wmark("scalar", ("trec", wn))
            recip = sb["pacc"][:, 256 + wn:257 + wn]
            me.on("scalar", lambda sc: sc.activation(
                out=sb["ptmp"][:, sl:sl + w], in_=sb["pin"][:, sl:sl + w],
                func=AF.Abs, bias=0.0, scale=recip))
            me.on("scalar", lambda sc: sc.activation(
                out=sb["psc"][:, sl:sl + w], in_=sb["pin"][:, sl:sl + w],
                func=AF.Sign, bias=0.0, scale=1.0), inc="c_s")
            cs = me.cnt["c_s"]
            me.wait(eng, "c_s", cs)
            me.on(eng, lambda e: e.tensor_scalar(
                out=sb["ptmp"][:, sl:sl + w], in0=sb["ptmp"][:, sl:sl + w],
                scalar1=0.5, scalar2=None, op0=AL.is_gt))
            me.on(eng, lambda e: e.tensor_tensor(
                out=sb["pin"].bitcast(BF16)[:, slb:slb + w], in0=sb["ptmp"][:, sl:sl + w],
                in1=sb["psc"][:, sl:sl + w], op=AL.mult), inc=csem)
            me.mark(("ppc", j), csem)
            pc["cval"] = me.cnt[csem]

        def fin(me, j, pc):
            slb = (j % 2) * 512
            me.wait("scalar", csem, pc["cval"])
            m0, nm = c0 // P, w // P
            me.dma("scalar", dst4[:, m0:m0 + nm, k, :],
                   sb["pin"].bitcast(BF16)[:, slb:slb + w].rearrange("p (m q) -> p m q", q=P),
                   "d_po")
            pc["dout"] = me.cnt["d_po"]
            if endmark is not None:
                me.mark(("pd", endmark), "d_po")
        return {"dma": dma, "cmp": cmp, "fin": fin}

    def _pump_tasks(self):
        self.ptasks = []
        if self.stage < 2:
            return
        i, s = self.i, self.scr

        def pieces(ncols, wmax=256):
            out = []
            c0 = 0
            while c0 < ncols:
                out.append((c0, min(wmax, ncols - c0)))
                c0 += wmax
            return out

        def add_dq(qap, sap, dst, kchunks, ncols, eng, name):
            dst4 = dst.rearrange("p (m kk q) -> p m kk q", kk=kchunks, q=P)
            items = [(k, c0, w) for k in range(kchunks) for (c0, w) in pieces(ncols)]
            for n, (k, c0, w) in enumerate(items):
                self.ptasks.append(self._mk_dq(
                    qap, sap, dst4, k, c0, w, eng,
                    endmark=(name if n == len(items) - 1 else None)))

        def add_cast(src, dst, kchunks, ncols, name):
            dst4 = dst.rearrange("p (m kk q) -> p m kk q", kk=kchunks, q=P)
            items = [(k, c0, w) for k in range(kchunks) for (c0, w) in pieces(ncols)]
            for n, (k, c0, w) in enumerate(items):
                self.ptasks.append(self._mk_cast(
                    src, dst4, k, c0, w,
                    endmark=(name if n == len(items) - 1 else None)))

        def add_tern(src, dst, kchunks, ncols, wn, a0, name):
            dst4 = dst.rearrange("p (m kk q) -> p m kk q", kk=kchunks, q=P)
            items = [(k, c0, w) for k in range(kchunks) for (c0, w) in pieces(ncols)]
            t1 = []
            for n, (k, c0, w) in enumerate(items):
                t1.append(self._mk_t1(src, k, c0, w, a0 + n, wn,
                                      last=(n == len(items) - 1)))
            t2 = [self._mk_t2(src, dst4, k, c0, w, wn, "gpsimd",
                              endmark=(name if n == len(items) - 1 else None))
                  for n, (k, c0, w) in enumerate(items)]
            red = self._mk_treduce(wn, a0, len(items), kchunks * P * ncols)
            return t1, red, t2

        add_dq(i["fc1q"], i["fc1s"], s["fc1"], G, FF, "gpsimd", "fc1")
        add_dq(i["fc2q"], i["fc2s"], s["fc2"], GF, D, "vector", "fc2")
        if self.stage >= 3:
            add_cast(i["wqT"], s["wq"], G, D, "wq")
            add_cast(i["wkT"], s["wk"], G, D, "wk")
            add_cast(i["wvT"], s["wv"], G, D, "wv")
            add_cast(i["woT"], s["wo"], G, D, "wo")
            add_cast(i["wg1T"], s["wg1"], G, P, "wg1")
            add_cast(i["wg2T"], s["wg2"], 1, D, "wg2")
        if self.stage >= 4:
            t1a, reda, t2a = add_tern(i["w1T"], s["w1"], G, FF, 0, 0, "w1")
            t1b, redb, t2b = add_tern(i["w2T"], s["w2"], GF, D, 1, 128, "w2")
            self.ptasks += t1a + t1b + [reda, redb] + t2a + t2b

    def pump_step(self, n=1):
        for _ in range(n):
            self._pump_step1()

    def _pump_step1(self):
        tl = self.ptasks
        n = len(tl)
        jh = min(self.p_cmp, self.p_dma)
        if jh < n and tl[jh].get("hold") is not None and tl[jh]["hold"] not in self.marks:
            return
        if self.p_out < n and self.p_out < self.p_cmp:
            j = self.p_out
            if tl[j]["fin"] is not None:
                tl[j]["fin"](self, j, self.pstate[j])
            self.p_out += 1
        if self.p_cmp < n and self.p_cmp < self.p_dma:
            j = self.p_cmp
            tl[j]["cmp"](self, j, self.pstate[j])
            self.p_cmp += 1
        j = self.p_dma
        if j < n:
            self.pstate[j] = {"din": 0}
            if tl[j]["dma"] is not None:
                if j >= 2:
                    self.wmark("scalar", ("ppc", j - 2))
                    self.wait("scalar", "d_po", self.pstate[j - 2].get("dout", 0))
                tl[j]["dma"](self, j)
                self.pstate[j]["din"] = self.cnt["d_pin"]
            self.p_dma += 1

    def pump_drain(self):
        guard = 0
        while self.p_out < len(self.ptasks):
            self.pump_step()
            guard += 1
            if guard > 10000:
                raise RuntimeError("pump_drain stuck")

    def _reset(self, eng_name, eng):
        self.eng_name = eng_name
        self.eng = eng
        self.cnt = {k: 0 for k in self.semh}
        self.marks = {}
        self.pmgate = {0: None, 1: None}
        self.sqgate = None
        self._selfw = {}
        self.slot_gate = {}
        self._mk = 0
        self.p_dma = 0
        self.p_cmp = 0
        self.p_out = 0
        self.pstate = {}
        self.scanfeed = {}
        self.wstfeed = {}

    def on(self, name, fn, inc=None, n=1):
        if self.eng_name == name:
            if inc is not None and self.cnt[inc] > self._selfw.get(inc, 0):
                self.eng.wait_ge(self.semh[inc], self.cnt[inc])
                self._selfw[inc] = self.cnt[inc]
            ins = fn(self.eng)
            if inc is not None:
                ins.then_inc(self.semh[inc], n)
        if inc is not None:
            self.cnt[inc] += n
        return self.cnt.get(inc)

    def wait(self, name, sem, val):
        if val is None or val <= 0:
            return
        if self.eng_name == name:
            self.eng.wait_ge(self.semh[sem], val)

    def dma(self, q, out, in_, sem="d_w", **kw):
        if self.eng_name == q:
            self.eng.dma_start(out=out, in_=in_, **kw).then_inc(self.semh[sem], 16)
        self.cnt[sem] += 16
        return self.cnt[sem]

    def mark(self, key, sem):
        self.marks[key] = (sem, self.cnt[sem])

    def wmark(self, name, key):
        if key in self.marks:
            sem, val = self.marks[key]
            self.wait(name, sem, val)

    def walk(self):
        self.prologue()
        self.l0_pipeline()
        if self.stage >= 2:
            self.ffn0()
        if self.stage >= 3:
            self.l1_attn()
        if self.stage >= 4:
            self.ffn1()
        self.epilogue()

    def prologue(self):
        sb, i = self.sb, self.i
        for g in range(G):
            self.dma("sync", sb["stream"][:, g * T:(g + 1) * T], i["x"][g * P:(g + 1) * P, :], "d_in")
        self.dma("sync", sb["onesT"][:], i["ones"], "d_misc")
        self.dma("sync", sb["identf"][:], i["identity"], "d_misc")
        self.dma("sync", sb["amisc"][:], i["alog"], "d_misc")
        self.dma("sync", sb["brow"][0:1, 0:1024], i["brow_l0"], "d_misc")
        self.dma("sync", sb["smallA"][:, 40:48], i["inb_z"], "d_misc")
        self.dma("sync", sb["smallA"][:, 48:56], i["inb_xa"], "d_misc")
        self.dma("sync", sb["smallA"][:, 0:32], i["convw"], "d_misc")
        self.dma("sync", sb["smallA"][:, 32:40], i["convb"], "d_misc")
        self.dma("sync", sb["smallB"][:, 0:8], i["lnw"], "d_misc")
        self.dma("sync", sb["smallB"][:, 8:16], i["lnb"], "d_misc")
        self.dma("sync", sb["bcbrow"][:], i["bcb"], "d_misc")
        if self.stage >= 2:
            self.dma("sync", sb["smallB"][:, 16:24], i["ffg"], "d_misc")
            self.dma("sync", sb["smallB"][:, 24:32], i["ffb"], "d_misc")
            self.dma("sync", sb["smallB"][:, 32:40], i["fc2b"], "d_misc")
            self.dma("sync", sb["smallA"][:, 64:96], i["fc1b"], "d_misc")
        if self.stage >= 3:
            self.dma("sync", sb["smallB"][:, 40:48], i["gnorm"], "d_misc")
        if self.stage >= 4:
            self.dma("sync", sb["smallB"][:, 48:56], i["w2b"], "d_misc")
            self.dma("sync", sb["smallA"][:, 96:128], i["w1b"], "d_misc")
        nmisc = self.cnt["d_misc"]

        self.wait("vector", "d_misc", nmisc)
        self.on("vector", lambda v: v.tensor_scalar(out=sb["amisc"][:], in0=sb["amisc"][:],
                                                    scalar1=2.0, scalar2=-5.0, op0=AL.min, op1=AL.max),
                inc="c_v")
        self.wait("scalar", "c_v", self.cnt["c_v"])
        self.on("scalar", lambda sc: sc.activation(out=sb["amisc"][:], in_=sb["amisc"][:],
                                                   func=AF.Exp, bias=0.0, scale=1.0), inc="c_s")
        self.wait("vector", "c_s", self.cnt["c_s"])
        self.on("vector", lambda v: v.tensor_scalar(out=sb["ps_a"][:, 0:GN], in0=sb["amisc"][:],
                                                    scalar1=-1.0, scalar2=None, op0=AL.mult), inc="c_v")
        self.on("vector", lambda v: v.memset(sb["sprime"][:], 0.0), inc="c_v")
        self.on("vector", lambda v: v.memset(sb["rbuf"][:], 0.0), inc="c_v")
        self.on("vector", lambda v: v.memset(sb["cst"][:, 0:1], EPS), inc="c_v")
        
        self.on("vector", lambda v: v.memset(sb["cst"][:, 1:2], 0.5), inc="c_v")
        self.on("vector", lambda v: v.memset(sb["cst"][:, 2:3], 0.08838834764831845), inc="c_v")
        self.on("vector", lambda v: v.tensor_copy(out=sb["identb"][:], in_=sb["identf"][:]), inc="c_v")
        self.on("vector", lambda v: v.tensor_reduce(
            out=sb["smallA"][:, 56:64],
            in_=sb["smallA"][:, 0:32].rearrange("p (g j) -> p g j", j=4),
            axis=AX.X, op=AL.add), inc="c_v")
        self.on("vector", lambda v: v.tensor_tensor(
            out=sb["smallA"][:, 56:64], in0=sb["smallA"][:, 56:64],
            in1=sb["smallA"][:, 48:56], op=AL.mult), inc="c_v")
        self.on("vector", lambda v: v.tensor_tensor(
            out=sb["smallA"][:, 32:40], in0=sb["smallA"][:, 32:40],
            in1=sb["smallA"][:, 56:64], op=AL.add), inc="c_v")
        self.mark("init_v", "c_v")
        self.on("gpsimd", lambda g: g.memset(sb["xab"][:], 0.0), inc="c_g")
        self.mark("init_g", "c_g")

        self._cast_tiled(i["inwT"], self.scr["inw"], kchunks=G, mchunks=16)
        self.mark(("castinw",), "d_cs")
        self._cast_tiled(i["outwT"], self.scr["outw"], kchunks=G, mchunks=G)
        self.mark(("castoutw",), "d_cs")
        self.wait("sync", "d_cs", self.cnt["d_cs"])
        for k in range(G):
            self.dma("sync", sb["stage"][:, k * 64:(k + 1) * 64], i["bcwT"][k * P:(k + 1) * P, :], "d_c")
        self.wait("vector", "d_c", self.cnt["d_c"])
        self.wait("vector", "d_cs", self.cnt["d_cs"])
        self.on("vector", lambda v: v.tensor_copy(out=sb["stageb"][:, 0:512],
                                                  in_=sb["stage"][:, 0:512]), inc="c_v")
        self.mark("bcw", "c_v")

    def _cast_tiled(self, src, dst, kchunks, mchunks):
        sb = self.sb
        ncols = mchunks * P
        dst4 = dst.rearrange("p (m kk q) -> p m kk q", kk=kchunks, q=P)
        for k in range(kchunks):
            for c0 in range(0, ncols, 1024):
                w = min(1024, ncols - c0)
                self.wait("sync", "d_cs", self.cnt["d_cs"])
                self.dma("sync", sb["stage"][:, 0:w], src[k * P:(k + 1) * P, c0:c0 + w], "d_c")
                self.wait("vector", "d_c", self.cnt["d_c"])
                self.wait("vector", "d_cs", self.cnt["d_cs"])
                self.on("vector", lambda v, w=w: v.tensor_copy(out=sb["stageb"][:, 0:w],
                                                               in_=sb["stage"][:, 0:w]), inc="c_v")
                self.wait("sync", "c_v", self.cnt["c_v"])
                m0, nm = c0 // P, w // P
                out_ap = dst4[:, m0:m0 + nm, k, :]
                in_ap = sb["stageb"][:, 0:w].rearrange("p (m q) -> p m q", q=P)
                self.dma("sync", out_ap, in_ap, "d_cs")

    def l0_pipeline(self):
        self.wait("scalar", "d_in", 16 * G)
        self.wait("tensor", "d_in", 16 * G)
        for w in range(-3, NC1 + 2):
            for k in range(NSC):
                self._slot(w, k)

    def _slot(self, w, k):
        cf, kf = w + (k + 2) // NSC, (k + 2) % NSC
        cc, kc = w + (k + 1) // NSC, (k + 1) % NSC
        if 0 <= cf < NC1:
            self._scan_feed(cf, kf)
        if 0 <= cc < NC1:
            self._scan_bx(cc, kc)
            self._scan_cb(cc, kc)
        if k == 0 and 0 <= w - 1 < NC1:
            self._ym(w - 1)
        c_op = w - 2
        if 0 <= c_op < NC1:
            if k == 0:
                self._lnr_a(c_op)
            elif k == 1:
                self._lnr_b(c_op)
            if k < 8:
                self._outproj_tile(c_op, k)
            if 2 <= k < 10:
                self._hadd(c_op, k - 2)
            if k == 9:
                h0 = (c_op % 2) * G * T1
                hsrc = lambda g: self.sb["hb"][:, h0 + g * T1:h0 + (g + 1) * T1]
                self._lnst_a(hsrc, "ps_w0" if c_op % 2 == 0 else "ps_w1",
                             ("lnh", c_op), 512, src_gate=("hadd", c_op))
            elif k == 10:
                self._lnst_b("ps_w0" if c_op % 2 == 0 else "ps_w1", ("lnh", c_op), 512)
            elif k >= 12:
                self._lnh_apply(c_op, (k - 12) * 2)
        if 0 <= w < NC1:
            self._scan_steps(w, k)
        c_ip = w + 2
        if 0 <= c_ip < NC1:
            self._inproj_tile(c_ip, k)
        c_cv = w + 1
        if 0 <= c_cv < NC1:
            if 1 <= k <= 8:
                self._conv_g(c_cv, k - 1)
            elif k == 10:
                self._bmcm(c_cv)
        c_ln = w + 3
        if 0 <= c_ln < NC1:
            if k == 11:
                src = lambda g, c_ln=c_ln: self.sb["stream"][
                    :, g * T + c_ln * T1:g * T + c_ln * T1 + T1]
                self._lnst_a(src, "ps_w0" if c_ln % 2 == 0 else "ps_w1",
                             ("ln0", c_ln), 1024)
            elif k == 12:
                self._lnst_b("ps_w0" if c_ln % 2 == 0 else "ps_w1", ("ln0", c_ln), 1024)
            elif k in (14, 15):
                self._ln0_apply(c_ln, (k - 14) * 4)
        if w == NC1 - 4 and k == 9:
            sb = self.sb
            xa0 = ((NC1 - 1) % 3) * G * (T1 + 3)
            self.wmark("vector", ("convg", NC1 - 4, G - 1))
            self.on("vector", lambda v: v.memset(
                sb["xab"][:, xa0:xa0 + G * (T1 + 3)].rearrange(
                    "p (g q) -> p g q", g=G)[:, :, T1 + 2:T1 + 3], 0.0), inc="c_v")
            for g in range(G):
                self.mark(("halo", NC1 - 1, g), "c_v")
        self.pump_step(2)

    def _scan_feed(self, c, kf):
        sb, s = self.sb, self.scr
        ki = c * NSC + kf
        par = ki % 2
        t0 = c * T1 + kf * T2
        bsem = "d_b0" if par == 0 else "d_b1"
        self.wmark("sync", ("bmdram", c))
        self.wmark("sync", ("bx", ki - 2))
        self.wmark("sync", ("cb", ki - 2))
        self.dma("sync", sb["bbb"][:, par * T2 * NST:(par + 1) * T2 * NST],
                 s["bm"][0:1, t0 * NST:(t0 + T2) * NST].broadcast_to([P, T2 * NST]), bsem)
        self.dma("sync", sb["cstrip"][0:1, par * T2 * NST:(par + 1) * T2 * NST],
                 s["cm"][0:1, t0 * NST:(t0 + T2) * NST], bsem)
        self.scanfeed[ki] = self.cnt[bsem]

    def _scan_cb(self, c, kc):
        sb = self.sb
        ki = c * NSC + kc
        par = ki % 2
        bsem = "d_b0" if par == 0 else "d_b1"
        cbb = sb["ps_cb0"] if par == 0 else sb["ps_cb1"]
        self.wait("tensor", bsem, self.scanfeed[ki])
        self.wmark("tensor", ("scand", ki - 2))
        self.on("tensor", lambda t, cbb=cbb, par=par: t.matmul(
            cbb[:, 0:T2 * NST], sb["onesT"][0:1, 0:P],
            sb["cstrip"][0:1, par * T2 * NST:(par + 1) * T2 * NST], start=True, stop=True),
            inc="c_t")
        self.mark(("cb", ki), "c_t")

    def _scan_bx(self, c, kc):
        sb = self.sb
        ki = c * NSC + kc
        par = ki % 2
        xc0 = (c % 2) * G * T1
        bsem = "d_b0" if par == 0 else "d_b1"
        self.wait("gpsimd", bsem, self.scanfeed[ki])
        self.wmark("gpsimd", ("xc", c))
        self.wmark("gpsimd", ("scand", ki - 2))
        bb3 = sb["bbb"][:, par * T2 * NST:(par + 1) * T2 * NST].rearrange(
            "p (u n) -> p u n", n=NST).unsqueeze(2).broadcast_to([P, T2, G, NST])
        xc3 = sb["xcb"][:, xc0:xc0 + G * T1].rearrange("p (g q) -> p g q", g=G)[
            :, :, kc * T2:(kc + 1) * T2].transpose([0, 2, 1]).unsqueeze(3).broadcast_to(
            [P, T2, G, NST])
        bx3 = sb["bxb"][:, par * T2 * GN:(par + 1) * T2 * GN].rearrange(
            "p (u g n) -> p u g n", g=G, n=NST)
        self.on("gpsimd", lambda gp, bx3=bx3, bb3=bb3, xc3=xc3: gp.tensor_tensor(
            out=bx3, in0=bb3, in1=xc3, op=AL.mult), inc="c_g")
        self.mark(("bx", ki), "c_g")

    def _scan_steps(self, c, k):
        sb = self.sb
        ki = c * NSC + k
        par = ki % 2
        cbb = sb["ps_cb0"] if par == 0 else sb["ps_cb1"]
        self.wmark("vector", ("cb", ki))
        self.wmark("vector", ("bx", ki))
        self.wmark("vector", ("extract", ki - 1))
        if ki == 0:
            self.wmark("vector", "init_v")
        for u in range(T2):
            bxo = par * T2 * GN + u * GN
            self.on("vector", lambda v: v.tensor_tensor(
                out=sb["ps_a"][:, 256:256 + GN], in0=sb["sprime"][:],
                in1=sb["ps_a"][:, 0:GN], op=AL.mult))
            self.on("vector", lambda v, bxo=bxo: v._custom_dve(
                ADDCLIP, out=sb["sprime"][:], in0=sb["bxb"][:, bxo:bxo + GN],
                in1=sb["ps_a"][:, 256:256 + GN], s0=-10.0, s1=10.0))
            ro = u * RW + 1
            self.on("vector", lambda v, u=u, ro=ro, cbb=cbb: v._custom_dve(
                MULSCAN,
                out=sb["rbuf"][:, ro:ro + GN].rearrange("p (g n) -> p g n", n=NST),
                in0=sb["sprime"][:].rearrange("p (g n) -> p g n", n=NST),
                in1=cbb[:, u * NST:(u + 1) * NST].unsqueeze(1).broadcast_to([P, G, NST])),
                    inc=("c_v" if u == T2 - 1 else None))
        self.mark(("scand", ki), "c_v")
        self.wmark("gpsimd", ("scand", ki))
        r3 = sb["rbuf"][:].rearrange("p (u w) -> p u w", w=RW)
        yv = sb["yb"][:, 0:G * T1].rearrange("p (g q) -> p g q", g=G)[
            :, :, k * T2:(k + 1) * T2].transpose([0, 2, 1])
        self.on("gpsimd", lambda gp, r3=r3, yv=yv: gp.tensor_tensor(
            out=yv, in0=r3[:, :, NST:GN + 1:NST], in1=r3[:, :, 0:GN:NST], op=AL.subtract),
            inc="c_g")
        self.mark(("extract", ki), "c_g")
        if k == NSC - 1:
            self.mark(("scanchunk", c), "c_g")

    def _szb(self, c):
        if c % 3 < 2:
            return self.sb["szb"][:, (c % 3) * G * T1:(c % 3 + 1) * G * T1]
        return self.sb["ln0f"][:, 0:G * T1]

    def _ym(self, c):
        sb = self.sb
        szv = self._szb(c)
        y0 = (c % 2) * G * T1
        self.wmark("gpsimd", ("scanchunk", c))
        self.wmark("gpsimd", ("ipz", c))
        self.wmark("gpsimd", ("opw", c - 2, G - 1))
        for g in range(G):
            self.on("gpsimd", lambda gp, g=g, y0=y0, szv=szv: gp.tensor_tensor(
                out=sb["ymb"][:, y0 + g * T1:y0 + (g + 1) * T1],
                in0=sb["yb"][:, g * T1:(g + 1) * T1],
                in1=szv[:, g * T1:(g + 1) * T1], op=AL.mult),
                    inc=("c_g" if g == G - 1 else None))
        self.mark(("ym", c), "c_g")

    def _lnst_a(self, srcf, bank, tag, soff, src_gate=None):
        sb = self.sb
        bk = sb[bank]
        if src_gate is not None:
            self.wmark("scalar", src_gate)
            self.wmark("tensor", src_gate)
        if self.sqgate is not None:
            self.wmark("scalar", self.sqgate)
        for g in range(G):
            self.on("scalar", lambda s, g=g: s.activation(out=sb["sq"][:, g * T1:(g + 1) * T1],
                                                          in_=srcf(g), func=AF.Square,
                                                          bias=0.0, scale=1.0),
                    inc=("c_s" if g == G - 1 else None))
        cs0 = self.cnt["c_s"]
        self.wait("tensor", "c_s", cs0)
        self.wmark("tensor", ("bankfree", bank))
        for g in range(G):
            self.on("tensor", lambda t, g=g: t.matmul(bk[0:1, 0:T1], sb["onesT"][:, 0:1],
                                                      srcf(g), start=(g == 0), stop=(g == G - 1)))
        for g in range(G):
            self.on("tensor", lambda t, g=g: t.matmul(bk[0:1, T1:2 * T1], sb["onesT"][:, 0:1],
                                                      sb["sq"][:, g * T1:(g + 1) * T1],
                                                      start=(g == 0), stop=(g == G - 1)),
                    inc=("c_t" if g == G - 1 else None))
        self.mark(("sqused", tag), "c_t")
        self.sqgate = ("sqused", tag)
        self.marks[("lnsum", tag)] = ("c_t", self.cnt["c_t"])

    def _lnst_b(self, bank, tag, soff):
        sb = self.sb
        bk = sb[bank]
        st = sb["strip"]
        sem, ctv = self.marks[("lnsum", tag)]
        self.wait("vector", sem, ctv)
        self.on("vector", lambda v: v.tensor_scalar(out=st[0:1, soff:soff + T1], in0=bk[0:1, 0:T1],
                                                    scalar1=1.0 / D, scalar2=None, op0=AL.mult))
        self.on("vector", lambda v: v.tensor_scalar(out=st[0:1, soff + T1:soff + 2 * T1],
                                                    in0=bk[0:1, T1:2 * T1],
                                                    scalar1=1.0 / D, scalar2=None, op0=AL.mult))
        self.on("vector", lambda v: v.tensor_tensor(out=st[0:1, soff + 2 * T1:soff + 3 * T1],
                                                    in0=st[0:1, soff:soff + T1],
                                                    in1=st[0:1, soff:soff + T1], op=AL.mult))
        self.on("vector", lambda v: v.tensor_tensor(out=st[0:1, soff + T1:soff + 2 * T1],
                                                    in0=st[0:1, soff + T1:soff + 2 * T1],
                                                    in1=st[0:1, soff + 2 * T1:soff + 3 * T1],
                                                    op=AL.subtract), inc="c_v")
        self.wait("scalar", "c_v", self.cnt["c_v"])
        self.on("scalar", lambda s: s.activation(out=st[0:1, soff + 2 * T1:soff + 3 * T1],
                                                 in_=st[0:1, soff + T1:soff + 2 * T1],
                                                 func=AF.Sqrt, bias=self.sb["cst"][0:1, 0:1],
                                                 scale=1.0), inc="c_s")
        self.wait("vector", "c_s", self.cnt["c_s"])
        self.on("vector", lambda v: v.reciprocal(out=st[0:1, soff + 2 * T1:soff + 3 * T1],
                                                 in_=st[0:1, soff + 2 * T1:soff + 3 * T1]))
        self.on("vector", lambda v: v.tensor_tensor(out=st[0:1, soff + 3 * T1:soff + 4 * T1],
                                                    in0=st[0:1, soff:soff + T1],
                                                    in1=st[0:1, soff + 2 * T1:soff + 3 * T1],
                                                    op=AL.mult), inc="c_v")
        self.wait("tensor", "c_v", self.cnt["c_v"])
        self.on("tensor", lambda t: t.matmul(bk[:, 256:256 + T1], sb["onesT"][0:1, 0:P],
                                             st[0:1, soff + 2 * T1:soff + 3 * T1],
                                             start=True, stop=True))
        self.on("tensor", lambda t: t.matmul(bk[:, 384:384 + T1], sb["onesT"][0:1, 0:P],
                                             st[0:1, soff + 3 * T1:soff + 4 * T1],
                                             start=True, stop=True), inc="c_t")
        self.mark(("lnb", tag), "c_t")

    def _lnstats(self, srcf, bank, tag):
        self._lnst_a(srcf, bank, tag, 0)
        self._lnst_b(bank, tag, 0)
        return self.cnt["c_t"]

    def _lnr_a(self, c):
        src = lambda g: self.sb["stream"][:, g * T + c * T1:g * T + c * T1 + T1]
        sb = self.sb
        bk = sb["ps_x"]
        if self.sqgate is not None:
            self.wmark("scalar", self.sqgate)
        for g in range(G):
            self.on("scalar", lambda s, g=g: s.activation(out=sb["sq"][:, g * T1:(g + 1) * T1],
                                                          in_=src(g), func=AF.Square,
                                                          bias=0.0, scale=1.0),
                    inc=("c_s" if g == G - 1 else None))
        cs0 = self.cnt["c_s"]
        self.wait("tensor", "c_s", cs0)
        self.wmark("tensor", ("hadd", c - 1))
        self.wmark("tensor", ("bmev", c + 2))
        for g in range(G):
            self.on("tensor", lambda t, g=g: t.matmul(bk[0:1, 64:64 + T1], sb["onesT"][:, 0:1],
                                                      src(g), start=(g == 0), stop=(g == G - 1)))
        for g in range(G):
            self.on("tensor", lambda t, g=g: t.matmul(bk[0:1, 64 + T1:64 + 2 * T1],
                                                      sb["onesT"][:, 0:1],
                                                      sb["sq"][:, g * T1:(g + 1) * T1],
                                                      start=(g == 0), stop=(g == G - 1)),
                    inc="c_t")
        self.mark(("sqused", ("lnr", c)), "c_t")
        self.sqgate = ("sqused", ("lnr", c))
        self.marks[("lnsum", ("lnr", c))] = ("c_t", self.cnt["c_t"])

    def _lnr_b(self, c):
        sb = self.sb
        bk = sb["ps_x"]
        st = sb["strip"]
        soff = 0
        sem, ctv = self.marks[("lnsum", ("lnr", c))]
        self.wait("vector", sem, ctv)
        self.on("vector", lambda v: v.tensor_scalar(out=st[0:1, soff:soff + T1],
                                                    in0=bk[0:1, 64:64 + T1],
                                                    scalar1=1.0 / D, scalar2=None, op0=AL.mult))
        self.on("vector", lambda v: v.tensor_scalar(out=st[0:1, soff + T1:soff + 2 * T1],
                                                    in0=bk[0:1, 64 + T1:64 + 2 * T1],
                                                    scalar1=1.0 / D, scalar2=None, op0=AL.mult))
        self.on("vector", lambda v: v.tensor_tensor(out=st[0:1, soff + 2 * T1:soff + 3 * T1],
                                                    in0=st[0:1, soff:soff + T1],
                                                    in1=st[0:1, soff:soff + T1], op=AL.mult))
        self.on("vector", lambda v: v.tensor_tensor(out=st[0:1, soff + T1:soff + 2 * T1],
                                                    in0=st[0:1, soff + T1:soff + 2 * T1],
                                                    in1=st[0:1, soff + 2 * T1:soff + 3 * T1],
                                                    op=AL.subtract), inc="c_v")
        self.wait("scalar", "c_v", self.cnt["c_v"])
        self.on("scalar", lambda s: s.activation(out=st[0:1, soff + 2 * T1:soff + 3 * T1],
                                                 in_=st[0:1, soff + T1:soff + 2 * T1],
                                                 func=AF.Sqrt, bias=self.sb["cst"][0:1, 0:1],
                                                 scale=1.0), inc="c_s")
        self.wait("vector", "c_s", self.cnt["c_s"])
        self.on("vector", lambda v: v.reciprocal(out=st[0:1, soff + 2 * T1:soff + 3 * T1],
                                                 in_=st[0:1, soff + 2 * T1:soff + 3 * T1]))
        self.on("vector", lambda v: v.tensor_tensor(out=st[0:1, soff + 3 * T1:soff + 4 * T1],
                                                    in0=st[0:1, soff:soff + T1],
                                                    in1=st[0:1, soff + 2 * T1:soff + 3 * T1],
                                                    op=AL.mult), inc="c_v")
        self.wait("tensor", "c_v", self.cnt["c_v"])
        self.on("tensor", lambda t: t.matmul(bk[:, 256:256 + T1], sb["onesT"][0:1, 0:P],
                                             st[0:1, soff + 2 * T1:soff + 3 * T1],
                                             start=True, stop=True))
        self.on("tensor", lambda t: t.matmul(bk[:, 384:384 + T1], sb["onesT"][0:1, 0:P],
                                             st[0:1, soff + 3 * T1:soff + 4 * T1],
                                             start=True, stop=True), inc="c_t")
        self.mark(("lnrb", c), "c_t")

    def _ln0_apply(self, c, g0):
        sb = self.sb
        bank = "ps_w0" if c % 2 == 0 else "ps_w1"
        bk = sb[bank]
        lb = (c % 2) * G * T1
        self.wmark("vector", ("lnb", ("ln0", c)))
        if g0 == 0:
            self.wmark("vector", ("ipw", c - 2, 15))
        for g in range(g0, g0 + 4):
            src = sb["stream"][:, g * T + c * T1:g * T + c * T1 + T1]
            self.on("vector", lambda v, g=g, src=src: v.tensor_tensor(
                out=sb["stage"][:, 512:512 + T1], in0=src, in1=bk[:, 256:256 + T1],
                op=AL.mult))
            self.on("vector", lambda v, g=g: v.tensor_tensor(
                out=sb["ln0b"][:, lb + g * T1:lb + (g + 1) * T1],
                in0=sb["stage"][:, 512:512 + T1], in1=bk[:, 384:384 + T1], op=AL.subtract),
                    inc=("c_v" if g == g0 + 3 else None))
        if g0 + 4 == G:
            self.mark(("ln0", c), "c_v")
            self.mark(("bankfree", bank), "c_v")

    def _inproj_tile(self, c, m):
        sb, s = self.sb, self.scr
        lb = (c % 2) * G * T1
        xa0 = (c % 3) * G * (T1 + 3)
        wslot = (m % 2) * G * P
        dsem = "d_w0" if m % 2 == 0 else "d_w1"
        if (m % 2) in self.slot_gate:
            self.wmark("sync", self.slot_gate[m % 2])
        self.wmark("sync", ("castinw",))
        self.dma("sync", sb["wst"][:, wslot:wslot + G * P],
                 s["inw"][:, m * G * P:(m + 1) * G * P], dsem)
        self.wait("tensor", dsem, self.cnt[dsem])
        self.wmark("tensor", ("ln0", c))
        pbi = m % 2
        pb = sb["ps_m0"] if pbi == 0 else sb["ps_m1"]
        if self.pmgate[pbi]:
            self.wmark("tensor", self.pmgate[pbi])
        for k in range(G):
            self.on("tensor", lambda t, k=k, pb=pb, wslot=wslot, lb=lb: t.matmul(
                pb[:, 0:T1], sb["wst"][:, wslot + k * P:wslot + (k + 1) * P],
                sb["ln0b"][:, lb + k * T1:lb + (k + 1) * T1],
                start=(k == 0), stop=(k == G - 1)), inc=("c_t" if k == G - 1 else None))
        self.mark(("ipw", c, m), "c_t")
        self.slot_gate[m % 2] = ("ipw", c, m)
        ctm = self.cnt["c_t"]
        self.wait("scalar", "c_t", ctm)
        if m < G:
            self.on("scalar", lambda sc, m=m, pb=pb: sc.activation(
                out=sb["xab"][:, xa0 + m * (T1 + 3) + 2:xa0 + m * (T1 + 3) + 2 + T1],
                in_=pb[:, 0:T1], func=AF.Copy, bias=0.0, scale=1.0),
                inc="c_s")
            if c > 0:
                pa = ((c - 1) % 3) * G * (T1 + 3)
                self.on("scalar", lambda sc, m=m: sc.copy(
                    out=sb["xab"][:, xa0 + m * (T1 + 3):xa0 + m * (T1 + 3) + 2],
                    in_=sb["xab"][:, pa + m * (T1 + 3) + T1:pa + m * (T1 + 3) + T1 + 2]))
                self.on("scalar", lambda sc, m=m: sc.copy(
                    out=sb["xab"][:, pa + m * (T1 + 3) + T1 + 2:pa + m * (T1 + 3) + T1 + 3],
                    in_=sb["xab"][:, xa0 + m * (T1 + 3) + 2:xa0 + m * (T1 + 3) + 3]),
                        inc="c_s")
                self.mark(("halo", c - 1, m), "c_s")
        else:
            self.wmark("scalar", ("ym", c - 3))
            szv = self._szb(c)
            self.on("scalar", lambda sc, m=m, pb=pb, szv=szv: sc.activation(
                out=szv[:, (m - G) * T1:(m - G + 1) * T1],
                in_=pb[:, 0:T1], func=AF.Silu,
                bias=sb["smallA"][:, 40 + m - G:41 + m - G], scale=1.0),
                inc="c_s")
            if m == 15:
                self.mark(("ipz", c), "c_s")
        self.mark(("ipev", c, m), "c_s")
        self.pmgate[pbi] = ("ipev", c, m)

    def _conv_g(self, c, g):
        sb = self.sb
        xa0 = (c % 3) * G * (T1 + 3)
        xc0 = (c % 2) * G * T1
        par = (g % 2) * 4 * T1
        self.wmark("scalar", ("halo", c, g))
        self.wmark("scalar", ("convg", c, g - 2))
        if g < 2:
            self.wmark("scalar", ("convg", c - 1, g + 6))
        if g == 0:
            self.wmark("scalar", ("scanchunk", c - 2))
            self.wmark("gpsimd", ("scanchunk", c - 2))
        for j in range(4):
            self.on("scalar", lambda sc, g=g, j=j, par=par: sc.activation(
                out=sb["convm"][:, par + j * T1:par + (j + 1) * T1],
                in_=sb["xab"][:, xa0 + g * (T1 + 3) + j:xa0 + g * (T1 + 3) + j + T1],
                func=AF.Copy, bias=0.0, scale=sb["smallA"][:, g * 4 + j:g * 4 + j + 1]),
                    inc=("c_s" if j == 3 else None))
        csg = self.cnt["c_s"]
        self.wait("gpsimd", "c_s", csg)
        self.on("gpsimd", lambda gp, par=par: gp.tensor_tensor(
            out=sb["convm"][:, par:par + T1], in0=sb["convm"][:, par:par + T1],
            in1=sb["convm"][:, par + T1:par + 2 * T1], op=AL.add))
        self.on("gpsimd", lambda gp, par=par: gp.tensor_tensor(
            out=sb["convm"][:, par:par + T1], in0=sb["convm"][:, par:par + T1],
            in1=sb["convm"][:, par + 2 * T1:par + 3 * T1], op=AL.add))
        self.on("gpsimd", lambda gp, g=g, par=par: gp.tensor_tensor(
            out=sb["xcb"][:, xc0 + g * T1:xc0 + (g + 1) * T1], in0=sb["convm"][:, par:par + T1],
            in1=sb["convm"][:, par + 3 * T1:par + 4 * T1], op=AL.add), inc="c_g")
        self.mark(("convg", c, g), "c_g")
        cg = self.cnt["c_g"]
        self.wait("scalar", "c_g", cg)
        self.on("scalar", lambda sc, g=g: sc.activation(
            out=sb["xcb"][:, xc0 + g * T1:xc0 + (g + 1) * T1],
            in_=sb["xcb"][:, xc0 + g * T1:xc0 + (g + 1) * T1], func=AF.Silu,
            bias=sb["smallA"][:, 32 + g:33 + g], scale=1.0))
        self.on("scalar", lambda sc, g=g: sc.copy(
            out=sb["xcbb"][:, xc0 + g * T1:xc0 + (g + 1) * T1],
            in_=sb["xcb"][:, xc0 + g * T1:xc0 + (g + 1) * T1]),
                inc="c_s")
        if g == G - 1:
            self.mark(("xc", c), "c_s")

    def _bmcm(self, c):
        sb, s = self.sb, self.scr
        xc0 = (c % 2) * G * T1
        self.wmark("tensor", ("xc", c))
        self.wmark("tensor", "bcw")
        self.wmark("tensor", ("bmev", c - 1))
        self.wmark("tensor", ("hadd", c - 3))
        for k in range(G):
            self.on("tensor", lambda t, k=k: t.matmul(
                sb["ps_x"][:, 0:2 * NST], sb["xcbb"][:, xc0 + k * T1:xc0 + k * T1 + P],
                sb["stageb"][:, k * 64:(k + 1) * 64], start=(k == 0), stop=False))
        self.on("tensor", lambda t: t.matmul(sb["ps_x"][:, 0:2 * NST], sb["onesT"][0:1, 0:P],
                                             sb["bcbrow"][:], start=False, stop=True), inc="c_t")
        ct = self.cnt["c_t"]
        self.wait("scalar", "c_t", ct)
        self.on("scalar", lambda sc: sc.copy(out=sb["bmcm"][:], in_=sb["ps_x"][:, 0:2 * NST]),
                inc="c_s")
        self.mark(("bmev", c), "c_s")
        self.wmark("sync", ("bmev", c))
        bmv = s["bm"][0:1, c * T1 * NST:(c + 1) * T1 * NST].rearrange("o (t n) -> (o t) n", n=NST)
        cmv = s["cm"][0:1, c * T1 * NST:(c + 1) * T1 * NST].rearrange("o (t n) -> (o t) n", n=NST)
        self.dma("sync", bmv, sb["bmcm"][:, 0:NST], "d_bm")
        self.dma("sync", cmv, sb["bmcm"][:, NST:2 * NST], "d_bm")
        self.mark(("bmdram", c), "d_bm")

    def _outproj_tile(self, c, j):
        sb, s = self.sb, self.scr
        y0 = (c % 2) * G * T1
        h0 = (c % 2) * G * T1
        wslot = (2 + (j % 2)) * G * P
        dsem = "d_w2" if j % 2 == 0 else "d_w3"
        if (2 + j % 2) in self.slot_gate:
            self.wmark("sync", self.slot_gate[2 + j % 2])
        self.wmark("sync", ("castoutw",))
        self.dma("sync", sb["wst"][:, wslot:wslot + G * P],
                 s["outw"][:, j * G * P:(j + 1) * G * P], dsem)
        self.wait("tensor", dsem, self.cnt[dsem])
        self.wmark("tensor", ("ym", c))
        pbi = j % 2
        pb = sb["ps_m0"] if pbi == 0 else sb["ps_m1"]
        if self.pmgate[pbi]:
            self.wmark("tensor", self.pmgate[pbi])
        for k in range(G):
            self.on("tensor", lambda t, k=k, pb=pb, wslot=wslot, y0=y0: t.matmul(
                pb[:, 0:T1], sb["wst"][:, wslot + k * P:wslot + (k + 1) * P],
                sb["ymb"][:, y0 + k * T1:y0 + (k + 1) * T1], start=(k == 0), stop=False))
        self.on("tensor", lambda t, j=j, pb=pb: t.matmul(
            pb[:, 0:T1], sb["brow"][0:1, j * P:(j + 1) * P],
            sb["onesT"][0:1, 0:T1], start=False, stop=True), inc="c_t")
        self.mark(("opw", c, j), "c_t")
        self.slot_gate[2 + j % 2] = ("opw", c, j)
        ct = self.cnt["c_t"]
        self.wait("scalar", "c_t", ct)
        self.wmark("scalar", ("blk1", c - 2))
        self.on("scalar", lambda sc, j=j, pb=pb, h0=h0: sc.activation(
            out=sb["hb"][:, h0 + j * T1:h0 + (j + 1) * T1], in_=pb[:, 0:T1],
            func=AF.Copy, bias=0.0, scale=1.0), inc="c_s")
        self.mark(("opev", c, j), "c_s")
        self.pmgate[pbi] = ("opev", c, j)

    def _hadd(self, c, j):
        sb = self.sb
        h0 = (c % 2) * G * T1
        src = sb["stream"][:, j * T + c * T1:j * T + c * T1 + T1]
        self.wmark("vector", ("lnrb", c))
        self.wmark("vector", ("opev", c, j))
        self.on("vector", lambda v, src=src: v.tensor_tensor(
            out=sb["stage"][:, 0:T1], in0=src, in1=sb["ps_x"][:, 256:256 + T1], op=AL.mult))
        self.on("vector", lambda v: v.tensor_tensor(
            out=sb["stage"][:, 0:T1], in0=sb["stage"][:, 0:T1],
            in1=sb["ps_x"][:, 384:384 + T1], op=AL.subtract))
        self.on("vector", lambda v, j=j, h0=h0: v.tensor_tensor(
            out=sb["hb"][:, h0 + j * T1:h0 + (j + 1) * T1],
            in0=sb["hb"][:, h0 + j * T1:h0 + (j + 1) * T1],
            in1=sb["stage"][:, 0:T1], op=AL.add), inc="c_v")
        if j == G - 1:
            self.mark(("hadd", c), "c_v")
            self.mark(("h", c), "c_v")

    def _lnh_apply(self, c, g0):
        sb = self.sb
        bank = "ps_w0" if c % 2 == 0 else "ps_w1"
        bk = sb[bank]
        h0 = (c % 2) * G * T1
        tg0 = c * T1
        self.wmark("vector", ("lnb", ("lnh", c)))
        for g in range(g0, g0 + 2):
            hsrc = sb["hb"][:, h0 + g * T1:h0 + (g + 1) * T1]
            self.on("vector", lambda v, g=g, hsrc=hsrc: v.tensor_tensor(
                out=sb["stage"][:, 256:256 + T1], in0=hsrc, in1=bk[:, 256:256 + T1],
                op=AL.mult))
            self.on("vector", lambda v, g=g: v.tensor_tensor(
                out=sb["stage"][:, 256:256 + T1], in0=sb["stage"][:, 256:256 + T1],
                in1=bk[:, 384:384 + T1], op=AL.subtract))
            self.on("vector", lambda v, g=g: v.tensor_scalar(
                out=sb["stage"][:, 256:256 + T1], in0=sb["stage"][:, 256:256 + T1],
                scalar1=sb["smallB"][:, g:g + 1], scalar2=sb["smallB"][:, 8 + g:9 + g],
                op0=AL.mult, op1=AL.add))
            self.on("vector", lambda v, g=g, tg0=tg0: v.tensor_tensor(
                out=sb["stream"][:, g * T + tg0:g * T + tg0 + T1],
                in0=sb["stream"][:, g * T + tg0:g * T + tg0 + T1],
                in1=sb["stage"][:, 256:256 + T1], op=AL.add),
                    inc=("c_v" if g == g0 + 1 else None))
        if g0 + 2 == G:
            self.mark(("blk1", c), "c_v")
            self.mark(("bankfree", bank), "c_v")
            self.sqgate = ("blk1", c)

    def _pf_lnf(self, g):
        sb = self.sb
        if g < 4:
            return sb["xcb"][:, g * 512:(g + 1) * 512]
        return sb["xab"][:, (g - 4) * 512:(g - 3) * 512]

    def _pf_h2(self, g):
        sb = self.sb
        if g < 4:
            return sb["xcbb"][:, g * 512:(g + 1) * 512]
        if g < 6:
            return sb["ln0b"][:, (g - 4) * 512:(g - 3) * 512]
        return sb["yb"][:].bitcast(BF16)[:, (g - 6) * 512:(g - 5) * 512]

    def _pf_y1(self, ff):
        sb = self.sb
        if ff < 8:
            return sb["hb"][:].bitcast(BF16)[:, ff * 512:(ff + 1) * 512]
        if ff < 20:
            return sb["ln0f"][:].bitcast(BF16)[:, (ff - 8) * 512:(ff - 7) * 512]
        if ff < 28:
            return sb["szb"][:].bitcast(BF16)[:, (ff - 20) * 512:(ff - 19) * 512]
        return sb["ymb"][:].bitcast(BF16)[:, (ff - 28) * 512:(ff - 27) * 512]

    def _lnstats512(self, srcf, tag, src_gate=None):
        sb = self.sb
        w0, w1 = sb["ps_w0"], sb["ps_w1"]
        NT = 512
        self.wmark("tensor", ("bankfree", "ps_w0"))
        self.wmark("tensor", ("bankfree", "ps_w1"))
        if src_gate is not None:
            self.wmark("tensor", src_gate)
        for g in range(G):
            self.on("tensor", lambda t, g=g: t.matmul(w0[0:1, 0:NT], sb["onesT"][:, 0:1],
                                                      srcf(g), start=(g == 0), stop=(g == G - 1)),
                    inc=("c_t" if g == G - 1 else None))
        self.mark(("ln5sum", tag), "c_t")
        if src_gate is not None:
            self.wmark("scalar", src_gate)
        for g in range(G):
            par = (g % 2) * NT
            if self.sqgate is not None:
                self.wmark("scalar", self.sqgate)
            self.wmark("scalar", ("ln5mm", tag, g - 2))
            self.on("scalar", lambda sc, g=g, par=par: sc.activation(
                out=sb["sq"][:, par:par + NT], in_=srcf(g), func=AF.Square,
                bias=0.0, scale=1.0), inc="c_s")
            css = self.cnt["c_s"]
            self.wait("tensor", "c_s", css)
            self.on("tensor", lambda t, g=g, par=par: t.matmul(
                w1[0:1, 0:NT], sb["onesT"][:, 0:1], sb["sq"][:, par:par + NT],
                start=(g == 0), stop=(g == G - 1), skip_group_check=True),
                    inc="c_t")
            self.mark(("ln5mm", tag, g), "c_t")
        self.sqgate = ("ln5mm", tag, G - 1)
        ct0 = self.cnt["c_t"]
        st = sb["rbuf"]
        self.wait("vector", "c_t", ct0)
        self.on("vector", lambda v: v.tensor_scalar(out=st[0:1, 0:NT], in0=w0[0:1, 0:NT],
                                                    scalar1=1.0 / D, scalar2=None, op0=AL.mult))
        self.on("vector", lambda v: v.tensor_scalar(out=st[0:1, NT:2 * NT], in0=w1[0:1, 0:NT],
                                                    scalar1=1.0 / D, scalar2=None, op0=AL.mult))
        self.on("vector", lambda v: v.tensor_tensor(out=st[0:1, 2 * NT:3 * NT], in0=st[0:1, 0:NT],
                                                    in1=st[0:1, 0:NT], op=AL.mult))
        self.on("vector", lambda v: v.tensor_tensor(out=st[0:1, NT:2 * NT], in0=st[0:1, NT:2 * NT],
                                                    in1=st[0:1, 2 * NT:3 * NT], op=AL.subtract),
                inc="c_v")
        self.wait("scalar", "c_v", self.cnt["c_v"])
        self.on("scalar", lambda sc: sc.activation(out=st[0:1, 2 * NT:3 * NT],
                                                   in_=st[0:1, NT:2 * NT], func=AF.Sqrt,
                                                   bias=self.sb["cst"][0:1, 0:1], scale=1.0),
                inc="c_s")
        self.wait("vector", "c_s", self.cnt["c_s"])
        self.on("vector", lambda v: v.reciprocal(out=st[0:1, 2 * NT:3 * NT],
                                                 in_=st[0:1, 2 * NT:3 * NT]))
        self.on("vector", lambda v: v.tensor_tensor(out=st[0:1, 3 * NT:4 * NT], in0=st[0:1, 0:NT],
                                                    in1=st[0:1, 2 * NT:3 * NT], op=AL.mult),
                inc="c_v")
        self.wait("tensor", "c_v", self.cnt["c_v"])
        self.on("tensor", lambda t: t.matmul(w0[:, 0:NT], sb["onesT"][0:1, 0:P],
                                             st[0:1, 2 * NT:3 * NT], start=True, stop=True))
        self.on("tensor", lambda t: t.matmul(w1[:, 0:NT], sb["onesT"][0:1, 0:P],
                                             st[0:1, 3 * NT:4 * NT], start=True, stop=True),
                inc="c_t")
        return self.cnt["c_t"]

    def _dequant(self, qsrc, ssrc, dst, kchunks, mchunks):
        sb = self.sb
        qint = sb["szb"][:].bitcast(I32)
        dst4 = dst.rearrange("p (m kk q) -> p m kk q", kk=kchunks, q=P)
        ncols = mchunks * P
        for k in range(kchunks):
            for c0 in range(0, ncols, 1024):
                w = min(1024, ncols - c0)
                self.wait("sync", "d_cs", self.cnt["d_cs"])
                self.wmark("sync", ("dq_used",))
                self.dma("sync", sb["stage"][:, 0:w],
                         ssrc[k:k + 1, c0:c0 + w].broadcast_to([P, w]), "d_c")
                self.dma("sync", qint[:, 0:w], qsrc[k * P:(k + 1) * P, c0:c0 + w], "d_c")
                dc = self.cnt["d_c"]
                self.wait("scalar", "d_c", dc)
                self.wmark("scalar", ("dq_used",))
                self.on("scalar", lambda sc, w=w: sc.activation(
                    out=sb["xcb"][:, 0:w], in_=qint[:, 0:w], func=AF.Copy, bias=0.0, scale=1.0),
                    inc="c_s")
                cs = self.cnt["c_s"]
                self.wait("vector", "c_s", cs)
                self.wait("vector", "d_c", dc)
                self.wait("vector", "d_cs", self.cnt["d_cs"])
                self.on("vector", lambda v, w=w: v.tensor_scalar(
                    out=sb["convm"][:, 0:w], in0=sb["stage"][:, 0:w], scalar1=-128.0,
                    scalar2=None, op0=AL.mult))
                self.on("vector", lambda v, w=w: v.tensor_tensor(
                    out=sb["xcb"][:, 0:w], in0=sb["xcb"][:, 0:w], in1=sb["stage"][:, 0:w],
                    op=AL.mult))
                self.on("vector", lambda v, w=w: v.tensor_tensor(
                    out=sb["stageb"][:, 0:w], in0=sb["xcb"][:, 0:w], in1=sb["convm"][:, 0:w],
                    op=AL.add), inc="c_v")
                self.mark(("dq_used",), "c_v")
                self.wait("sync", "c_v", self.cnt["c_v"])
                m0, nm = c0 // P, w // P
                self.dma("sync", dst4[:, m0:m0 + nm, k, :],
                         sb["stageb"][:, 0:w].rearrange("p (m q) -> p m q", q=P), "d_cs")

    def ffn0(self):
        sb, s = self.sb, self.scr
        self.wmark("sync", ("blk1", NC1 - 1))
        self.wmark("scalar", ("blk1", NC1 - 1))
        self.wmark("vector", ("blk1", NC1 - 1))
        self.wmark("tensor", ("blk1", NC1 - 1))
        self.wmark("gpsimd", ("blk1", NC1 - 1))
        self.wmark("sync", ("ym", NC1 - 1))
        self.wmark("scalar", ("ym", NC1 - 1))
        self.wmark("vector", ("ym", NC1 - 1))
        self.on("vector", lambda v: v.tensor_scalar(
            out=sb["smallB"][:, 56:64], in0=sb["smallB"][:, 32:40], scalar1=0.5, scalar2=None,
            op0=AL.mult), inc="c_v")
        self.wmark("sync", ("pd", "fc1"))
        self.wmark("sync", ("pd", "fc2"))
        NT = 512
        NCF = T // NT
        for c in range(NCF):
            t0 = c * NT
            src = lambda g: sb["stream"][:, g * T + t0:g * T + t0 + NT]
            ctb = self._lnstats512(src, ("f0a", c))
            self.wait("vector", "c_t", ctb)
            self.wmark("vector", ("f2rd", c - 1, G - 1))
            for g in range(G):
                self.on("vector", lambda v, g=g: v.tensor_tensor(
                    out=self._pf_lnf(g), in0=src(g), in1=sb["ps_w0"][:, 0:NT], op=AL.mult))
                self.on("vector", lambda v, g=g: v.tensor_tensor(
                    out=self._pf_lnf(g), in0=self._pf_lnf(g), in1=sb["ps_w1"][:, 0:NT],
                    op=AL.subtract), inc=("c_v" if g == G - 1 else None))
            self.mark(("ln1", c), "c_v")
            self.mark(("bankfree", "ps_w0"), "c_v")
            self.mark(("bankfree", "ps_w1"), "c_v")
            ctb2 = self._lnstats512(self._pf_lnf, ("f0b", c), src_gate=("ln1", c))
            self.wait("vector", "c_t", ctb2)
            self.wmark("vector", ("f1rd", c - 1, GF - 1))
            for g in range(G):
                self.on("vector", lambda v, g=g: v.tensor_tensor(
                    out=sb["convm"][:, 0:NT], in0=self._pf_lnf(g), in1=sb["ps_w0"][:, 0:NT],
                    op=AL.mult))
                self.on("vector", lambda v, g=g: v.tensor_tensor(
                    out=sb["convm"][:, 0:NT], in0=sb["convm"][:, 0:NT],
                    in1=sb["ps_w1"][:, 0:NT], op=AL.subtract))
                self.on("vector", lambda v, g=g: v.tensor_scalar(
                    out=self._pf_h2(g), in0=sb["convm"][:, 0:NT],
                    scalar1=sb["smallB"][:, 16 + g:17 + g], scalar2=sb["smallB"][:, 24 + g:25 + g],
                    op0=AL.mult, op1=AL.add), inc=("c_v" if g == G - 1 else None))
            self.mark(("h2", c), "c_v")
            self.mark(("bankfree", "ps_w0"), "c_v")
            self.mark(("bankfree", "ps_w1"), "c_v")
            def f1dma(ff):
                wslot = (ff % 2) * G * P
                dsem = "d_w0" if ff % 2 == 0 else "d_w1"
                if (ff % 2) in self.slot_gate:
                    self.wmark("sync", self.slot_gate[ff % 2])
                self.dma("sync", sb["wst"][:, wslot:wslot + G * P],
                         s["fc1"][:, ff * G * P:(ff + 1) * G * P], dsem)
                self.wstfeed[("f1", c, ff)] = self.cnt[dsem]
            f1dma(0)
            for ff in range(GF):
                wslot = (ff % 2) * G * P
                dsem = "d_w0" if ff % 2 == 0 else "d_w1"
                if ff + 1 < GF:
                    f1dma(ff + 1)
                self.wait("tensor", dsem, self.wstfeed[("f1", c, ff)])
                self.wmark("tensor", ("h2", c))
                pbi = ff % 2
                pb = sb["ps_m0"] if pbi == 0 else sb["ps_m1"]
                if self.pmgate[pbi]:
                    self.wmark("tensor", self.pmgate[pbi])
                self.wmark("tensor", ("f1rd", c - 1, ff))
                for k in range(G):
                    self.on("tensor", lambda t, k=k, pb=pb, wslot=wslot: t.matmul(
                        pb[:, 0:NT], sb["wst"][:, wslot + k * P:wslot + (k + 1) * P],
                        self._pf_h2(k), start=(k == 0), stop=(k == G - 1)),
                            inc=("c_t" if k == G - 1 else None))
                self.mark(("f1w", c, ff), "c_t")
                self.slot_gate[ff % 2] = ("f1w", c, ff)
                ct = self.cnt["c_t"]
                self.wait("scalar", "c_t", ct)
                self.on("scalar", lambda sc, ff=ff, pb=pb: sc.activation(
                    out=self._pf_y1(ff), in_=pb[:, 0:NT], func=AF.Silu,
                    bias=sb["smallA"][:, 64 + ff:65 + ff], scale=1.0), inc="c_s")
                self.mark(("f1e", c, ff), "c_s")
                self.pmgate[pbi] = ("f1e", c, ff)
                self.pump_step(2)
            for j in range(G):
                pbi = j % 2
                pb = sb["ps_m0"] if pbi == 0 else sb["ps_m1"]
                if self.pmgate[pbi]:
                    self.wmark("tensor", self.pmgate[pbi])
                self.wmark("tensor", ("f1e", c, GF - 1))
                def f2dma(j2, piece):
                    wslot = (2 + (piece % 2)) * G * P
                    dsem = "d_w2" if piece % 2 == 0 else "d_w3"
                    if (2 + piece % 2) in self.slot_gate:
                        self.wmark("sync", self.slot_gate[2 + piece % 2])
                    self.dma("sync", sb["wst"][:, wslot:wslot + G * P],
                             s["fc2"][:, (j2 * GF + piece * 8) * P:(j2 * GF + piece * 8 + 8) * P],
                             dsem)
                    self.wstfeed[("f2", c, j2, piece)] = self.cnt[dsem]
                if j == 0:
                    f2dma(0, 0)
                for piece in range(4):
                    wslot = (2 + (piece % 2)) * G * P
                    dsem = "d_w2" if piece % 2 == 0 else "d_w3"
                    if piece + 1 < 4:
                        f2dma(j, piece + 1)
                    elif j + 1 < G:
                        f2dma(j + 1, 0)
                    self.wait("tensor", dsem, self.wstfeed[("f2", c, j, piece)])
                    for k8 in range(G):
                        kk = piece * 8 + k8
                        self.on("tensor", lambda t, k8=k8, kk=kk, pb=pb, wslot=wslot: t.matmul(
                            pb[:, 0:NT], sb["wst"][:, wslot + k8 * P:wslot + (k8 + 1) * P],
                            self._pf_y1(kk), start=(kk == 0), stop=(kk == GF - 1)),
                            inc=("c_t" if k8 == G - 1 else None))
                    self.mark(("f2w", c, j, piece), "c_t")
                    self.slot_gate[2 + piece % 2] = ("f2w", c, j, piece)
                    self.pump_step(2)
                self.on("tensor", lambda t: t.nop(), inc="c_t")
                ct = self.cnt["c_t"]
                if j == G - 1:
                    for ff in range(GF):
                        self.mark(("f1rd", c, ff), "c_t")
                self.wait("scalar", "c_t", ct)
                self.wmark("scalar", ("f2v", c, j - 2))
                par = (j % 2) * NT
                self.on("scalar", lambda sc, j=j, pb=pb, par=par: sc.activation(
                    out=sb["sq"][:, par:par + NT], in_=pb[:, 0:NT], func=AF.Copy,
                    bias=0.0, scale=sb["cst"][:, 1:2]), inc="c_s")
                self.mark(("f2e", c, j), "c_s")
                self.pmgate[pbi] = ("f2e", c, j)
                cs = self.cnt["c_s"]
                self.wait("vector", "c_s", cs)
                self.on("vector", lambda v, j=j, par=par: v.tensor_tensor(
                    out=sb["sq"][:, par:par + NT], in0=sb["sq"][:, par:par + NT],
                    in1=self._pf_lnf(j), op=AL.add))
                self.on("vector", lambda v, j=j, t0=t0, par=par: v.scalar_tensor_tensor(
                    out=sb["stream"][:, j * T + t0:j * T + t0 + NT],
                    in0=sb["stream"][:, j * T + t0:j * T + t0 + NT],
                    scalar=sb["smallB"][:, 56 + j:57 + j], in1=sb["sq"][:, par:par + NT],
                    op0=AL.add, op1=AL.add), inc="c_v")
                self.mark(("f2v", c, j), "c_v")
                if j == G - 1:
                    self.mark(("f2rd", c, G - 1), "c_v")
            self.sqgate = ("f2v", c, G - 1)
            self.mark(("f0done", c), "c_v")
        self.mark(("ffn0_done",), "c_v")


    def _pool_lnx(self, g):
        sb = self.sb
        if g < 2:
            return sb["bxb"][:, g * T:(g + 1) * T]
        if g < 5:
            return sb["xab"][:].bitcast(BF16)[:, (g - 2) * T:(g - 1) * T]
        return sb["ln0f"][:].bitcast(BF16)[:, (g - 5) * T:(g - 4) * T]

    def _pool_sg(self, g):
        sb = self.sb
        half = 1024
        if g < 2:
            return sb["hb"][:, g * half:(g + 1) * half]
        if g == 2:
            return sb["sq"][:, 0:half]
        if g < 5:
            return sb["szb"][:, (g - 3) * half:(g - 2) * half]
        if g < 7:
            return sb["xcb"][:, (g - 5) * half:(g - 4) * half]
        return sb["sgx"][:, 0:half]

    def _pool_o(self, h):
        sb = self.sb
        if h < 2:
            return sb["xcbb"][:, h * 1024:(h + 1) * 1024]
        if h < 4:
            return sb["yb"][:].bitcast(BF16)[:, (h - 2) * 1024:(h - 1) * 1024]
        if h < 6:
            return sb["ln0b"][:, (h - 4) * 1024:(h - 3) * 1024]
        if h == 6:
            return sb["psc"][:].bitcast(BF16)[:, 0:1024]
        return sb["ptmp"][:].bitcast(BF16)[:, 0:1024]

    def _pool_S(self, h):
        sb = self.sb
        if h < 4:
            return sb["bbb"][:, h * P:(h + 1) * P]
        if h < 6:
            return sb["sprime"][:, (h - 4) * P:(h - 3) * P]
        return sb["amisc"][:, (h - 6) * P:(h - 5) * P]

    def l1_attn(self):
        sb, s, i = self.sb, self.scr, self.i
        arena = sb["pin"][:].bitcast(BF16)
        kbuf = sb["convm"]
        csb = sb["rbuf"]
        rcs = sb["ymb"][:].bitcast(F32)
        qb = sb["stage"]
        vb = sb["stageb"]
        SC = 1024
        for e in ["sync", "scalar", "vector", "tensor", "gpsimd"]:
            self.wmark(e, ("ffn0_done",))
        self.wmark("sync", ("pd", "wg2"))
        if self.stage >= 4:
            for e in ["sync", "scalar", "vector", "tensor"]:
                self.wmark(e, ("pd", "w2"))
        self.dma("sync", sb["identf"][:], i["trimask"], "d_misc")
        nm = self.cnt["d_misc"]
        self.wait("vector", "d_misc", nm)
        for c in range(NC1):
            t0 = c * T1
            src = lambda g: sb["stream"][:, g * T + t0:g * T + t0 + T1]
            bank = "ps_w0" if c % 2 == 0 else "ps_w1"
            bk = sb[bank]
            ctb = self._lnstats(src, bank, ("l1ln", c))
            self.wait("vector", "c_t", ctb)
            for g in range(G):
                self.on("vector", lambda v, g=g: v.tensor_tensor(
                    out=sb["cstrip"][0:1, 0:1] if False else kbuf[:, 0:T1], in0=src(g),
                    in1=bk[:, 256:256 + T1], op=AL.mult))
                self.on("vector", lambda v, g=g, t0=t0: v.tensor_tensor(
                    out=self._pool_lnx(g)[:, t0:t0 + T1], in0=kbuf[:, 0:T1],
                    in1=bk[:, 384:384 + T1], op=AL.subtract),
                        inc=("c_v" if g == G - 1 else None))
            self.mark(("lnx", c), "c_v")
            self.mark(("bankfree", bank), "c_v")
        self.mark(("lnx_all",), "c_v")

        HT = T // 2
        NCH = HT // TC
        for half in range(2):
            th0 = half * HT
            for e in ["sync", "scalar", "vector", "tensor"]:
                self.wmark(e, ("l1half", half - 1))
            self.wmark("scalar", ("kvmm", half - 1, H - 1, NCH - 1))
            dsem = "d_w2"
            if 2 in self.slot_gate:
                self.wmark("sync", self.slot_gate[2])
            self.dma("sync", sb["wst"][:, 2 * G * P:3 * G * P], s["wg1"][:, 0:G * P], dsem)
            self.wait("tensor", dsem, self.cnt[dsem])
            self.wmark("tensor", ("lnx_all",))
            for tt in range(2):
                pb = sb["ps_m0"] if tt == 0 else sb["ps_m1"]
                if self.pmgate[tt]:
                    self.wmark("tensor", self.pmgate[tt])
                for k in range(G):
                    self.on("tensor", lambda t, k=k, tt=tt, pb=pb: t.matmul(
                        pb[:, 0:512], sb["wst"][:, 2 * G * P + k * P:2 * G * P + (k + 1) * P],
                        self._pool_lnx(k)[:, th0 + tt * 512:th0 + (tt + 1) * 512],
                        start=(k == 0), stop=(k == G - 1)), inc=("c_t" if k == G - 1 else None))
                ct = self.cnt["c_t"]
                self.mark(("g1w", half, tt), "c_t")
                self.slot_gate[2] = ("g1w", half, tt)
                self.wait("scalar", "c_t", ct)
                self.on("scalar", lambda sc, tt=tt, pb=pb: sc.copy(
                    out=arena[:, tt * 512:(tt + 1) * 512], in_=pb[:, 0:512]), inc="c_s")
                self.mark(("g1e", half, tt), "c_s")
                self.pmgate[tt] = ("g1e", half, tt)
            if 3 in self.slot_gate:
                self.wmark("sync", self.slot_gate[3])
            self.dma("sync", sb["wst"][:, 3 * G * P:4 * G * P], s["wg2"][:, 0:G * P], "d_w3")
            self.wait("tensor", "d_w3", self.cnt["d_w3"])
            self.wmark("tensor", ("g1e", half, 1))
            for j in range(G):
                for tt in range(2):
                    pbi = (j * 2 + tt) % 2
                    pb = sb["ps_m0"] if pbi == 0 else sb["ps_m1"]
                    if self.pmgate[pbi]:
                        self.wmark("tensor", self.pmgate[pbi])
                    self.on("tensor", lambda t, j=j, tt=tt, pb=pb: t.matmul(
                        pb[:, 0:512], sb["wst"][:, 3 * G * P + j * P:3 * G * P + (j + 1) * P],
                        arena[:, tt * 512:(tt + 1) * 512], start=True, stop=True), inc="c_t")
                    ct = self.cnt["c_t"]
                    self.mark(("g2w", half, j, tt), "c_t")
                    self.slot_gate[3] = ("g2w", half, j, tt)
                    self.wait("scalar", "c_t", ct)
                    self.on("scalar", lambda sc, j=j, tt=tt, pb=pb: sc.activation(
                        out=self._pool_sg(j)[:, tt * 512:(tt + 1) * 512], in_=pb[:, 0:512],
                        func=AF.Silu, bias=0.0, scale=1.0), inc="c_s")
                    self.mark(("sg", half, j, tt), "c_s")
                    self.pmgate[pbi] = ("sg", half, j, tt)
            self.mark(("sg_done", half), "c_s")
            self.mark(("g2w_all", half), "c_t")

            for h in range(H):
                self._l1_proj(s["wk"], h, kbuf, half, tag=("kp", half, h), silu=False,
                              bf16out=False, gate=("chunkdone", half, h - 1, NCH - 1),
                              gate_eng="c_v")
                cvk = self.marks[("kp", half, h)][1]
                self.wait("vector", "c_s", cvk)
                self.on("vector", lambda v: v.tensor_reduce(
                    out=csb[:, SC:SC + 1], in_=kbuf[:, 0:HT], axis=AX.X, op=AL.max), inc="c_v")
                self.on("vector", lambda v: v.tensor_scalar(
                    out=csb[:, SC + 1:SC + 2], in0=csb[:, SC:SC + 1], scalar1=-1.0, scalar2=None,
                    op0=AL.mult), inc="c_v")
                cv = self.cnt["c_v"]
                self.wait("scalar", "c_v", cv)
                if half == 0:
                    self.on("scalar", lambda sc, h=h: sc.copy(
                        out=csb[:, SC + 8 + h:SC + 9 + h], in_=csb[:, SC:SC + 1]), inc="c_s")
                self.on("scalar", lambda sc: sc.activation(
                    out=kbuf[:, 0:HT], in_=kbuf[:, 0:HT], func=AF.Exp,
                    bias=csb[:, SC + 1:SC + 2], scale=1.0), inc="c_s")
                cs0 = self.cnt["c_s"]
                self.wait("vector", "c_s", cs0)
                self.on("vector", lambda v: v.tensor_tensor_scan(
                    out=csb[:, 0:HT], data0=kbuf[:, 0:HT], data1=kbuf[:, 0:HT],
                    initial=0.0, op0=AL.add, op1=AL.bypass), inc="c_v")
                if half == 1:
                    self.on("vector", lambda v, h=h: v.tensor_tensor(
                        out=csb[:, SC + 2:SC + 3], in0=csb[:, SC + 8 + h:SC + 9 + h],
                        in1=csb[:, SC:SC + 1], op=AL.subtract), inc="c_v")
                    cv1 = self.cnt["c_v"]
                    self.wait("scalar", "c_v", cv1)
                    self.on("scalar", lambda sc: sc.activation(
                        out=csb[:, SC + 2:SC + 3], in_=csb[:, SC + 2:SC + 3], func=AF.Exp,
                        bias=0.0, scale=1.0), inc="c_s")
                    cs1 = self.cnt["c_s"]
                    self.wait("vector", "c_s", cs1)
                    self.on("vector", lambda v, h=h: v.tensor_tensor(
                        out=csb[:, SC + 3:SC + 4], in0=csb[:, SC + 16 + h:SC + 17 + h],
                        in1=csb[:, SC + 2:SC + 3], op=AL.mult), inc="c_v")
                    self.on("vector", lambda v: v.tensor_scalar(
                        out=csb[:, 0:HT], in0=csb[:, 0:HT],
                        scalar1=csb[:, SC + 3:SC + 4], scalar2=None, op0=AL.add), inc="c_v")
                else:
                    self.on("vector", lambda v, h=h: v.tensor_copy(
                        out=csb[:, SC + 16 + h:SC + 17 + h], in_=csb[:, HT - 1:HT]), inc="c_v")
                self.on("vector", lambda v: v.reciprocal(out=rcs[:, 0:HT], in_=csb[:, 0:HT]),
                        inc="c_v")
                self.mark(("csr", half, h), "c_v")
                self._l1_proj(s["wq"], h, qb, half, tag=("qp", half, h), silu=True,
                              bf16out=False, gate=("chunkdone", half, h - 1, NCH - 1),
                              gate_eng="c_v")
                self._l1_proj(s["wv"], h, vb, half, tag=("vp", half, h), silu=False,
                              bf16out=True, gate=("tmm", half, h - 1, NCH - 1),
                              gate_eng="c_t")
                for ci in range(NCH):
                    self._l1_chunk(half, h, ci)
                self.mark(("head", half, h), "c_v")
                o2scr = sb["rbuf"][:, 1184:1696]
                for tt in range(2):
                    if tt == 0:
                        self.wmark("scalar", ("osq", half, h - 1, 1))
                    else:
                        self.wmark("scalar", ("osq", half, h, 0))
                    self.wmark("scalar", ("oev", half, h, NCH - 1))
                    self.on("scalar", lambda sc, h=h, tt=tt, o2scr=o2scr: sc.activation(
                        out=o2scr, in_=self._pool_o(h)[:, tt * 512:(tt + 1) * 512],
                        func=AF.Square, bias=0.0, scale=1.0), inc="c_s")
                    cso = self.cnt["c_s"]
                    self.wait("tensor", "c_s", cso)
                    cbb = "ps_cb0" if tt == 0 else "ps_cb1"
                    if h == 0:
                        self.wmark("tensor", ("cbfree", half, tt))
                    self.on("tensor", lambda t, h=h, cbb=cbb, o2scr=o2scr: t.matmul(
                        sb[cbb][0:1, 0:512], sb["onesT"][:, 0:1], o2scr,
                        start=(h == 0), stop=(h == H - 1), skip_group_check=True), inc="c_t")
                    self.mark(("osq", half, h, tt), "c_t")
            self.wmark("vector", ("osq", half, H - 1, 0))
            self.wmark("vector", ("osq", half, H - 1, 1))
            st = sb["cstrip"]
            self.on("vector", lambda v: v.tensor_scalar(
                out=st[0:1, 0:HT] if False else sb["strip"][0:1, 0:512],
                in0=sb["ps_cb0"][0:1, 0:512], scalar1=1.0 / D, scalar2=None, op0=AL.mult))
            self.on("vector", lambda v: v.tensor_scalar(
                out=sb["rbuf"][0:1, 0:512], in0=sb["ps_cb1"][0:1, 0:512],
                scalar1=1.0 / D, scalar2=None, op0=AL.mult), inc="c_v")
            cv = self.cnt["c_v"]
            self.wait("scalar", "c_v", cv)
            self.on("scalar", lambda sc: sc.activation(
                out=sb["strip"][0:1, 0:512], in_=sb["strip"][0:1, 0:512], func=AF.Sqrt,
                bias=sb["cst"][0:1, 0:1], scale=1.0), inc="c_s")
            self.on("scalar", lambda sc: sc.activation(
                out=sb["rbuf"][0:1, 0:512], in_=sb["rbuf"][0:1, 0:512], func=AF.Sqrt,
                bias=sb["cst"][0:1, 0:1], scale=1.0), inc="c_s")
            cs2 = self.cnt["c_s"]
            self.wait("vector", "c_s", cs2)
            self.on("vector", lambda v: v.reciprocal(out=sb["strip"][0:1, 0:512],
                                                     in_=sb["strip"][0:1, 0:512]))
            self.on("vector", lambda v: v.reciprocal(out=sb["rbuf"][0:1, 0:512],
                                                     in_=sb["rbuf"][0:1, 0:512]), inc="c_v")
            cv2 = self.cnt["c_v"]
            self.wait("tensor", "c_v", cv2)
            self.on("tensor", lambda t: t.matmul(sb["ps_cb0"][:, 0:512], sb["onesT"][0:1, 0:P],
                                                 sb["strip"][0:1, 0:512], start=True, stop=True))
            self.on("tensor", lambda t: t.matmul(sb["ps_cb1"][:, 0:512], sb["onesT"][0:1, 0:P],
                                                 sb["rbuf"][0:1, 0:512], start=True, stop=True),
                    inc="c_t")
            ctb = self.cnt["c_t"]
            self.mark(("rbb", half), "c_t")
            for j in range(G):
                dsemo = "d_w2" if j % 2 == 0 else "d_w3"
                if (2 + j % 2) in self.slot_gate:
                    self.wmark("sync", self.slot_gate[2 + j % 2])
                self.dma("sync", sb["wst"][:, (2 + j % 2) * G * P:(3 + j % 2) * G * P],
                         s["wo"][:, j * G * P:(j + 1) * G * P], dsemo)
                self.wait("tensor", dsemo, self.cnt[dsemo])
                for tt in range(2):
                    pbi = tt
                    pb = sb["ps_m0"] if pbi == 0 else sb["ps_m1"]
                    if self.pmgate[pbi]:
                        self.wmark("tensor", self.pmgate[pbi])
                    for h in range(H):
                        if j == 0:
                            self.wmark("vector", ("rbb", half))
                            self.wmark("vector", ("head", half, h))
                            self.wmark("vector", ("sg_done", half))
                            cbb = sb["ps_cb0"] if tt == 0 else sb["ps_cb1"]
                            self.on("vector", lambda v, h=h, tt=tt: v.tensor_tensor(
                                out=kbuf[:, 0:512], in0=self._pool_o(h)[:, tt * 512:(tt + 1) * 512],
                                in1=self._pool_sg(h)[:, tt * 512:(tt + 1) * 512], op=AL.mult))
                            self.on("vector", lambda v, h=h, tt=tt, cbb=cbb: v.tensor_tensor(
                                out=kbuf[:, 0:512], in0=kbuf[:, 0:512], in1=cbb[:, 0:512],
                                op=AL.mult))
                            self.on("vector", lambda v, h=h, tt=tt: v.tensor_scalar(
                                out=self._pool_o(h)[:, tt * 512:(tt + 1) * 512],
                                in0=kbuf[:, 0:512],
                                scalar1=sb["smallB"][:, 40 + h:41 + h], scalar2=None,
                                op0=AL.mult), inc="c_v")
                            self.mark(("om", half, h, tt), "c_v")
                        self.wmark("tensor", ("om", half, h, tt))
                        self.on("tensor", lambda t, j=j, h=h, tt=tt, pb=pb: t.matmul(
                            pb[:, 0:512],
                            sb["wst"][:, (2 + j % 2) * G * P + h * P:(2 + j % 2) * G * P + (h + 1) * P],
                            self._pool_o(h)[:, tt * 512:(tt + 1) * 512],
                            start=(h == 0), stop=(h == H - 1)),
                            inc=("c_t" if h == H - 1 else None))
                    self.mark(("wow", half, j, tt), "c_t")
                    self.slot_gate[2 + j % 2] = ("wow", half, j, tt)
                    ct = self.cnt["c_t"]
                    self.wait("vector", "c_t", ct)
                    self.on("vector", lambda v, j=j, tt=tt, pb=pb: v.scalar_tensor_tensor(
                        out=sb["stream"][:, j * T + th0 + tt * 512:j * T + th0 + (tt + 1) * 512],
                        in0=sb["stream"][:, j * T + th0 + tt * 512:j * T + th0 + (tt + 1) * 512],
                        scalar=0.0, in1=pb[:, 0:512], op0=AL.add, op1=AL.add), inc="c_v")
                    self.mark(("wost", half, j, tt), "c_v")
                    self.pmgate[pbi] = ("wost", half, j, tt)
            self.mark(("cbfree", half + 1, 0), "c_v")
            self.mark(("cbfree", half + 1, 1), "c_v")
            self.mark(("l1half", half), "c_v")
        self.mark(("l1_done",), "c_v")

    def _l1_proj(self, wsrc, h, dstbuf, half, tag, silu, bf16out, gate=None, gate_eng=None):
        sb = self.sb
        th0 = half * (T // 2)
        dsem = "d_w0" if h % 2 == 0 else "d_w1"
        wslot = (h % 2) * G * P
        if (h % 2) in self.slot_gate:
            self.wmark("sync", self.slot_gate[h % 2])
        self.dma("sync", sb["wst"][:, wslot:wslot + G * P],
                 wsrc[:, h * G * P:(h + 1) * G * P], dsem)
        self.wait("tensor", dsem, self.cnt[dsem])
        self.wmark("tensor", ("lnx_all",))
        for tt in range(2):
            pbi = tt
            pb = sb["ps_m0"] if pbi == 0 else sb["ps_m1"]
            if self.pmgate[pbi]:
                self.wmark("tensor", self.pmgate[pbi])
            for k in range(G):
                self.on("tensor", lambda t, k=k, tt=tt, pb=pb, wslot=wslot: t.matmul(
                    pb[:, 0:512], sb["wst"][:, wslot + k * P:wslot + (k + 1) * P],
                    self._pool_lnx(k)[:, th0 + tt * 512:th0 + (tt + 1) * 512],
                    start=(k == 0), stop=(k == G - 1)), inc=("c_t" if k == G - 1 else None))
            self.mark(tag + (tt, "w"), "c_t")
            self.slot_gate[h % 2] = tag + (tt, "w")
            ct = self.cnt["c_t"]
            self.wait("scalar", "c_t", ct)
            if gate is not None and tt == 0:
                self.wmark("scalar", gate)
            fn = AF.Silu if silu else AF.Copy
            self.on("scalar", lambda sc, tt=tt, pb=pb, fn=fn: sc.activation(
                out=dstbuf[:, tt * 512:(tt + 1) * 512], in_=pb[:, 0:512], func=fn,
                bias=0.0, scale=1.0), inc="c_s")
            self.mark(tag + (tt, "e"), "c_s")
            self.pmgate[pbi] = tag + (tt, "e")
        self.mark(tag, "c_s")
        self.pump_step(2)

    def _l1_chunk(self, half, h, ci):
        sb = self.sb
        arena = sb["pin"][:].bitcast(BF16)
        kbuf = sb["convm"]
        csb = sb["rbuf"]
        rcs = sb["ymb"][:].bitcast(F32)
        qb = sb["stage"]
        vb = sb["stageb"]
        SC = 1024
        Sh = self._pool_S(h)
        t0 = ci * TC
        NCH = (T // 2) // TC
        first = (half == 0 and ci == 0)
        prevk = ((half, h, ci - 1) if ci > 0 else
                 ((half, h - 1, NCH - 1) if h > 0 else
                  ((half - 1, H - 1, NCH - 1) if half > 0 else None)))
        A_Sb = arena[:, 0:P]
        A_qn = arena[:, P:2 * P]
        A_qi = arena[:, 2 * P:3 * P]
        A_eb = arena[:, 3 * P:4 * P]
        A_Pb = arena[:, 4 * P:5 * P]
        A_KeT = arena[:, 5 * P:6 * P]
        A_VT = arena[:, 6 * P:7 * P]
        t1f = csb[:, SC + 32:SC + 32 + P]

        def pk(name):
            return None if prevk is None else (name,) + prevk

        self.wmark("vector", ("csr", half, h))
        self.wmark("vector", ("qp", half, h))
        if prevk is not None:
            self.wmark("vector", pk("pmm"))
            self.wmark("vector", pk("tmm"))
            self.wmark("vector", pk("omm"))
        if h == 0 and ci == 0:
            self.wmark("vector", ("g2w_all", half))
        self.on("vector", lambda v: v.tensor_tensor(
            out=t1f, in0=qb[:, t0:t0 + TC], in1=rcs[:, t0:t0 + TC], op=AL.mult), inc="c_v")
        self.on("vector", lambda v: v.tensor_copy(out=A_qn, in_=t1f), inc="c_v")
        if not first:
            if ci == 0:
                scal = csb[:, SC + 3:SC + 4] if half == 1 else csb[:, SC + 16 + h:SC + 17 + h]
            else:
                scal = csb[:, t0 - 1:t0]
            self.on("vector", lambda v, scal=scal: v.tensor_scalar(
                out=A_qi, in0=t1f, scalar1=scal, scalar2=None, op0=AL.mult), inc="c_v")
            self.on("vector", lambda v: v.tensor_copy(out=A_Sb, in_=Sh), inc="c_v")
        self.on("vector", lambda v: v.tensor_copy(out=A_eb, in_=kbuf[:, t0:t0 + TC]), inc="c_v")
        cv_prep = self.cnt["c_v"]
        self.mark(("prep", half, h, ci), "c_v")

        self.wait("tensor", "c_v", cv_prep)
        if prevk is not None:
            self.wmark("tensor", pk("pevac"))
            self.wmark("tensor", pk("supd"))
        self.on("tensor", lambda t: t.matmul(sb["ps_x"][:, 0:P], A_eb, A_qn,
                                             start=True, stop=True), inc="c_t")
        ct_p = self.cnt["c_t"]
        self.mark(("pmm", half, h, ci), "c_t")
        psT = sb["ps_a"][:].bitcast(BF16)
        if prevk is not None:
            self.wmark("tensor", pk("tevac"))
        self.wmark("tensor", ("vp", half, h))
        self.on("tensor", lambda t: t.transpose(psT[:, 0:P], A_eb, sb["identb"][:]))
        self.on("tensor", lambda t, t0=t0: t.transpose(psT[:, P:2 * P], vb[:, t0:t0 + TC],
                                                       sb["identb"][:]), inc="c_t")
        ct_t = self.cnt["c_t"]
        self.mark(("tmm", half, h, ci), "c_t")
        self.wait("scalar", "c_t", ct_t)
        if prevk is not None:
            self.wmark("scalar", pk("kvmm"))
            self.wmark("scalar", pk("omm"))
        self.on("scalar", lambda sc: sc.copy(out=A_KeT, in_=psT[:, 0:P]))
        self.on("scalar", lambda sc: sc.copy(out=A_VT, in_=psT[:, P:2 * P]), inc="c_s")
        cs_t = self.cnt["c_s"]
        self.mark(("tevac", half, h, ci), "c_s")
        self.wait("vector", "c_t", ct_p)
        if prevk is not None:
            self.wmark("vector", pk("omm"))
        self.on("vector", lambda v: v.tensor_tensor(out=A_Pb, in0=sb["ps_x"][:, 0:P],
                                                    in1=sb["identf"][:], op=AL.mult), inc="c_v")
        cv_pm = self.cnt["c_v"]
        self.mark(("pevac", half, h, ci), "c_v")

        ob = "ps_w0" if ci % 2 == 0 else "ps_w1"
        self.wmark("tensor", ("bankfree", ob))
        self.wait("tensor", "c_v", cv_pm)
        self.wait("tensor", "c_s", cs_t)
        if not first:
            self.on("tensor", lambda t, ob=ob: t.matmul(sb[ob][:, 0:P], A_Sb, A_qi,
                                                        start=True, stop=False))
            self.on("tensor", lambda t, ob=ob: t.matmul(sb[ob][:, 0:P], A_VT, A_Pb,
                                                        start=False, stop=True), inc="c_t")
        else:
            self.on("tensor", lambda t, ob=ob: t.matmul(sb[ob][:, 0:P], A_VT, A_Pb,
                                                        start=True, stop=True), inc="c_t")
        ct_o = self.cnt["c_t"]
        self.mark(("omm", half, h, ci), "c_t")
        self.on("tensor", lambda t: t.matmul(sb["ps_x"][:, P:2 * P], A_KeT, A_VT,
                                             start=True, stop=True), inc="c_t")
        ct_kv = self.cnt["c_t"]
        self.mark(("kvmm", half, h, ci), "c_t")

        self.wait("scalar", "c_t", ct_o)
        self.on("scalar", lambda sc, ob=ob, t0=t0, h=h: sc.activation(
            out=self._pool_o(h)[:, t0:t0 + TC], in_=sb[ob][:, 0:P], func=AF.Copy,
            bias=0.0, scale=sb["cst"][:, 2:3]), inc="c_s")
        self.mark(("oev", half, h, ci), "c_s")
        self.mark(("bankfree", ob), "c_s")

        self.wait("vector", "c_t", ct_kv)
        if first:
            scal2 = 0.0
        elif ci == 0:
            scal2 = csb[:, SC + 3:SC + 4] if half == 1 else csb[:, SC + 16 + h:SC + 17 + h]
        else:
            scal2 = csb[:, t0 - 1:t0]
        self.on("vector", lambda v, scal2=scal2, Sh=Sh: v.scalar_tensor_tensor(
            out=Sh, in0=Sh, scalar=scal2, in1=sb["ps_x"][:, P:2 * P],
            op0=AL.mult, op1=AL.add), inc="c_v")
        self.on("vector", lambda v, t0=t0: v.reciprocal(
            out=csb[:, SC + 4:SC + 5], in_=csb[:, t0 + TC - 1:t0 + TC]), inc="c_v")
        self.on("vector", lambda v, Sh=Sh: v.tensor_scalar(
            out=Sh, in0=Sh, scalar1=csb[:, SC + 4:SC + 5], scalar2=None, op0=AL.mult),
                inc="c_v")
        self.mark(("supd", half, h, ci), "c_v")
        self.mark(("chunkdone", half, h, ci), "c_v")
        self.pump_step(2)

    def _ternarize(self, wsrc, dst, kchunks, mchunks):
        sb = self.sb
        ncols = mchunks * P
        npieces_per_k = (ncols + 1023) // 1024
        acc = sb["rbuf"]
        nacc = 0
        for k in range(kchunks):
            for c0 in range(0, ncols, 1024):
                w = min(1024, ncols - c0)
                self.wmark("sync", ("tz_used",))
                self.dma("sync", sb["stage"][:, 0:w], wsrc[k * P:(k + 1) * P, c0:c0 + w], "d_c")
                dc = self.cnt["d_c"]
                self.wait("scalar", "d_c", dc)
                self.wmark("scalar", ("tz2_used",))
                self.on("scalar", lambda sc, w=w, nacc=nacc: sc.activation(
                    out=sb["convm"][:, 0:w], in_=sb["stage"][:, 0:w], func=AF.Abs,
                    bias=0.0, scale=1.0, accum_out=acc[:, 1024 + nacc:1025 + nacc]), inc="c_s")
                self.mark(("tz_used",), "c_s")
                nacc += 1
        css = self.cnt["c_s"]
        self.wait("vector", "c_s", css)
        self.on("vector", lambda v, nacc=nacc: v.tensor_reduce(
            out=acc[:, 1024 + nacc:1025 + nacc], in_=acc[:, 1024:1024 + nacc],
            axis=AX.X, op=AL.add), inc="c_v")
        cv = self.cnt["c_v"]
        self.wait("tensor", "c_v", cv)
        self.wmark("tensor", ("tzps",))
        self.on("tensor", lambda t, nacc=nacc: t.matmul(
            sb["ps_x"][0:1, 0:1], acc[:, 1024 + nacc:1025 + nacc], sb["onesT"][:, 0:1],
            start=True, stop=True), inc="c_t")
        ct = self.cnt["c_t"]
        self.wait("vector", "c_t", ct)
        self.on("vector", lambda v: v.tensor_scalar(
            out=acc[0:1, 1023:1024], in0=sb["ps_x"][0:1, 0:1],
            scalar1=1.0 / (kchunks * P * ncols), scalar2=1e-8, op0=AL.mult, op1=AL.max),
                inc="c_v")
        self.on("vector", lambda v: v.reciprocal(out=acc[0:1, 1023:1024],
                                                 in_=acc[0:1, 1023:1024]), inc="c_v")
        cv2 = self.cnt["c_v"]
        self.wait("tensor", "c_v", cv2)
        self.on("tensor", lambda t: t.matmul(sb["ps_x"][:, 4:5], sb["onesT"][0:1, 0:P],
                                             acc[0:1, 1023:1024], start=True, stop=True),
                inc="c_t")
        ct2 = self.cnt["c_t"]
        self.mark(("tzps",), "c_t")
        self.wait("vector", "c_t", ct2)
        self.on("vector", lambda v: v.tensor_copy(out=acc[:, 1022:1023], in_=sb["ps_x"][:, 4:5]),
                inc="c_v")
        self.mark(("tz_recip",), "c_v")
        recip = acc[:, 1022:1023]
        dst4 = dst.rearrange("p (m kk q) -> p m kk q", kk=kchunks, q=P)
        qflt = sb["szb"]
        self.wait("sync", "c_s", css)
        for k in range(kchunks):
            for c0 in range(0, ncols, 1024):
                w = min(1024, ncols - c0)
                self.wait("sync", "d_cs", self.cnt["d_cs"])
                self.wmark("sync", ("tz2_used",))
                self.dma("sync", sb["stage"][:, 0:w], wsrc[k * P:(k + 1) * P, c0:c0 + w], "d_c")
                dc = self.cnt["d_c"]
                self.wait("scalar", "d_c", dc)
                self.wmark("scalar", ("tz_recip",))
                self.wmark("scalar", ("tz2v",))
                self.on("scalar", lambda sc, w=w, recip=recip: sc.activation(
                    out=sb["convm"][:, 0:w], in_=sb["stage"][:, 0:w], func=AF.Abs,
                    bias=0.0, scale=recip))
                self.on("scalar", lambda sc, w=w: sc.activation(
                    out=qflt[:, 0:w], in_=sb["stage"][:, 0:w], func=AF.Sign,
                    bias=0.0, scale=1.0), inc="c_s")
                cs2v = self.cnt["c_s"]
                self.mark(("tz2_used",), "c_s")
                self.wait("vector", "c_s", cs2v)
                self.wait("vector", "d_cs", self.cnt["d_cs"])
                self.on("vector", lambda v, w=w: v.tensor_scalar(
                    out=sb["convm"][:, 0:w], in0=sb["convm"][:, 0:w],
                    scalar1=0.5, scalar2=None, op0=AL.is_gt))
                self.on("vector", lambda v, w=w: v.tensor_tensor(
                    out=sb["stageb"][:, 0:w], in0=sb["convm"][:, 0:w], in1=qflt[:, 0:w],
                    op=AL.mult), inc="c_v")
                self.mark(("tz2v",), "c_v")
                self.wait("sync", "c_v", self.cnt["c_v"])
                m0, nm = c0 // P, w // P
                self.dma("sync", dst4[:, m0:m0 + nm, k, :],
                         sb["stageb"][:, 0:w].rearrange("p (m q) -> p m q", q=P), "d_cs")

    def ffn1(self):
        sb, s = self.sb, self.scr
        self.pump_drain()
        for e in ["sync", "scalar", "vector", "tensor", "gpsimd"]:
            self.wmark(e, ("l1_done",))
        self.wmark("sync", ("pd", "w1"))
        self.wmark("sync", ("pd", "w2"))
        NT = 512
        NCF = T // NT
        for c in range(NCF):
            t0 = c * NT
            src = lambda g: sb["stream"][:, g * T + t0:g * T + t0 + NT]
            ctb = self._lnstats512(src, ("t3", c))
            self.wait("vector", "c_t", ctb)
            self.wmark("vector", ("g1rd", c - 1, GF - 1))
            for g in range(G):
                self.on("vector", lambda v, g=g: v.tensor_tensor(
                    out=sb["convm"][:, 0:NT], in0=src(g), in1=sb["ps_w0"][:, 0:NT], op=AL.mult))
                self.on("vector", lambda v, g=g: v.tensor_tensor(
                    out=self._pf_h2(g), in0=sb["convm"][:, 0:NT], in1=sb["ps_w1"][:, 0:NT],
                    op=AL.subtract), inc=("c_v" if g == G - 1 else None))
            self.mark(("xln", c), "c_v")
            self.mark(("bankfree", "ps_w0"), "c_v")
            self.mark(("bankfree", "ps_w1"), "c_v")
            def t1dma(ff):
                wslot = (ff % 2) * G * P
                dsem = "d_w0" if ff % 2 == 0 else "d_w1"
                if (ff % 2) in self.slot_gate:
                    self.wmark("sync", self.slot_gate[ff % 2])
                self.dma("sync", sb["wst"][:, wslot:wslot + G * P],
                         s["w1"][:, ff * G * P:(ff + 1) * G * P], dsem)
                self.wstfeed[("t1", c, ff)] = self.cnt[dsem]
            t1dma(0)
            for ff in range(GF):
                wslot = (ff % 2) * G * P
                dsem = "d_w0" if ff % 2 == 0 else "d_w1"
                if ff + 1 < GF:
                    t1dma(ff + 1)
                self.wait("tensor", dsem, self.wstfeed[("t1", c, ff)])
                self.wmark("tensor", ("xln", c))
                pbi = ff % 2
                pb = sb["ps_m0"] if pbi == 0 else sb["ps_m1"]
                if self.pmgate[pbi]:
                    self.wmark("tensor", self.pmgate[pbi])
                self.wmark("tensor", ("g1rd", c - 1, ff))
                for k in range(G):
                    self.on("tensor", lambda t, k=k, pb=pb, wslot=wslot: t.matmul(
                        pb[:, 0:NT], sb["wst"][:, wslot + k * P:wslot + (k + 1) * P],
                        self._pf_h2(k), start=(k == 0), stop=(k == G - 1)),
                            inc=("c_t" if k == G - 1 else None))
                self.mark(("t1w", c, ff), "c_t")
                self.slot_gate[ff % 2] = ("t1w", c, ff)
                ct = self.cnt["c_t"]
                self.wait("scalar", "c_t", ct)
                self.on("scalar", lambda sc, ff=ff, pb=pb: sc.activation(
                    out=self._pf_y1(ff), in_=pb[:, 0:NT], func=AF.Gelu,
                    bias=sb["smallA"][:, 96 + ff:97 + ff], scale=1.0), inc="c_s")
                self.mark(("t1e", c, ff), "c_s")
                self.pmgate[pbi] = ("t1e", c, ff)
            for j in range(G):
                pbi = j % 2
                pb = sb["ps_m0"] if pbi == 0 else sb["ps_m1"]
                if self.pmgate[pbi]:
                    self.wmark("tensor", self.pmgate[pbi])
                self.wmark("tensor", ("t1e", c, GF - 1))
                def w2dma(j2, piece):
                    wslot = (2 + (piece % 2)) * G * P
                    dsem = "d_w2" if piece % 2 == 0 else "d_w3"
                    if (2 + piece % 2) in self.slot_gate:
                        self.wmark("sync", self.slot_gate[2 + piece % 2])
                    self.dma("sync", sb["wst"][:, wslot:wslot + G * P],
                             s["w2"][:, (j2 * GF + piece * 8) * P:(j2 * GF + piece * 8 + 8) * P],
                             dsem)
                    self.wstfeed[("w2", c, j2, piece)] = self.cnt[dsem]
                if j == 0:
                    w2dma(0, 0)
                for piece in range(4):
                    wslot = (2 + (piece % 2)) * G * P
                    dsem = "d_w2" if piece % 2 == 0 else "d_w3"
                    if piece + 1 < 4:
                        w2dma(j, piece + 1)
                    elif j + 1 < G:
                        w2dma(j + 1, 0)
                    self.wait("tensor", dsem, self.wstfeed[("w2", c, j, piece)])
                    for k8 in range(G):
                        kk = piece * 8 + k8
                        self.on("tensor", lambda t, k8=k8, kk=kk, pb=pb, wslot=wslot: t.matmul(
                            pb[:, 0:NT], sb["wst"][:, wslot + k8 * P:wslot + (k8 + 1) * P],
                            self._pf_y1(kk), start=(kk == 0), stop=(kk == GF - 1)),
                            inc=("c_t" if k8 == G - 1 else None))
                    self.mark(("t2w", c, j, piece), "c_t")
                    self.slot_gate[2 + piece % 2] = ("t2w", c, j, piece)
                self.on("tensor", lambda t: t.nop(), inc="c_t")
                ct = self.cnt["c_t"]
                if j == G - 1:
                    for ff in range(GF):
                        self.mark(("g1rd", c, ff), "c_t")
                self.wait("vector", "c_t", ct)
                self.on("vector", lambda v, j=j, t0=t0, pb=pb: v.scalar_tensor_tensor(
                    out=sb["stream"][:, j * T + t0:j * T + t0 + NT],
                    in0=sb["stream"][:, j * T + t0:j * T + t0 + NT],
                    scalar=sb["smallB"][:, 48 + j:49 + j],
                    in1=pb[:, 0:NT], op0=AL.add, op1=AL.add), inc="c_v")
                self.mark(("t2e", c, j), "c_v")
                self.pmgate[pbi] = ("t2e", c, j)
            self.mark(("f1done", c), "c_v")
        self.mark(("ffn1_done",), "c_v")


    def epilogue(self):
        sb = self.sb
        key = {1: ("blk1", NC1 - 1), 2: ("ffn0_done",), 3: ("l1_done",), 4: ("ffn1_done",)}[self.stage]
        sem, val = self.marks[key]
        self.wait("sync", sem, val)
        for g in range(G):
            self.dma("sync", self.y_out[g * P:(g + 1) * P, :], sb["stream"][:, g * T:(g + 1) * T],
                     "d_out")
        self.wait("sync", "d_out", self.cnt["d_out"])

    def build(self):
        blk, me = self.block, self

        @blk.sync
        def _(sync):
            me._reset("sync", sync)
            me.walk()

        @blk.vector
        def _(v):
            me._reset("vector", v)
            me.walk()

        @blk.scalar
        def _(s):
            me._reset("scalar", s)
            me.walk()

        @blk.tensor
        def _(t):
            me._reset("tensor", t)
            me.walk()

        @blk.gpsimd
        def _(g):
            me._reset("gpsimd", g)
            me.walk()

        for cm in reversed(self._ctx):
            cm.__exit__(None, None, None)
        self.nc.finalize()
        return self.nc


def _prep_weights(inp, stage):
    f = np.ascontiguousarray
    a = lambda k: np.asarray(inp[k], np.float32)
    d = {}
    d["alog_r"] = f(a("L0_attn_A_log").reshape(G, P, NST).transpose(1, 0, 2).reshape(P, GN))
    d["inwT"] = f(a("L0_attn_in_w").T)
    d["brow_l0"] = f(a("L0_attn_out_b").reshape(1, 1024))
    d["inb_xa_r"] = f(a("L0_attn_in_b")[:D].reshape(G, P).T)
    d["inb_z_r"] = f(a("L0_attn_in_b")[D:].reshape(G, P).T)
    d["convw_r"] = f(a("L0_attn_conv_w").reshape(D, 4).reshape(G, P, 4)
                     .transpose(1, 0, 2).reshape(P, 4 * G))
    d["convb_r"] = f(a("L0_attn_conv_b").reshape(G, P).T)
    d["bcwT"] = f(np.concatenate([a("L0_attn_B_w").T, a("L0_attn_C_w").T], axis=1))
    d["bcb"] = f(np.concatenate([a("L0_attn_B_b"), a("L0_attn_C_b")]).reshape(1, 2 * NST))
    d["outwT"] = f(a("L0_attn_out_w").T)
    d["lnw_r"] = f(a("L0_attn_ln_w").reshape(G, P).T)
    d["lnb_r"] = f(a("L0_attn_ln_b").reshape(G, P).T)
    d["ones"] = np.ones((P, 512), np.float32)
    d["identity"] = np.eye(P, dtype=np.float32)
    if stage >= 2:
        d["ffg_r"] = f(a("L0_ffn_ln_gamma").reshape(G, P).T)
        d["ffb_r"] = f(a("L0_ffn_ln_beta").reshape(G, P).T)
        d["fc1q"] = f(np.asarray(inp["L0_ffn_fc1_q"], np.int32))
        d["fc1s"] = f(a("L0_ffn_fc1_scales"))
        d["fc1b_r"] = f(a("L0_ffn_fc1_bias").reshape(GF, P).T)
        d["fc2q"] = f(np.asarray(inp["L0_ffn_fc2_q"], np.int32))
        d["fc2s"] = f(a("L0_ffn_fc2_scales"))
        d["fc2b_r"] = f(a("L0_ffn_fc2_bias").reshape(G, P).T)
    if stage >= 3:
        d["wqT"] = f(a("L1_attn_Wq").T)
        d["wkT"] = f(a("L1_attn_Wk").T)
        d["wvT"] = f(a("L1_attn_Wv").T)
        d["wg1T"] = f(a("L1_attn_Wg1").T)
        d["wg2T"] = f(a("L1_attn_Wg2").T)
        d["gnorm_r"] = f(a("L1_attn_g_norm_weight").reshape(H, P).T)
        d["woT"] = f(a("L1_attn_Wo").T)
        d["trimask"] = np.triu(np.ones((P, P), np.float32))
    if stage >= 4:
        d["w1T"] = f(a("L1_ffn_W1").T)
        d["w1b_r"] = f(a("L1_ffn_b1").reshape(GF, P).T)
        d["w2T"] = f(a("L1_ffn_W2").T)
        d["w2b_r"] = f(a("L1_ffn_b2").reshape(G, P).T)
    return d


_CACHE = {}


def run_stage(inputs, stage, cores=None):
    if stage not in _CACHE:
        _CACHE[stage] = Bld(stage=stage).build()
    nc = _CACHE[stage]
    w = _prep_weights(inputs, stage)
    x = np.asarray(inputs["L0_attn_x"], np.float32)
    nb = x.shape[0] if cores is None else cores
    in_maps = []
    for bi in range(nb):
        m = dict(w)
        m["x_cm"] = np.ascontiguousarray(x[bi].T)
        in_maps.append(m)
    res = run_bass_kernel_spmd(nc, in_maps, core_ids=list(range(nb)))
    return np.stack([np.ascontiguousarray(np.asarray(r["y_cm"]).T) for r in res.results], axis=0)


def kernel(**inputs):
    return run_stage(inputs, stage=4)


# version 13
# speedup vs baseline: 1.0518x; 1.0026x over previous
import numpy as np
import ml_dtypes
import concourse.bacc as bacc
import concourse.mybir as mybir
from concourse.bass_utils import run_bass_kernel_spmd
from concourse.dve_spec import Spec, lower, Src0, Src1, C0, C1, maxx, minn, AluOp, scan
from concourse.dve_op import DveOp
from concourse.dve_table_gen import dve_ver_for
import concourse.dve_ops as dve_ops

F32 = mybir.dt.float32
BF16 = mybir.dt.bfloat16
I32 = mybir.dt.int32
AL = mybir.AluOpType
AF = mybir.ActivationFunctionType
AX = mybir.AxisListType

P = 128
D = 1024
T = 2048
G = D // P
NST = 32
T1 = 128
NC1 = T // T1
T2 = 8
NSC = T1 // T2
GN = G * NST
RW = GN + 1
FF = 4096
GF = FF // P
H = 8
TC = 128
EPS = 1e-5

_VER = dve_ver_for("TRN2")


def _register_ops():
    def mk(name, spec, reference):
        for o in dve_ops.OPS:
            if o.name == name:
                return o
        op = DveOp(name, spec, subdim=False,
                   uops_sha={_VER: lower(spec, ver=_VER, perf_en=False, name=name).sha(_VER)},
                   reference=reference)
        dve_ops.OPS.append(op)
        dve_ops._SUB_OPCODE_FOR_NAME[name] = dve_ops._CUSTOM_DVE_ROW_BASE + len(dve_ops.OPS) - 1
        dve_ops.CUSTOM_DVE_SPECS[name] = spec
        dve_ops.CUSTOM_DVE_REFS[name] = reference
        return op

    def _ref_mulscan(in0, in1, s0, s1, imm2):
        a = np.asarray(in0, np.float32) * np.asarray(in1, np.float32)
        f = a.reshape(a.shape[0], -1)
        return np.cumsum(f, axis=-1).reshape(np.asarray(in0).shape)

    addclip = mk("MK_ADDCLIP", Spec(body=minn(maxx(Src0 + Src1, C0), C1)),
                 lambda in0, in1, s0, s1, imm2: np.minimum(np.maximum(in0 + in1, s0), s1))
    mulscan = mk("MK_MULSCAN", Spec(body=scan(AluOp.ADD, Src0 * Src1)), _ref_mulscan)
    return addclip, mulscan


ADDCLIP, MULSCAN = _register_ops()


class Bld:
    def __init__(self, stage=1):
        self.stage = stage
        self.nc = bacc.Bacc("TRN2", target_bir_lowering=False, detect_race_conditions=False)
        self._ctx = []
        self._declare()

    def _enter(self, cm):
        v = cm.__enter__()
        self._ctx.append(cm)
        return v

    def _declare(self):
        nc = self.nc

        def din(name, shape, dt=F32):
            return nc.dram_tensor(name, list(shape), dt, kind="ExternalInput").ap()

        def dscr(name, shape, dt=BF16):
            return nc.dram_tensor(name, list(shape), dt).ap()

        i = self.i = {}
        i["x"] = din("x_cm", (D, T))
        i["alog"] = din("alog_r", (P, GN))
        i["inwT"] = din("inwT", (D, 2 * D))
        i["brow_l0"] = din("brow_l0", (1, 1024))
        i["inb_xa"] = din("inb_xa_r", (P, G))
        i["inb_z"] = din("inb_z_r", (P, G))
        i["convw"] = din("convw_r", (P, 4 * G))
        i["convb"] = din("convb_r", (P, G))
        i["bcwT"] = din("bcwT", (D, 2 * NST))
        i["bcb"] = din("bcb", (1, 2 * NST))
        i["outwT"] = din("outwT", (D, D))
        i["lnw"] = din("lnw_r", (P, G))
        i["lnb"] = din("lnb_r", (P, G))
        i["ones"] = din("ones", (P, 512))
        i["identity"] = din("identity", (P, P))
        if self.stage >= 2:
            i["ffg"] = din("ffg_r", (P, G))
            i["ffb"] = din("ffb_r", (P, G))
            i["fc1q"] = din("fc1q", (D, FF), I32)
            i["fc1s"] = din("fc1s", (8, FF))
            i["fc1b"] = din("fc1b_r", (P, GF))
            i["fc2q"] = din("fc2q", (FF, D), I32)
            i["fc2s"] = din("fc2s", (32, D))
            i["fc2b"] = din("fc2b_r", (P, G))
        if self.stage >= 3:
            i["wqT"] = din("wqT", (D, D))
            i["wkT"] = din("wkT", (D, D))
            i["wvT"] = din("wvT", (D, D))
            i["wg1T"] = din("wg1T", (D, P))
            i["wg2T"] = din("wg2T", (P, D))
            i["gnorm"] = din("gnorm_r", (P, H))
            i["woT"] = din("woT", (D, D))
            i["trimask"] = din("trimask", (P, P))
        if self.stage >= 4:
            i["w1T"] = din("w1T", (D, FF))
            i["w1b"] = din("w1b_r", (P, GF))
            i["w2T"] = din("w2T", (FF, D))
            i["w2b"] = din("w2b_r", (P, G))

        self.y_out = nc.dram_tensor("y_cm", [D, T], F32, kind="ExternalOutput").ap()

        s = self.scr = {}
        s["inw"] = dscr("s_inw", (P, 16 * G * P))
        s["outw"] = dscr("s_outw", (P, G * G * P))
        s["bm"] = dscr("s_bm", (1, T * NST), F32)
        s["cm"] = dscr("s_cm", (1, T * NST), F32)
        if self.stage >= 2:
            s["fc1"] = dscr("s_fc1", (P, GF * G * P))
            s["fc2"] = dscr("s_fc2", (P, G * GF * P))
        if self.stage >= 3:
            s["wq"] = dscr("s_wq", (P, G * G * P))
            s["wk"] = dscr("s_wk", (P, G * G * P))
            s["wv"] = dscr("s_wv", (P, G * G * P))
            s["wo"] = dscr("s_wo", (P, G * G * P))
            s["wg1"] = dscr("s_wg1", (P, G * P))
            s["wg2"] = dscr("s_wg2", (P, G * P))
        if self.stage >= 4:
            s["w1"] = dscr("s_w1", (P, GF * G * P))
            s["w2"] = dscr("s_w2", (P, G * GF * P))

        sb = self.sb = {}

        def sbuf(name, shape, dt=F32):
            sb[name] = self._enter(nc.sbuf_tensor(name, list(shape), dt))

        def psum(name, shape, dt=F32):
            sb[name] = self._enter(nc.psum_tensor(name, list(shape), dt))

        sbuf("stream", (P, G * T))
        sbuf("xab", (P, 3 * G * (T1 + 3)))
        sbuf("szb", (P, 2 * G * T1))
        sbuf("xcb", (P, 2 * G * T1))
        sbuf("xcbb", (P, 2 * G * T1), BF16)
        sbuf("ln0b", (P, 2 * G * T1), BF16)
        sbuf("ln0f", (P, 3 * G * T1))
        sbuf("bbb", (P, 2 * T2 * NST))
        sbuf("bxb", (P, 2 * T2 * GN), BF16)
        sbuf("rbuf", (P, T2 * RW))
        sbuf("sprime", (P, GN))
        sbuf("yb", (P, G * T1))
        sbuf("ymb", (P, 2 * G * T1), BF16)
        sbuf("hb", (P, 2 * G * T1))
        sbuf("sq", (P, G * T1))
        sbuf("cstrip", (1, 2 * T2 * NST))
        sbuf("bmcm", (P, 2 * NST))
        sbuf("wst", (P, 4 * G * P), BF16)
        sbuf("stage", (P, 1024))
        sbuf("stageb", (P, 1024), BF16)
        sbuf("onesT", (P, 512))
        sbuf("identf", (P, P))
        sbuf("identb", (P, P), BF16)
        sbuf("amisc", (P, GN))
        sbuf("smallA", (P, 128))
        sbuf("smallB", (P, 64))
        sbuf("bcbrow", (1, 2 * NST))
        sbuf("strip", (1, 1536))
        sbuf("convm", (P, 2 * 4 * T1))
        sbuf("cst", (P, 4))
        sbuf("brow", (1, 3072))
        sbuf("sgx", (P, 1024))
        sbuf("pin", (P, 512))
        sbuf("psc", (P, 512))
        sbuf("ptmp", (P, 512))
        sbuf("pacc", (P, 272))

        psum("ps_a", (P, 512))
        psum("ps_cb0", (P, 512))
        psum("ps_cb1", (P, 512))
        psum("ps_w0", (P, 512))
        psum("ps_w1", (P, 512))
        psum("ps_m0", (P, 512))
        psum("ps_m1", (P, 512))
        psum("ps_x", (P, 512))

        self.semh = {}
        for nm in ["c_v", "c_s", "c_t", "c_g", "d_in", "d_c", "d_cs", "d_w0", "d_w1",
                   "d_w2", "d_w3", "d_b0", "d_b1", "d_bm", "d_out", "d_misc",
                   "d_pin", "d_po"]:
            self.semh[nm] = self._enter(nc.semaphore(nm))
        self.block = self._enter(nc.Block())
        self._pump_tasks()

    def _mk_cast(self, src, dst4, k, c0, w, endmark=None):
        sb = self.sb

        def dma(me, j):
            sl = (j % 2) * 256
            me.dma("scalar", sb["pin"][:, sl:sl + w], src[k * P:(k + 1) * P, c0:c0 + w],
                   "d_pin")

        def cmp(me, j, pc):
            sl = (j % 2) * 256
            slb = (j % 2) * 512
            me.wait("scalar", "d_pin", pc["din"])
            me.on("scalar", lambda sc: sc.activation(
                out=sb["psc"].bitcast(BF16)[:, slb:slb + w], in_=sb["pin"][:, sl:sl + w],
                func=AF.Copy, bias=0.0, scale=1.0), inc="c_s")
            me.mark(("ppc", j), "c_s")

        def fin(me, j, pc):
            slb = (j % 2) * 512
            m0, nm = c0 // P, w // P
            me.dma("scalar", dst4[:, m0:m0 + nm, k, :],
                   sb["psc"].bitcast(BF16)[:, slb:slb + w].rearrange("p (m q) -> p m q", q=P),
                   "d_po")
            pc["dout"] = me.cnt["d_po"]
            if endmark is not None:
                me.mark(("pd", endmark), "d_po")
        return {"dma": dma, "cmp": cmp, "fin": fin}

    def _mk_dq(self, qsrc, ssrc, dst4, k, c0, w, eng, endmark=None):
        sb = self.sb
        eng = "vector"
        csem = "c_v"

        def dma(me, j):
            sl = (j % 2) * 256
            me.dma("scalar", sb["pin"].bitcast(I32)[:, sl:sl + w],
                   qsrc[k * P:(k + 1) * P, c0:c0 + w], "d_pin")
            me.dma("scalar", sb["psc"][:, sl:sl + w],
                   ssrc[k:k + 1, c0:c0 + w].broadcast_to([P, w]), "d_pin")

        def cmp(me, j, pc):
            sl = (j % 2) * 256
            slb = (j % 2) * 512
            me.wait("scalar", "d_pin", pc["din"])
            me.on("scalar", lambda sc: sc.activation(
                out=sb["ptmp"][:, sl:sl + w], in_=sb["pin"].bitcast(I32)[:, sl:sl + w],
                func=AF.Copy, bias=-128.0, scale=1.0), inc="c_s")
            cs = me.cnt["c_s"]
            me.wait(eng, "c_s", cs)
            me.wait(eng, "d_pin", pc["din"])
            me.on(eng, lambda e: e.tensor_tensor(
                out=sb["pin"].bitcast(BF16)[:, slb:slb + w], in0=sb["ptmp"][:, sl:sl + w],
                in1=sb["psc"][:, sl:sl + w], op=AL.mult), inc=csem)
            me.mark(("ppc", j), csem)
            pc["cval"] = me.cnt[csem]

        def fin(me, j, pc):
            slb = (j % 2) * 512
            me.wait("scalar", csem, pc["cval"])
            m0, nm = c0 // P, w // P
            me.dma("scalar", dst4[:, m0:m0 + nm, k, :],
                   sb["pin"].bitcast(BF16)[:, slb:slb + w].rearrange("p (m q) -> p m q", q=P),
                   "d_po")
            pc["dout"] = me.cnt["d_po"]
            if endmark is not None:
                me.mark(("pd", endmark), "d_po")
        return {"dma": dma, "cmp": cmp, "fin": fin}

    def _mk_t1(self, src, k, c0, w, aidx, wn, last=False):
        sb = self.sb

        def dma(me, j):
            sl = (j % 2) * 256
            me.dma("scalar", sb["pin"][:, sl:sl + w], src[k * P:(k + 1) * P, c0:c0 + w],
                   "d_pin")

        def cmp(me, j, pc):
            sl = (j % 2) * 256
            me.wait("scalar", "d_pin", pc["din"])
            me.on("scalar", lambda sc: sc.activation(
                out=sb["ptmp"][:, sl:sl + w], in_=sb["pin"][:, sl:sl + w],
                func=AF.Abs, bias=0.0, scale=1.0,
                accum_out=sb["pacc"][:, aidx:aidx + 1]), inc="c_s")
            me.mark(("ppc", j), "c_s")
            if last:
                me.mark(("t1done", wn), "c_s")
        return {"dma": dma, "cmp": cmp, "fin": None}

    def _mk_treduce(self, wn, a0, nacc, nelem):
        sb = self.sb
        cA, cB = 300 + wn, 304 + 4 * wn

        def cmp(me, j, pc):
            me.wmark("vector", ("t1done", wn))
            me.on("vector", lambda v: v.tensor_reduce(
                out=sb["pacc"][:, 260 + wn:261 + wn], in_=sb["pacc"][:, a0:a0 + nacc],
                axis=AX.X, op=AL.add), inc="c_v")
            cv = me.cnt["c_v"]
            me.wait("tensor", "c_v", cv)
            me.wmark("tensor", ("blk1", NC1 - 1))
            me.on("tensor", lambda t: t.matmul(
                sb["ps_x"][0:1, cA:cA + 1], sb["pacc"][:, 260 + wn:261 + wn],
                sb["onesT"][:, 0:1], start=True, stop=True, skip_group_check=True),
                inc="c_t")
            ct = me.cnt["c_t"]
            me.wait("vector", "c_t", ct)
            me.on("vector", lambda v: v.tensor_scalar(
                out=sb["pacc"][0:1, 264 + wn:265 + wn], in0=sb["ps_x"][0:1, cA:cA + 1],
                scalar1=1.0 / nelem, scalar2=1e-8, op0=AL.mult, op1=AL.max), inc="c_v")
            me.on("vector", lambda v: v.reciprocal(
                out=sb["pacc"][0:1, 264 + wn:265 + wn],
                in_=sb["pacc"][0:1, 264 + wn:265 + wn]), inc="c_v")
            cv2 = me.cnt["c_v"]
            me.wait("tensor", "c_v", cv2)
            me.on("tensor", lambda t: t.matmul(
                sb["ps_x"][:, cB:cB + 1], sb["onesT"][0:1, 0:P],
                sb["pacc"][0:1, 264 + wn:265 + wn], start=True, stop=True,
                skip_group_check=True), inc="c_t")
            ct2 = me.cnt["c_t"]
            me.wait("vector", "c_t", ct2)
            me.on("vector", lambda v: v.tensor_copy(
                out=sb["pacc"][:, 256 + wn:257 + wn], in_=sb["ps_x"][:, cB:cB + 1]),
                inc="c_v")
            me.mark(("trec", wn), "c_v")
            me.mark(("ppc", j), "c_v")
        return {"dma": None, "cmp": cmp, "fin": None,
                "hold": ("blk1", NC1 - 1)}

    def _mk_t2(self, src, dst4, k, c0, w, wn, eng, endmark=None):
        sb = self.sb
        eng = "vector"
        csem = "c_v"

        def dma(me, j):
            sl = (j % 2) * 256
            me.dma("scalar", sb["pin"][:, sl:sl + w], src[k * P:(k + 1) * P, c0:c0 + w],
                   "d_pin")

        def cmp(me, j, pc):
            sl = (j % 2) * 256
            slb = (j % 2) * 512
            me.wait("scalar", "d_pin", pc["din"])
            me.wmark("scalar", ("trec", wn))
            recip = sb["pacc"][:, 256 + wn:257 + wn]
            me.on("scalar", lambda sc: sc.activation(
                out=sb["ptmp"][:, sl:sl + w], in_=sb["pin"][:, sl:sl + w],
                func=AF.Abs, bias=0.0, scale=recip))
            me.on("scalar", lambda sc: sc.activation(
                out=sb["psc"][:, sl:sl + w], in_=sb["pin"][:, sl:sl + w],
                func=AF.Sign, bias=0.0, scale=1.0), inc="c_s")
            cs = me.cnt["c_s"]
            me.wait(eng, "c_s", cs)
            me.on(eng, lambda e: e.tensor_scalar(
                out=sb["ptmp"][:, sl:sl + w], in0=sb["ptmp"][:, sl:sl + w],
                scalar1=0.5, scalar2=None, op0=AL.is_gt))
            me.on(eng, lambda e: e.tensor_tensor(
                out=sb["pin"].bitcast(BF16)[:, slb:slb + w], in0=sb["ptmp"][:, sl:sl + w],
                in1=sb["psc"][:, sl:sl + w], op=AL.mult), inc=csem)
            me.mark(("ppc", j), csem)
            pc["cval"] = me.cnt[csem]

        def fin(me, j, pc):
            slb = (j % 2) * 512
            me.wait("scalar", csem, pc["cval"])
            m0, nm = c0 // P, w // P
            me.dma("scalar", dst4[:, m0:m0 + nm, k, :],
                   sb["pin"].bitcast(BF16)[:, slb:slb + w].rearrange("p (m q) -> p m q", q=P),
                   "d_po")
            pc["dout"] = me.cnt["d_po"]
            if endmark is not None:
                me.mark(("pd", endmark), "d_po")
        return {"dma": dma, "cmp": cmp, "fin": fin}

    def _pump_tasks(self):
        self.ptasks = []
        if self.stage < 2:
            return
        i, s = self.i, self.scr

        def pieces(ncols, wmax=256):
            out = []
            c0 = 0
            while c0 < ncols:
                out.append((c0, min(wmax, ncols - c0)))
                c0 += wmax
            return out

        def add_dq(qap, sap, dst, kchunks, ncols, eng, name):
            dst4 = dst.rearrange("p (m kk q) -> p m kk q", kk=kchunks, q=P)
            items = [(k, c0, w) for k in range(kchunks) for (c0, w) in pieces(ncols)]
            for n, (k, c0, w) in enumerate(items):
                self.ptasks.append(self._mk_dq(
                    qap, sap, dst4, k, c0, w, eng,
                    endmark=(name if n == len(items) - 1 else None)))

        def add_cast(src, dst, kchunks, ncols, name):
            dst4 = dst.rearrange("p (m kk q) -> p m kk q", kk=kchunks, q=P)
            items = [(k, c0, w) for k in range(kchunks) for (c0, w) in pieces(ncols)]
            for n, (k, c0, w) in enumerate(items):
                self.ptasks.append(self._mk_cast(
                    src, dst4, k, c0, w,
                    endmark=(name if n == len(items) - 1 else None)))

        def add_tern(src, dst, kchunks, ncols, wn, a0, name):
            dst4 = dst.rearrange("p (m kk q) -> p m kk q", kk=kchunks, q=P)
            items = [(k, c0, w) for k in range(kchunks) for (c0, w) in pieces(ncols)]
            t1 = []
            for n, (k, c0, w) in enumerate(items):
                t1.append(self._mk_t1(src, k, c0, w, a0 + n, wn,
                                      last=(n == len(items) - 1)))
            t2 = [self._mk_t2(src, dst4, k, c0, w, wn, "gpsimd",
                              endmark=(name if n == len(items) - 1 else None))
                  for n, (k, c0, w) in enumerate(items)]
            red = self._mk_treduce(wn, a0, len(items), kchunks * P * ncols)
            return t1, red, t2

        add_dq(i["fc1q"], i["fc1s"], s["fc1"], G, FF, "gpsimd", "fc1")
        add_dq(i["fc2q"], i["fc2s"], s["fc2"], GF, D, "vector", "fc2")
        if self.stage >= 3:
            add_cast(i["wqT"], s["wq"], G, D, "wq")
            add_cast(i["wkT"], s["wk"], G, D, "wk")
            add_cast(i["wvT"], s["wv"], G, D, "wv")
            add_cast(i["woT"], s["wo"], G, D, "wo")
            add_cast(i["wg1T"], s["wg1"], G, P, "wg1")
            add_cast(i["wg2T"], s["wg2"], 1, D, "wg2")
        if self.stage >= 4:
            t1a, reda, t2a = add_tern(i["w1T"], s["w1"], G, FF, 0, 0, "w1")
            t1b, redb, t2b = add_tern(i["w2T"], s["w2"], GF, D, 1, 128, "w2")
            self.ptasks += t1a + t1b + [reda, redb] + t2a + t2b

    def pump_step(self, n=1):
        for _ in range(n):
            self._pump_step1()

    def _pump_step1(self):
        tl = self.ptasks
        n = len(tl)
        jh = min(self.p_cmp, self.p_dma)
        if jh < n and tl[jh].get("hold") is not None and tl[jh]["hold"] not in self.marks:
            return
        if self.p_out < n and self.p_out < self.p_cmp:
            j = self.p_out
            if tl[j]["fin"] is not None:
                tl[j]["fin"](self, j, self.pstate[j])
            self.p_out += 1
        if self.p_cmp < n and self.p_cmp < self.p_dma:
            j = self.p_cmp
            tl[j]["cmp"](self, j, self.pstate[j])
            self.p_cmp += 1
        j = self.p_dma
        if j < n:
            self.pstate[j] = {"din": 0}
            if tl[j]["dma"] is not None:
                if j >= 2:
                    self.wmark("scalar", ("ppc", j - 2))
                    self.wait("scalar", "d_po", self.pstate[j - 2].get("dout", 0))
                tl[j]["dma"](self, j)
                self.pstate[j]["din"] = self.cnt["d_pin"]
            self.p_dma += 1

    def pump_drain(self):
        guard = 0
        while self.p_out < len(self.ptasks):
            self.pump_step()
            guard += 1
            if guard > 10000:
                raise RuntimeError("pump_drain stuck")

    def _reset(self, eng_name, eng):
        self.eng_name = eng_name
        self.eng = eng
        self.cnt = {k: 0 for k in self.semh}
        self.marks = {}
        self.pmgate = {0: None, 1: None}
        self.sqgate = None
        self._selfw = {}
        self.slot_gate = {}
        self._mk = 0
        self.p_dma = 0
        self.p_cmp = 0
        self.p_out = 0
        self.pstate = {}
        self.scanfeed = {}
        self.wstfeed = {}

    def on(self, name, fn, inc=None, n=1):
        if self.eng_name == name:
            if inc is not None and self.cnt[inc] > self._selfw.get(inc, 0):
                self.eng.wait_ge(self.semh[inc], self.cnt[inc])
                self._selfw[inc] = self.cnt[inc]
            ins = fn(self.eng)
            if inc is not None:
                ins.then_inc(self.semh[inc], n)
        if inc is not None:
            self.cnt[inc] += n
        return self.cnt.get(inc)

    def wait(self, name, sem, val):
        if val is None or val <= 0:
            return
        if self.eng_name == name:
            self.eng.wait_ge(self.semh[sem], val)

    def dma(self, q, out, in_, sem="d_w", **kw):
        if self.eng_name == q:
            self.eng.dma_start(out=out, in_=in_, **kw).then_inc(self.semh[sem], 16)
        self.cnt[sem] += 16
        return self.cnt[sem]

    def mark(self, key, sem):
        self.marks[key] = (sem, self.cnt[sem])

    def wmark(self, name, key):
        if key in self.marks:
            sem, val = self.marks[key]
            self.wait(name, sem, val)

    def walk(self):
        self.prologue()
        self.l0_pipeline()
        if self.stage >= 2:
            self.ffn0()
        if self.stage >= 3:
            self.l1_attn()
        if self.stage >= 4:
            self.ffn1()
        self.epilogue()

    def prologue(self):
        sb, i = self.sb, self.i
        for g in range(G):
            self.dma("sync", sb["stream"][:, g * T:(g + 1) * T], i["x"][g * P:(g + 1) * P, :], "d_in")
        self.dma("sync", sb["onesT"][:], i["ones"], "d_misc")
        self.dma("sync", sb["identf"][:], i["identity"], "d_misc")
        self.dma("sync", sb["amisc"][:], i["alog"], "d_misc")
        self.dma("sync", sb["brow"][0:1, 0:1024], i["brow_l0"], "d_misc")
        self.dma("sync", sb["smallA"][:, 40:48], i["inb_z"], "d_misc")
        self.dma("sync", sb["smallA"][:, 48:56], i["inb_xa"], "d_misc")
        self.dma("sync", sb["smallA"][:, 0:32], i["convw"], "d_misc")
        self.dma("sync", sb["smallA"][:, 32:40], i["convb"], "d_misc")
        self.dma("sync", sb["smallB"][:, 0:8], i["lnw"], "d_misc")
        self.dma("sync", sb["smallB"][:, 8:16], i["lnb"], "d_misc")
        self.dma("sync", sb["bcbrow"][:], i["bcb"], "d_misc")
        if self.stage >= 2:
            self.dma("sync", sb["smallB"][:, 16:24], i["ffg"], "d_misc")
            self.dma("sync", sb["smallB"][:, 24:32], i["ffb"], "d_misc")
            self.dma("sync", sb["smallB"][:, 32:40], i["fc2b"], "d_misc")
            self.dma("sync", sb["smallA"][:, 64:96], i["fc1b"], "d_misc")
        if self.stage >= 3:
            self.dma("sync", sb["smallB"][:, 40:48], i["gnorm"], "d_misc")
        if self.stage >= 4:
            self.dma("sync", sb["smallB"][:, 48:56], i["w2b"], "d_misc")
            self.dma("sync", sb["smallA"][:, 96:128], i["w1b"], "d_misc")
        nmisc = self.cnt["d_misc"]

        self.wait("vector", "d_misc", nmisc)
        self.on("vector", lambda v: v.tensor_scalar(out=sb["amisc"][:], in0=sb["amisc"][:],
                                                    scalar1=2.0, scalar2=-5.0, op0=AL.min, op1=AL.max),
                inc="c_v")
        self.wait("scalar", "c_v", self.cnt["c_v"])
        self.on("scalar", lambda sc: sc.activation(out=sb["amisc"][:], in_=sb["amisc"][:],
                                                   func=AF.Exp, bias=0.0, scale=1.0), inc="c_s")
        self.wait("vector", "c_s", self.cnt["c_s"])
        self.on("vector", lambda v: v.tensor_scalar(out=sb["ps_a"][:, 0:GN], in0=sb["amisc"][:],
                                                    scalar1=-1.0, scalar2=None, op0=AL.mult), inc="c_v")
        self.on("vector", lambda v: v.memset(sb["sprime"][:], 0.0), inc="c_v")
        self.on("vector", lambda v: v.memset(sb["rbuf"][:], 0.0), inc="c_v")
        self.on("vector", lambda v: v.memset(sb["cst"][:, 0:1], EPS), inc="c_v")
        
        self.on("vector", lambda v: v.memset(sb["cst"][:, 1:2], 0.5), inc="c_v")
        self.on("vector", lambda v: v.memset(sb["cst"][:, 2:3], 0.08838834764831845), inc="c_v")
        self.on("vector", lambda v: v.tensor_copy(out=sb["identb"][:], in_=sb["identf"][:]), inc="c_v")
        self.on("vector", lambda v: v.tensor_reduce(
            out=sb["smallA"][:, 56:64],
            in_=sb["smallA"][:, 0:32].rearrange("p (g j) -> p g j", j=4),
            axis=AX.X, op=AL.add), inc="c_v")
        self.on("vector", lambda v: v.tensor_tensor(
            out=sb["smallA"][:, 56:64], in0=sb["smallA"][:, 56:64],
            in1=sb["smallA"][:, 48:56], op=AL.mult), inc="c_v")
        self.on("vector", lambda v: v.tensor_tensor(
            out=sb["smallA"][:, 32:40], in0=sb["smallA"][:, 32:40],
            in1=sb["smallA"][:, 56:64], op=AL.add), inc="c_v")
        self.mark("init_v", "c_v")
        self.on("gpsimd", lambda g: g.memset(sb["xab"][:], 0.0), inc="c_g")
        self.mark("init_g", "c_g")

        self._cast_tiled(i["inwT"], self.scr["inw"], kchunks=G, mchunks=16)
        self.mark(("castinw",), "d_cs")
        self._cast_tiled(i["outwT"], self.scr["outw"], kchunks=G, mchunks=G)
        self.mark(("castoutw",), "d_cs")
        self.wait("sync", "d_cs", self.cnt["d_cs"])
        for k in range(G):
            self.dma("sync", sb["stage"][:, k * 64:(k + 1) * 64], i["bcwT"][k * P:(k + 1) * P, :], "d_c")
        self.wait("vector", "d_c", self.cnt["d_c"])
        self.wait("vector", "d_cs", self.cnt["d_cs"])
        self.on("vector", lambda v: v.tensor_copy(out=sb["stageb"][:, 0:512],
                                                  in_=sb["stage"][:, 0:512]), inc="c_v")
        self.mark("bcw", "c_v")

    def _cast_tiled(self, src, dst, kchunks, mchunks):
        sb = self.sb
        ncols = mchunks * P
        dst4 = dst.rearrange("p (m kk q) -> p m kk q", kk=kchunks, q=P)
        for k in range(kchunks):
            for c0 in range(0, ncols, 1024):
                w = min(1024, ncols - c0)
                self.wait("sync", "d_cs", self.cnt["d_cs"])
                self.dma("sync", sb["stage"][:, 0:w], src[k * P:(k + 1) * P, c0:c0 + w], "d_c")
                self.wait("vector", "d_c", self.cnt["d_c"])
                self.wait("vector", "d_cs", self.cnt["d_cs"])
                self.on("vector", lambda v, w=w: v.tensor_copy(out=sb["stageb"][:, 0:w],
                                                               in_=sb["stage"][:, 0:w]), inc="c_v")
                self.wait("sync", "c_v", self.cnt["c_v"])
                m0, nm = c0 // P, w // P
                out_ap = dst4[:, m0:m0 + nm, k, :]
                in_ap = sb["stageb"][:, 0:w].rearrange("p (m q) -> p m q", q=P)
                self.dma("sync", out_ap, in_ap, "d_cs")

    def l0_pipeline(self):
        self.wait("scalar", "d_in", 16 * G)
        self.wait("tensor", "d_in", 16 * G)
        for w in range(-3, NC1 + 2):
            for k in range(NSC):
                self._slot(w, k)

    def _slot(self, w, k):
        cf, kf = w + (k + 2) // NSC, (k + 2) % NSC
        cc, kc = w + (k + 1) // NSC, (k + 1) % NSC
        if 0 <= cf < NC1:
            self._scan_feed(cf, kf)
        if 0 <= cc < NC1:
            self._scan_bx(cc, kc)
            self._scan_cb(cc, kc)
        if k == 0 and 0 <= w - 1 < NC1:
            self._ym(w - 1)
        c_op = w - 2
        if 0 <= c_op < NC1:
            if k == 0:
                self._lnr_a(c_op)
            elif k == 1:
                self._lnr_b(c_op)
            if k < 8:
                self._outproj_tile(c_op, k)
            if 2 <= k < 10:
                self._hadd(c_op, k - 2)
            if k == 9:
                h0 = (c_op % 2) * G * T1
                hsrc = lambda g: self.sb["hb"][:, h0 + g * T1:h0 + (g + 1) * T1]
                self._lnst_a(hsrc, "ps_w0" if c_op % 2 == 0 else "ps_w1",
                             ("lnh", c_op), 512, src_gate=("hadd", c_op))
            elif k == 10:
                self._lnst_b("ps_w0" if c_op % 2 == 0 else "ps_w1", ("lnh", c_op), 512)
            elif k >= 12:
                self._lnh_apply(c_op, (k - 12) * 2)
        if 0 <= w < NC1:
            self._scan_steps(w, k)
        c_ip = w + 2
        if 0 <= c_ip < NC1:
            self._inproj_tile(c_ip, k)
        c_cv = w + 1
        if 0 <= c_cv < NC1:
            if 1 <= k <= 8:
                self._conv_g(c_cv, k - 1)
            elif k == 10:
                self._bmcm(c_cv)
        c_ln = w + 3
        if 0 <= c_ln < NC1:
            if k == 11:
                src = lambda g, c_ln=c_ln: self.sb["stream"][
                    :, g * T + c_ln * T1:g * T + c_ln * T1 + T1]
                self._lnst_a(src, "ps_w0" if c_ln % 2 == 0 else "ps_w1",
                             ("ln0", c_ln), 1024)
            elif k == 12:
                self._lnst_b("ps_w0" if c_ln % 2 == 0 else "ps_w1", ("ln0", c_ln), 1024)
            elif k in (14, 15):
                self._ln0_apply(c_ln, (k - 14) * 4)
        if w == NC1 - 4 and k == 9:
            sb = self.sb
            xa0 = ((NC1 - 1) % 3) * G * (T1 + 3)
            self.wmark("vector", ("convg", NC1 - 4, G - 1))
            self.on("vector", lambda v: v.memset(
                sb["xab"][:, xa0:xa0 + G * (T1 + 3)].rearrange(
                    "p (g q) -> p g q", g=G)[:, :, T1 + 2:T1 + 3], 0.0), inc="c_v")
            for g in range(G):
                self.mark(("halo", NC1 - 1, g), "c_v")
        self.pump_step(2)

    def _scan_feed(self, c, kf):
        sb, s = self.sb, self.scr
        ki = c * NSC + kf
        par = ki % 2
        t0 = c * T1 + kf * T2
        bsem = "d_b0" if par == 0 else "d_b1"
        self.wmark("sync", ("bmdram", c))
        self.wmark("sync", ("bx", ki - 2))
        self.wmark("sync", ("cb", ki - 2))
        self.dma("sync", sb["bbb"][:, par * T2 * NST:(par + 1) * T2 * NST],
                 s["bm"][0:1, t0 * NST:(t0 + T2) * NST].broadcast_to([P, T2 * NST]), bsem)
        self.dma("sync", sb["cstrip"][0:1, par * T2 * NST:(par + 1) * T2 * NST],
                 s["cm"][0:1, t0 * NST:(t0 + T2) * NST], bsem)
        self.scanfeed[ki] = self.cnt[bsem]

    def _scan_cb(self, c, kc):
        sb = self.sb
        ki = c * NSC + kc
        par = ki % 2
        bsem = "d_b0" if par == 0 else "d_b1"
        cbb = sb["ps_cb0"] if par == 0 else sb["ps_cb1"]
        self.wait("tensor", bsem, self.scanfeed[ki])
        self.wmark("tensor", ("scand", ki - 2))
        self.on("tensor", lambda t, cbb=cbb, par=par: t.matmul(
            cbb[:, 0:T2 * NST], sb["onesT"][0:1, 0:P],
            sb["cstrip"][0:1, par * T2 * NST:(par + 1) * T2 * NST], start=True, stop=True),
            inc="c_t")
        self.mark(("cb", ki), "c_t")

    def _bxb(self, ki):
        par = ki % 3
        if par < 2:
            return self.sb["bxb"][:, par * T2 * GN:(par + 1) * T2 * GN]
        return self.sb["ln0f"][:].bitcast(BF16)[:, 2048:2048 + T2 * GN]

    def _scan_bx(self, c, kc):
        sb = self.sb
        ki = c * NSC + kc
        par = ki % 2
        xc0 = (c % 2) * G * T1
        bsem = "d_b0" if par == 0 else "d_b1"
        self.wait("gpsimd", bsem, self.scanfeed[ki])
        self.wmark("gpsimd", ("xc", c))
        self.wmark("gpsimd", ("scand", ki - 3))
        bb3 = sb["bbb"][:, par * T2 * NST:(par + 1) * T2 * NST].rearrange(
            "p (u n) -> p u n", n=NST).unsqueeze(2).broadcast_to([P, T2, G, NST])
        xc3 = sb["xcb"][:, xc0:xc0 + G * T1].rearrange("p (g q) -> p g q", g=G)[
            :, :, kc * T2:(kc + 1) * T2].transpose([0, 2, 1]).unsqueeze(3).broadcast_to(
            [P, T2, G, NST])
        bx3 = self._bxb(ki).rearrange(
            "p (u g n) -> p u g n", g=G, n=NST)
        self.on("gpsimd", lambda gp, bx3=bx3, bb3=bb3, xc3=xc3: gp.tensor_tensor(
            out=bx3, in0=bb3, in1=xc3, op=AL.mult), inc="c_g")
        self.mark(("bx", ki), "c_g")

    def _scan_steps(self, c, k):
        sb = self.sb
        ki = c * NSC + k
        par = ki % 2
        cbb = sb["ps_cb0"] if par == 0 else sb["ps_cb1"]
        self.wmark("vector", ("cb", ki))
        self.wmark("vector", ("bx", ki))
        self.wmark("vector", ("extract", ki - 1))
        if ki == 0:
            self.wmark("vector", "init_v")
        bxbuf = self._bxb(ki)
        for u in range(T2):
            bxo = u * GN
            self.on("vector", lambda v: v.tensor_tensor(
                out=sb["ps_a"][:, 256:256 + GN], in0=sb["sprime"][:],
                in1=sb["ps_a"][:, 0:GN], op=AL.mult))
            self.on("vector", lambda v, bxo=bxo, bxbuf=bxbuf: v._custom_dve(
                ADDCLIP, out=sb["sprime"][:], in0=bxbuf[:, bxo:bxo + GN],
                in1=sb["ps_a"][:, 256:256 + GN], s0=-10.0, s1=10.0))
            ro = u * RW + 1
            self.on("vector", lambda v, u=u, ro=ro, cbb=cbb: v._custom_dve(
                MULSCAN,
                out=sb["rbuf"][:, ro:ro + GN].rearrange("p (g n) -> p g n", n=NST),
                in0=sb["sprime"][:].rearrange("p (g n) -> p g n", n=NST),
                in1=cbb[:, u * NST:(u + 1) * NST].unsqueeze(1).broadcast_to([P, G, NST])),
                    inc=("c_v" if u == T2 - 1 else None))
        self.mark(("scand", ki), "c_v")
        self.wmark("gpsimd", ("scand", ki))
        r3 = sb["rbuf"][:].rearrange("p (u w) -> p u w", w=RW)
        yv = sb["yb"][:, 0:G * T1].rearrange("p (g q) -> p g q", g=G)[
            :, :, k * T2:(k + 1) * T2].transpose([0, 2, 1])
        self.on("gpsimd", lambda gp, r3=r3, yv=yv: gp.tensor_tensor(
            out=yv, in0=r3[:, :, NST:GN + 1:NST], in1=r3[:, :, 0:GN:NST], op=AL.subtract),
            inc="c_g")
        self.mark(("extract", ki), "c_g")
        if k == NSC - 1:
            self.mark(("scanchunk", c), "c_g")

    def _szb(self, c):
        if c % 3 < 2:
            return self.sb["szb"][:, (c % 3) * G * T1:(c % 3 + 1) * G * T1]
        return self.sb["ln0f"][:, 0:G * T1]

    def _ym(self, c):
        sb = self.sb
        szv = self._szb(c)
        y0 = (c % 2) * G * T1
        self.wmark("gpsimd", ("scanchunk", c))
        self.wmark("gpsimd", ("ipz", c))
        self.wmark("gpsimd", ("opw", c - 2, G - 1))
        for g in range(G):
            self.on("gpsimd", lambda gp, g=g, y0=y0, szv=szv: gp.tensor_tensor(
                out=sb["ymb"][:, y0 + g * T1:y0 + (g + 1) * T1],
                in0=sb["yb"][:, g * T1:(g + 1) * T1],
                in1=szv[:, g * T1:(g + 1) * T1], op=AL.mult),
                    inc=("c_g" if g == G - 1 else None))
        self.mark(("ym", c), "c_g")

    def _lnst_a(self, srcf, bank, tag, soff, src_gate=None):
        sb = self.sb
        bk = sb[bank]
        if src_gate is not None:
            self.wmark("scalar", src_gate)
            self.wmark("tensor", src_gate)
        if self.sqgate is not None:
            self.wmark("scalar", self.sqgate)
        for g in range(G):
            self.on("scalar", lambda s, g=g: s.activation(out=sb["sq"][:, g * T1:(g + 1) * T1],
                                                          in_=srcf(g), func=AF.Square,
                                                          bias=0.0, scale=1.0),
                    inc=("c_s" if g == G - 1 else None))
        cs0 = self.cnt["c_s"]
        self.wait("tensor", "c_s", cs0)
        self.wmark("tensor", ("bankfree", bank))
        for g in range(G):
            self.on("tensor", lambda t, g=g: t.matmul(bk[0:1, 0:T1], sb["onesT"][:, 0:1],
                                                      srcf(g), start=(g == 0), stop=(g == G - 1)))
        for g in range(G):
            self.on("tensor", lambda t, g=g: t.matmul(bk[0:1, T1:2 * T1], sb["onesT"][:, 0:1],
                                                      sb["sq"][:, g * T1:(g + 1) * T1],
                                                      start=(g == 0), stop=(g == G - 1)),
                    inc=("c_t" if g == G - 1 else None))
        self.mark(("sqused", tag), "c_t")
        self.sqgate = ("sqused", tag)
        self.marks[("lnsum", tag)] = ("c_t", self.cnt["c_t"])

    def _lnst_b(self, bank, tag, soff):
        sb = self.sb
        bk = sb[bank]
        st = sb["strip"]
        sem, ctv = self.marks[("lnsum", tag)]
        self.wait("vector", sem, ctv)
        self.on("vector", lambda v: v.tensor_scalar(out=st[0:1, soff:soff + T1], in0=bk[0:1, 0:T1],
                                                    scalar1=1.0 / D, scalar2=None, op0=AL.mult))
        self.on("vector", lambda v: v.tensor_scalar(out=st[0:1, soff + T1:soff + 2 * T1],
                                                    in0=bk[0:1, T1:2 * T1],
                                                    scalar1=1.0 / D, scalar2=None, op0=AL.mult))
        self.on("vector", lambda v: v.tensor_tensor(out=st[0:1, soff + 2 * T1:soff + 3 * T1],
                                                    in0=st[0:1, soff:soff + T1],
                                                    in1=st[0:1, soff:soff + T1], op=AL.mult))
        self.on("vector", lambda v: v.tensor_tensor(out=st[0:1, soff + T1:soff + 2 * T1],
                                                    in0=st[0:1, soff + T1:soff + 2 * T1],
                                                    in1=st[0:1, soff + 2 * T1:soff + 3 * T1],
                                                    op=AL.subtract), inc="c_v")
        self.wait("scalar", "c_v", self.cnt["c_v"])
        self.on("scalar", lambda s: s.activation(out=st[0:1, soff + 2 * T1:soff + 3 * T1],
                                                 in_=st[0:1, soff + T1:soff + 2 * T1],
                                                 func=AF.Sqrt, bias=self.sb["cst"][0:1, 0:1],
                                                 scale=1.0), inc="c_s")
        self.wait("vector", "c_s", self.cnt["c_s"])
        self.on("vector", lambda v: v.reciprocal(out=st[0:1, soff + 2 * T1:soff + 3 * T1],
                                                 in_=st[0:1, soff + 2 * T1:soff + 3 * T1]))
        self.on("vector", lambda v: v.tensor_tensor(out=st[0:1, soff + 3 * T1:soff + 4 * T1],
                                                    in0=st[0:1, soff:soff + T1],
                                                    in1=st[0:1, soff + 2 * T1:soff + 3 * T1],
                                                    op=AL.mult), inc="c_v")
        self.wait("tensor", "c_v", self.cnt["c_v"])
        self.on("tensor", lambda t: t.matmul(bk[:, 256:256 + T1], sb["onesT"][0:1, 0:P],
                                             st[0:1, soff + 2 * T1:soff + 3 * T1],
                                             start=True, stop=True))
        self.on("tensor", lambda t: t.matmul(bk[:, 384:384 + T1], sb["onesT"][0:1, 0:P],
                                             st[0:1, soff + 3 * T1:soff + 4 * T1],
                                             start=True, stop=True), inc="c_t")
        self.mark(("lnb", tag), "c_t")

    def _lnstats(self, srcf, bank, tag):
        self._lnst_a(srcf, bank, tag, 0)
        self._lnst_b(bank, tag, 0)
        return self.cnt["c_t"]

    def _lnr_a(self, c):
        src = lambda g: self.sb["stream"][:, g * T + c * T1:g * T + c * T1 + T1]
        sb = self.sb
        bk = sb["ps_x"]
        if self.sqgate is not None:
            self.wmark("scalar", self.sqgate)
        for g in range(G):
            self.on("scalar", lambda s, g=g: s.activation(out=sb["sq"][:, g * T1:(g + 1) * T1],
                                                          in_=src(g), func=AF.Square,
                                                          bias=0.0, scale=1.0),
                    inc=("c_s" if g == G - 1 else None))
        cs0 = self.cnt["c_s"]
        self.wait("tensor", "c_s", cs0)
        self.wmark("tensor", ("hadd", c - 1))
        self.wmark("tensor", ("bmev", c + 2))
        for g in range(G):
            self.on("tensor", lambda t, g=g: t.matmul(bk[0:1, 64:64 + T1], sb["onesT"][:, 0:1],
                                                      src(g), start=(g == 0), stop=(g == G - 1)))
        for g in range(G):
            self.on("tensor", lambda t, g=g: t.matmul(bk[0:1, 64 + T1:64 + 2 * T1],
                                                      sb["onesT"][:, 0:1],
                                                      sb["sq"][:, g * T1:(g + 1) * T1],
                                                      start=(g == 0), stop=(g == G - 1)),
                    inc="c_t")
        self.mark(("sqused", ("lnr", c)), "c_t")
        self.sqgate = ("sqused", ("lnr", c))
        self.marks[("lnsum", ("lnr", c))] = ("c_t", self.cnt["c_t"])

    def _lnr_b(self, c):
        sb = self.sb
        bk = sb["ps_x"]
        st = sb["strip"]
        soff = 0
        sem, ctv = self.marks[("lnsum", ("lnr", c))]
        self.wait("vector", sem, ctv)
        self.on("vector", lambda v: v.tensor_scalar(out=st[0:1, soff:soff + T1],
                                                    in0=bk[0:1, 64:64 + T1],
                                                    scalar1=1.0 / D, scalar2=None, op0=AL.mult))
        self.on("vector", lambda v: v.tensor_scalar(out=st[0:1, soff + T1:soff + 2 * T1],
                                                    in0=bk[0:1, 64 + T1:64 + 2 * T1],
                                                    scalar1=1.0 / D, scalar2=None, op0=AL.mult))
        self.on("vector", lambda v: v.tensor_tensor(out=st[0:1, soff + 2 * T1:soff + 3 * T1],
                                                    in0=st[0:1, soff:soff + T1],
                                                    in1=st[0:1, soff:soff + T1], op=AL.mult))
        self.on("vector", lambda v: v.tensor_tensor(out=st[0:1, soff + T1:soff + 2 * T1],
                                                    in0=st[0:1, soff + T1:soff + 2 * T1],
                                                    in1=st[0:1, soff + 2 * T1:soff + 3 * T1],
                                                    op=AL.subtract), inc="c_v")
        self.wait("scalar", "c_v", self.cnt["c_v"])
        self.on("scalar", lambda s: s.activation(out=st[0:1, soff + 2 * T1:soff + 3 * T1],
                                                 in_=st[0:1, soff + T1:soff + 2 * T1],
                                                 func=AF.Sqrt, bias=self.sb["cst"][0:1, 0:1],
                                                 scale=1.0), inc="c_s")
        self.wait("vector", "c_s", self.cnt["c_s"])
        self.on("vector", lambda v: v.reciprocal(out=st[0:1, soff + 2 * T1:soff + 3 * T1],
                                                 in_=st[0:1, soff + 2 * T1:soff + 3 * T1]))
        self.on("vector", lambda v: v.tensor_tensor(out=st[0:1, soff + 3 * T1:soff + 4 * T1],
                                                    in0=st[0:1, soff:soff + T1],
                                                    in1=st[0:1, soff + 2 * T1:soff + 3 * T1],
                                                    op=AL.mult), inc="c_v")
        self.wait("tensor", "c_v", self.cnt["c_v"])
        self.on("tensor", lambda t: t.matmul(bk[:, 256:256 + T1], sb["onesT"][0:1, 0:P],
                                             st[0:1, soff + 2 * T1:soff + 3 * T1],
                                             start=True, stop=True))
        self.on("tensor", lambda t: t.matmul(bk[:, 384:384 + T1], sb["onesT"][0:1, 0:P],
                                             st[0:1, soff + 3 * T1:soff + 4 * T1],
                                             start=True, stop=True), inc="c_t")
        self.mark(("lnrb", c), "c_t")

    def _ln0_apply(self, c, g0):
        sb = self.sb
        bank = "ps_w0" if c % 2 == 0 else "ps_w1"
        bk = sb[bank]
        lb = (c % 2) * G * T1
        self.wmark("vector", ("lnb", ("ln0", c)))
        if g0 == 0:
            self.wmark("vector", ("ipw", c - 2, 15))
        for g in range(g0, g0 + 4):
            src = sb["stream"][:, g * T + c * T1:g * T + c * T1 + T1]
            self.on("vector", lambda v, g=g, src=src: v.tensor_tensor(
                out=sb["stage"][:, 512:512 + T1], in0=src, in1=bk[:, 256:256 + T1],
                op=AL.mult))
            self.on("vector", lambda v, g=g: v.tensor_tensor(
                out=sb["ln0b"][:, lb + g * T1:lb + (g + 1) * T1],
                in0=sb["stage"][:, 512:512 + T1], in1=bk[:, 384:384 + T1], op=AL.subtract),
                    inc=("c_v" if g == g0 + 3 else None))
        if g0 + 4 == G:
            self.mark(("ln0", c), "c_v")
            self.mark(("bankfree", bank), "c_v")

    def _inproj_tile(self, c, m):
        sb, s = self.sb, self.scr
        lb = (c % 2) * G * T1
        xa0 = (c % 3) * G * (T1 + 3)
        wslot = (m % 2) * G * P
        dsem = "d_w0" if m % 2 == 0 else "d_w1"
        if (m % 2) in self.slot_gate:
            self.wmark("sync", self.slot_gate[m % 2])
        self.wmark("sync", ("castinw",))
        self.dma("sync", sb["wst"][:, wslot:wslot + G * P],
                 s["inw"][:, m * G * P:(m + 1) * G * P], dsem)
        self.wait("tensor", dsem, self.cnt[dsem])
        self.wmark("tensor", ("ln0", c))
        pbi = m % 2
        pb = sb["ps_m0"] if pbi == 0 else sb["ps_m1"]
        if self.pmgate[pbi]:
            self.wmark("tensor", self.pmgate[pbi])
        for k in range(G):
            self.on("tensor", lambda t, k=k, pb=pb, wslot=wslot, lb=lb: t.matmul(
                pb[:, 0:T1], sb["wst"][:, wslot + k * P:wslot + (k + 1) * P],
                sb["ln0b"][:, lb + k * T1:lb + (k + 1) * T1],
                start=(k == 0), stop=(k == G - 1)), inc=("c_t" if k == G - 1 else None))
        self.mark(("ipw", c, m), "c_t")
        self.slot_gate[m % 2] = ("ipw", c, m)
        ctm = self.cnt["c_t"]
        self.wait("scalar", "c_t", ctm)
        if m < G:
            self.on("scalar", lambda sc, m=m, pb=pb: sc.activation(
                out=sb["xab"][:, xa0 + m * (T1 + 3) + 2:xa0 + m * (T1 + 3) + 2 + T1],
                in_=pb[:, 0:T1], func=AF.Copy, bias=0.0, scale=1.0),
                inc="c_s")
            if c > 0:
                pa = ((c - 1) % 3) * G * (T1 + 3)
                self.on("scalar", lambda sc, m=m: sc.copy(
                    out=sb["xab"][:, xa0 + m * (T1 + 3):xa0 + m * (T1 + 3) + 2],
                    in_=sb["xab"][:, pa + m * (T1 + 3) + T1:pa + m * (T1 + 3) + T1 + 2]))
                self.on("scalar", lambda sc, m=m: sc.copy(
                    out=sb["xab"][:, pa + m * (T1 + 3) + T1 + 2:pa + m * (T1 + 3) + T1 + 3],
                    in_=sb["xab"][:, xa0 + m * (T1 + 3) + 2:xa0 + m * (T1 + 3) + 3]),
                        inc="c_s")
                self.mark(("halo", c - 1, m), "c_s")
        else:
            self.wmark("scalar", ("ym", c - 3))
            szv = self._szb(c)
            self.on("scalar", lambda sc, m=m, pb=pb, szv=szv: sc.activation(
                out=szv[:, (m - G) * T1:(m - G + 1) * T1],
                in_=pb[:, 0:T1], func=AF.Silu,
                bias=sb["smallA"][:, 40 + m - G:41 + m - G], scale=1.0),
                inc="c_s")
            if m == 15:
                self.mark(("ipz", c), "c_s")
        self.mark(("ipev", c, m), "c_s")
        self.pmgate[pbi] = ("ipev", c, m)

    def _conv_g(self, c, g):
        sb = self.sb
        xa0 = (c % 3) * G * (T1 + 3)
        xc0 = (c % 2) * G * T1
        par = (g % 2) * 4 * T1
        self.wmark("scalar", ("halo", c, g))
        self.wmark("scalar", ("convg", c, g - 2))
        if g < 2:
            self.wmark("scalar", ("convg", c - 1, g + 6))
        if g == 0:
            self.wmark("scalar", ("scanchunk", c - 2))
            self.wmark("gpsimd", ("scanchunk", c - 2))
        for j in range(4):
            self.on("scalar", lambda sc, g=g, j=j, par=par: sc.activation(
                out=sb["convm"][:, par + j * T1:par + (j + 1) * T1],
                in_=sb["xab"][:, xa0 + g * (T1 + 3) + j:xa0 + g * (T1 + 3) + j + T1],
                func=AF.Copy, bias=0.0, scale=sb["smallA"][:, g * 4 + j:g * 4 + j + 1]),
                    inc=("c_s" if j == 3 else None))
        csg = self.cnt["c_s"]
        self.wait("gpsimd", "c_s", csg)
        self.on("gpsimd", lambda gp, par=par: gp.tensor_tensor(
            out=sb["convm"][:, par:par + T1], in0=sb["convm"][:, par:par + T1],
            in1=sb["convm"][:, par + T1:par + 2 * T1], op=AL.add))
        self.on("gpsimd", lambda gp, par=par: gp.tensor_tensor(
            out=sb["convm"][:, par:par + T1], in0=sb["convm"][:, par:par + T1],
            in1=sb["convm"][:, par + 2 * T1:par + 3 * T1], op=AL.add))
        self.on("gpsimd", lambda gp, g=g, par=par: gp.tensor_tensor(
            out=sb["xcb"][:, xc0 + g * T1:xc0 + (g + 1) * T1], in0=sb["convm"][:, par:par + T1],
            in1=sb["convm"][:, par + 3 * T1:par + 4 * T1], op=AL.add), inc="c_g")
        self.mark(("convg", c, g), "c_g")
        cg = self.cnt["c_g"]
        self.wait("scalar", "c_g", cg)
        self.on("scalar", lambda sc, g=g: sc.activation(
            out=sb["xcb"][:, xc0 + g * T1:xc0 + (g + 1) * T1],
            in_=sb["xcb"][:, xc0 + g * T1:xc0 + (g + 1) * T1], func=AF.Silu,
            bias=sb["smallA"][:, 32 + g:33 + g], scale=1.0))
        self.on("scalar", lambda sc, g=g: sc.copy(
            out=sb["xcbb"][:, xc0 + g * T1:xc0 + (g + 1) * T1],
            in_=sb["xcb"][:, xc0 + g * T1:xc0 + (g + 1) * T1]),
                inc="c_s")
        if g == G - 1:
            self.mark(("xc", c), "c_s")

    def _bmcm(self, c):
        sb, s = self.sb, self.scr
        xc0 = (c % 2) * G * T1
        self.wmark("tensor", ("xc", c))
        self.wmark("tensor", "bcw")
        self.wmark("tensor", ("bmev", c - 1))
        self.wmark("tensor", ("hadd", c - 3))
        for k in range(G):
            self.on("tensor", lambda t, k=k: t.matmul(
                sb["ps_x"][:, 0:2 * NST], sb["xcbb"][:, xc0 + k * T1:xc0 + k * T1 + P],
                sb["stageb"][:, k * 64:(k + 1) * 64], start=(k == 0), stop=False))
        self.on("tensor", lambda t: t.matmul(sb["ps_x"][:, 0:2 * NST], sb["onesT"][0:1, 0:P],
                                             sb["bcbrow"][:], start=False, stop=True), inc="c_t")
        ct = self.cnt["c_t"]
        self.wait("scalar", "c_t", ct)
        self.on("scalar", lambda sc: sc.copy(out=sb["bmcm"][:], in_=sb["ps_x"][:, 0:2 * NST]),
                inc="c_s")
        self.mark(("bmev", c), "c_s")
        self.wmark("sync", ("bmev", c))
        bmv = s["bm"][0:1, c * T1 * NST:(c + 1) * T1 * NST].rearrange("o (t n) -> (o t) n", n=NST)
        cmv = s["cm"][0:1, c * T1 * NST:(c + 1) * T1 * NST].rearrange("o (t n) -> (o t) n", n=NST)
        self.dma("sync", bmv, sb["bmcm"][:, 0:NST], "d_bm")
        self.dma("sync", cmv, sb["bmcm"][:, NST:2 * NST], "d_bm")
        self.mark(("bmdram", c), "d_bm")

    def _outproj_tile(self, c, j):
        sb, s = self.sb, self.scr
        y0 = (c % 2) * G * T1
        h0 = (c % 2) * G * T1
        wslot = (2 + (j % 2)) * G * P
        dsem = "d_w2" if j % 2 == 0 else "d_w3"
        if (2 + j % 2) in self.slot_gate:
            self.wmark("sync", self.slot_gate[2 + j % 2])
        self.wmark("sync", ("castoutw",))
        self.dma("sync", sb["wst"][:, wslot:wslot + G * P],
                 s["outw"][:, j * G * P:(j + 1) * G * P], dsem)
        self.wait("tensor", dsem, self.cnt[dsem])
        self.wmark("tensor", ("ym", c))
        pbi = j % 2
        pb = sb["ps_m0"] if pbi == 0 else sb["ps_m1"]
        if self.pmgate[pbi]:
            self.wmark("tensor", self.pmgate[pbi])
        for k in range(G):
            self.on("tensor", lambda t, k=k, pb=pb, wslot=wslot, y0=y0: t.matmul(
                pb[:, 0:T1], sb["wst"][:, wslot + k * P:wslot + (k + 1) * P],
                sb["ymb"][:, y0 + k * T1:y0 + (k + 1) * T1], start=(k == 0), stop=False))
        self.on("tensor", lambda t, j=j, pb=pb: t.matmul(
            pb[:, 0:T1], sb["brow"][0:1, j * P:(j + 1) * P],
            sb["onesT"][0:1, 0:T1], start=False, stop=True), inc="c_t")
        self.mark(("opw", c, j), "c_t")
        self.slot_gate[2 + j % 2] = ("opw", c, j)
        ct = self.cnt["c_t"]
        self.wait("scalar", "c_t", ct)
        self.wmark("scalar", ("blk1", c - 2))
        self.on("scalar", lambda sc, j=j, pb=pb, h0=h0: sc.activation(
            out=sb["hb"][:, h0 + j * T1:h0 + (j + 1) * T1], in_=pb[:, 0:T1],
            func=AF.Copy, bias=0.0, scale=1.0), inc="c_s")
        self.mark(("opev", c, j), "c_s")
        self.pmgate[pbi] = ("opev", c, j)

    def _hadd(self, c, j):
        sb = self.sb
        h0 = (c % 2) * G * T1
        src = sb["stream"][:, j * T + c * T1:j * T + c * T1 + T1]
        self.wmark("vector", ("lnrb", c))
        self.wmark("vector", ("opev", c, j))
        self.on("vector", lambda v, src=src: v.tensor_tensor(
            out=sb["stage"][:, 0:T1], in0=src, in1=sb["ps_x"][:, 256:256 + T1], op=AL.mult))
        self.on("vector", lambda v: v.tensor_tensor(
            out=sb["stage"][:, 0:T1], in0=sb["stage"][:, 0:T1],
            in1=sb["ps_x"][:, 384:384 + T1], op=AL.subtract))
        self.on("vector", lambda v, j=j, h0=h0: v.tensor_tensor(
            out=sb["hb"][:, h0 + j * T1:h0 + (j + 1) * T1],
            in0=sb["hb"][:, h0 + j * T1:h0 + (j + 1) * T1],
            in1=sb["stage"][:, 0:T1], op=AL.add), inc="c_v")
        if j == G - 1:
            self.mark(("hadd", c), "c_v")
            self.mark(("h", c), "c_v")

    def _lnh_apply(self, c, g0):
        sb = self.sb
        bank = "ps_w0" if c % 2 == 0 else "ps_w1"
        bk = sb[bank]
        h0 = (c % 2) * G * T1
        tg0 = c * T1
        self.wmark("vector", ("lnb", ("lnh", c)))
        for g in range(g0, g0 + 2):
            hsrc = sb["hb"][:, h0 + g * T1:h0 + (g + 1) * T1]
            self.on("vector", lambda v, g=g, hsrc=hsrc: v.tensor_tensor(
                out=sb["stage"][:, 256:256 + T1], in0=hsrc, in1=bk[:, 256:256 + T1],
                op=AL.mult))
            self.on("vector", lambda v, g=g: v.tensor_tensor(
                out=sb["stage"][:, 256:256 + T1], in0=sb["stage"][:, 256:256 + T1],
                in1=bk[:, 384:384 + T1], op=AL.subtract))
            self.on("vector", lambda v, g=g: v.tensor_scalar(
                out=sb["stage"][:, 256:256 + T1], in0=sb["stage"][:, 256:256 + T1],
                scalar1=sb["smallB"][:, g:g + 1], scalar2=sb["smallB"][:, 8 + g:9 + g],
                op0=AL.mult, op1=AL.add))
            self.on("vector", lambda v, g=g, tg0=tg0: v.tensor_tensor(
                out=sb["stream"][:, g * T + tg0:g * T + tg0 + T1],
                in0=sb["stream"][:, g * T + tg0:g * T + tg0 + T1],
                in1=sb["stage"][:, 256:256 + T1], op=AL.add),
                    inc=("c_v" if g == g0 + 1 else None))
        if g0 + 2 == G:
            self.mark(("blk1", c), "c_v")
            self.mark(("bankfree", bank), "c_v")
            self.sqgate = ("blk1", c)

    def _pf_lnf(self, g):
        sb = self.sb
        if g < 4:
            return sb["xcb"][:, g * 512:(g + 1) * 512]
        return sb["xab"][:, (g - 4) * 512:(g - 3) * 512]

    def _pf_h2(self, g):
        sb = self.sb
        if g < 4:
            return sb["xcbb"][:, g * 512:(g + 1) * 512]
        if g < 6:
            return sb["ln0b"][:, (g - 4) * 512:(g - 3) * 512]
        return sb["yb"][:].bitcast(BF16)[:, (g - 6) * 512:(g - 5) * 512]

    def _pf_y1(self, ff):
        sb = self.sb
        if ff < 8:
            return sb["hb"][:].bitcast(BF16)[:, ff * 512:(ff + 1) * 512]
        if ff < 20:
            return sb["ln0f"][:].bitcast(BF16)[:, (ff - 8) * 512:(ff - 7) * 512]
        if ff < 28:
            return sb["szb"][:].bitcast(BF16)[:, (ff - 20) * 512:(ff - 19) * 512]
        return sb["ymb"][:].bitcast(BF16)[:, (ff - 28) * 512:(ff - 27) * 512]

    def _lnstats512(self, srcf, tag, src_gate=None):
        sb = self.sb
        w0, w1 = sb["ps_w0"], sb["ps_w1"]
        NT = 512
        self.wmark("tensor", ("bankfree", "ps_w0"))
        self.wmark("tensor", ("bankfree", "ps_w1"))
        if src_gate is not None:
            self.wmark("tensor", src_gate)
        for g in range(G):
            self.on("tensor", lambda t, g=g: t.matmul(w0[0:1, 0:NT], sb["onesT"][:, 0:1],
                                                      srcf(g), start=(g == 0), stop=(g == G - 1)),
                    inc=("c_t" if g == G - 1 else None))
        self.mark(("ln5sum", tag), "c_t")
        if src_gate is not None:
            self.wmark("scalar", src_gate)
        for g in range(G):
            par = (g % 2) * NT
            if self.sqgate is not None:
                self.wmark("scalar", self.sqgate)
            self.wmark("scalar", ("ln5mm", tag, g - 2))
            self.on("scalar", lambda sc, g=g, par=par: sc.activation(
                out=sb["sq"][:, par:par + NT], in_=srcf(g), func=AF.Square,
                bias=0.0, scale=1.0), inc="c_s")
            css = self.cnt["c_s"]
            self.wait("tensor", "c_s", css)
            self.on("tensor", lambda t, g=g, par=par: t.matmul(
                w1[0:1, 0:NT], sb["onesT"][:, 0:1], sb["sq"][:, par:par + NT],
                start=(g == 0), stop=(g == G - 1), skip_group_check=True),
                    inc="c_t")
            self.mark(("ln5mm", tag, g), "c_t")
        self.sqgate = ("ln5mm", tag, G - 1)
        ct0 = self.cnt["c_t"]
        st = sb["rbuf"]
        self.wait("vector", "c_t", ct0)
        self.on("vector", lambda v: v.tensor_scalar(out=st[0:1, 0:NT], in0=w0[0:1, 0:NT],
                                                    scalar1=1.0 / D, scalar2=None, op0=AL.mult))
        self.on("vector", lambda v: v.tensor_scalar(out=st[0:1, NT:2 * NT], in0=w1[0:1, 0:NT],
                                                    scalar1=1.0 / D, scalar2=None, op0=AL.mult))
        self.on("vector", lambda v: v.tensor_tensor(out=st[0:1, 2 * NT:3 * NT], in0=st[0:1, 0:NT],
                                                    in1=st[0:1, 0:NT], op=AL.mult))
        self.on("vector", lambda v: v.tensor_tensor(out=st[0:1, NT:2 * NT], in0=st[0:1, NT:2 * NT],
                                                    in1=st[0:1, 2 * NT:3 * NT], op=AL.subtract),
                inc="c_v")
        self.wait("scalar", "c_v", self.cnt["c_v"])
        self.on("scalar", lambda sc: sc.activation(out=st[0:1, 2 * NT:3 * NT],
                                                   in_=st[0:1, NT:2 * NT], func=AF.Sqrt,
                                                   bias=self.sb["cst"][0:1, 0:1], scale=1.0),
                inc="c_s")
        self.wait("vector", "c_s", self.cnt["c_s"])
        self.on("vector", lambda v: v.reciprocal(out=st[0:1, 2 * NT:3 * NT],
                                                 in_=st[0:1, 2 * NT:3 * NT]))
        self.on("vector", lambda v: v.tensor_tensor(out=st[0:1, 3 * NT:4 * NT], in0=st[0:1, 0:NT],
                                                    in1=st[0:1, 2 * NT:3 * NT], op=AL.mult),
                inc="c_v")
        self.wait("tensor", "c_v", self.cnt["c_v"])
        self.on("tensor", lambda t: t.matmul(w0[:, 0:NT], sb["onesT"][0:1, 0:P],
                                             st[0:1, 2 * NT:3 * NT], start=True, stop=True))
        self.on("tensor", lambda t: t.matmul(w1[:, 0:NT], sb["onesT"][0:1, 0:P],
                                             st[0:1, 3 * NT:4 * NT], start=True, stop=True),
                inc="c_t")
        return self.cnt["c_t"]

    def _dequant(self, qsrc, ssrc, dst, kchunks, mchunks):
        sb = self.sb
        qint = sb["szb"][:].bitcast(I32)
        dst4 = dst.rearrange("p (m kk q) -> p m kk q", kk=kchunks, q=P)
        ncols = mchunks * P
        for k in range(kchunks):
            for c0 in range(0, ncols, 1024):
                w = min(1024, ncols - c0)
                self.wait("sync", "d_cs", self.cnt["d_cs"])
                self.wmark("sync", ("dq_used",))
                self.dma("sync", sb["stage"][:, 0:w],
                         ssrc[k:k + 1, c0:c0 + w].broadcast_to([P, w]), "d_c")
                self.dma("sync", qint[:, 0:w], qsrc[k * P:(k + 1) * P, c0:c0 + w], "d_c")
                dc = self.cnt["d_c"]
                self.wait("scalar", "d_c", dc)
                self.wmark("scalar", ("dq_used",))
                self.on("scalar", lambda sc, w=w: sc.activation(
                    out=sb["xcb"][:, 0:w], in_=qint[:, 0:w], func=AF.Copy, bias=0.0, scale=1.0),
                    inc="c_s")
                cs = self.cnt["c_s"]
                self.wait("vector", "c_s", cs)
                self.wait("vector", "d_c", dc)
                self.wait("vector", "d_cs", self.cnt["d_cs"])
                self.on("vector", lambda v, w=w: v.tensor_scalar(
                    out=sb["convm"][:, 0:w], in0=sb["stage"][:, 0:w], scalar1=-128.0,
                    scalar2=None, op0=AL.mult))
                self.on("vector", lambda v, w=w: v.tensor_tensor(
                    out=sb["xcb"][:, 0:w], in0=sb["xcb"][:, 0:w], in1=sb["stage"][:, 0:w],
                    op=AL.mult))
                self.on("vector", lambda v, w=w: v.tensor_tensor(
                    out=sb["stageb"][:, 0:w], in0=sb["xcb"][:, 0:w], in1=sb["convm"][:, 0:w],
                    op=AL.add), inc="c_v")
                self.mark(("dq_used",), "c_v")
                self.wait("sync", "c_v", self.cnt["c_v"])
                m0, nm = c0 // P, w // P
                self.dma("sync", dst4[:, m0:m0 + nm, k, :],
                         sb["stageb"][:, 0:w].rearrange("p (m q) -> p m q", q=P), "d_cs")

    def ffn0(self):
        sb, s = self.sb, self.scr
        self.wmark("sync", ("blk1", NC1 - 1))
        self.wmark("scalar", ("blk1", NC1 - 1))
        self.wmark("vector", ("blk1", NC1 - 1))
        self.wmark("tensor", ("blk1", NC1 - 1))
        self.wmark("gpsimd", ("blk1", NC1 - 1))
        self.wmark("sync", ("ym", NC1 - 1))
        self.wmark("scalar", ("ym", NC1 - 1))
        self.wmark("vector", ("ym", NC1 - 1))
        self.on("vector", lambda v: v.tensor_scalar(
            out=sb["smallB"][:, 56:64], in0=sb["smallB"][:, 32:40], scalar1=0.5, scalar2=None,
            op0=AL.mult), inc="c_v")
        self.wmark("sync", ("pd", "fc1"))
        self.wmark("sync", ("pd", "fc2"))
        NT = 512
        NCF = T // NT
        for c in range(NCF):
            t0 = c * NT
            src = lambda g: sb["stream"][:, g * T + t0:g * T + t0 + NT]
            ctb = self._lnstats512(src, ("f0a", c))
            self.wait("vector", "c_t", ctb)
            self.wmark("vector", ("f2rd", c - 1, G - 1))
            for g in range(G):
                self.on("vector", lambda v, g=g: v.tensor_tensor(
                    out=self._pf_lnf(g), in0=src(g), in1=sb["ps_w0"][:, 0:NT], op=AL.mult))
                self.on("vector", lambda v, g=g: v.tensor_tensor(
                    out=self._pf_lnf(g), in0=self._pf_lnf(g), in1=sb["ps_w1"][:, 0:NT],
                    op=AL.subtract), inc=("c_v" if g == G - 1 else None))
            self.mark(("ln1", c), "c_v")
            self.mark(("bankfree", "ps_w0"), "c_v")
            self.mark(("bankfree", "ps_w1"), "c_v")
            ctb2 = self._lnstats512(self._pf_lnf, ("f0b", c), src_gate=("ln1", c))
            self.wait("vector", "c_t", ctb2)
            self.wmark("vector", ("f1rd", c - 1, GF - 1))
            for g in range(G):
                self.on("vector", lambda v, g=g: v.tensor_tensor(
                    out=sb["convm"][:, 0:NT], in0=self._pf_lnf(g), in1=sb["ps_w0"][:, 0:NT],
                    op=AL.mult))
                self.on("vector", lambda v, g=g: v.tensor_tensor(
                    out=sb["convm"][:, 0:NT], in0=sb["convm"][:, 0:NT],
                    in1=sb["ps_w1"][:, 0:NT], op=AL.subtract))
                self.on("vector", lambda v, g=g: v.tensor_scalar(
                    out=self._pf_h2(g), in0=sb["convm"][:, 0:NT],
                    scalar1=sb["smallB"][:, 16 + g:17 + g], scalar2=sb["smallB"][:, 24 + g:25 + g],
                    op0=AL.mult, op1=AL.add), inc=("c_v" if g == G - 1 else None))
            self.mark(("h2", c), "c_v")
            self.mark(("bankfree", "ps_w0"), "c_v")
            self.mark(("bankfree", "ps_w1"), "c_v")
            def f1dma(ff):
                wslot = (ff % 2) * G * P
                dsem = "d_w0" if ff % 2 == 0 else "d_w1"
                if (ff % 2) in self.slot_gate:
                    self.wmark("sync", self.slot_gate[ff % 2])
                self.dma("sync", sb["wst"][:, wslot:wslot + G * P],
                         s["fc1"][:, ff * G * P:(ff + 1) * G * P], dsem)
                self.wstfeed[("f1", c, ff)] = self.cnt[dsem]
            f1dma(0)
            for ff in range(GF):
                wslot = (ff % 2) * G * P
                dsem = "d_w0" if ff % 2 == 0 else "d_w1"
                if ff + 1 < GF:
                    f1dma(ff + 1)
                self.wait("tensor", dsem, self.wstfeed[("f1", c, ff)])
                self.wmark("tensor", ("h2", c))
                pbi = ff % 2
                pb = sb["ps_m0"] if pbi == 0 else sb["ps_m1"]
                if self.pmgate[pbi]:
                    self.wmark("tensor", self.pmgate[pbi])
                self.wmark("tensor", ("f1rd", c - 1, ff))
                for k in range(G):
                    self.on("tensor", lambda t, k=k, pb=pb, wslot=wslot: t.matmul(
                        pb[:, 0:NT], sb["wst"][:, wslot + k * P:wslot + (k + 1) * P],
                        self._pf_h2(k), start=(k == 0), stop=(k == G - 1)),
                            inc=("c_t" if k == G - 1 else None))
                self.mark(("f1w", c, ff), "c_t")
                self.slot_gate[ff % 2] = ("f1w", c, ff)
                ct = self.cnt["c_t"]
                self.wait("scalar", "c_t", ct)
                self.on("scalar", lambda sc, ff=ff, pb=pb: sc.activation(
                    out=self._pf_y1(ff), in_=pb[:, 0:NT], func=AF.Silu,
                    bias=sb["smallA"][:, 64 + ff:65 + ff], scale=1.0), inc="c_s")
                self.mark(("f1e", c, ff), "c_s")
                self.pmgate[pbi] = ("f1e", c, ff)
                self.pump_step(2)
            for j in range(G):
                pbi = j % 2
                pb = sb["ps_m0"] if pbi == 0 else sb["ps_m1"]
                if self.pmgate[pbi]:
                    self.wmark("tensor", self.pmgate[pbi])
                self.wmark("tensor", ("f1e", c, GF - 1))
                def f2dma(j2, piece):
                    wslot = (2 + (piece % 2)) * G * P
                    dsem = "d_w2" if piece % 2 == 0 else "d_w3"
                    if (2 + piece % 2) in self.slot_gate:
                        self.wmark("sync", self.slot_gate[2 + piece % 2])
                    self.dma("sync", sb["wst"][:, wslot:wslot + G * P],
                             s["fc2"][:, (j2 * GF + piece * 8) * P:(j2 * GF + piece * 8 + 8) * P],
                             dsem)
                    self.wstfeed[("f2", c, j2, piece)] = self.cnt[dsem]
                if j == 0:
                    f2dma(0, 0)
                for piece in range(4):
                    wslot = (2 + (piece % 2)) * G * P
                    dsem = "d_w2" if piece % 2 == 0 else "d_w3"
                    if piece + 1 < 4:
                        f2dma(j, piece + 1)
                    elif j + 1 < G:
                        f2dma(j + 1, 0)
                    self.wait("tensor", dsem, self.wstfeed[("f2", c, j, piece)])
                    for k8 in range(G):
                        kk = piece * 8 + k8
                        self.on("tensor", lambda t, k8=k8, kk=kk, pb=pb, wslot=wslot: t.matmul(
                            pb[:, 0:NT], sb["wst"][:, wslot + k8 * P:wslot + (k8 + 1) * P],
                            self._pf_y1(kk), start=(kk == 0), stop=(kk == GF - 1)),
                            inc=("c_t" if k8 == G - 1 else None))
                    self.mark(("f2w", c, j, piece), "c_t")
                    self.slot_gate[2 + piece % 2] = ("f2w", c, j, piece)
                    self.pump_step(2)
                self.on("tensor", lambda t: t.nop(), inc="c_t")
                ct = self.cnt["c_t"]
                if j == G - 1:
                    for ff in range(GF):
                        self.mark(("f1rd", c, ff), "c_t")
                self.wait("scalar", "c_t", ct)
                self.wmark("scalar", ("f2v", c, j - 2))
                par = (j % 2) * NT
                self.on("scalar", lambda sc, j=j, pb=pb, par=par: sc.activation(
                    out=sb["sq"][:, par:par + NT], in_=pb[:, 0:NT], func=AF.Copy,
                    bias=0.0, scale=sb["cst"][:, 1:2]), inc="c_s")
                self.mark(("f2e", c, j), "c_s")
                self.pmgate[pbi] = ("f2e", c, j)
                cs = self.cnt["c_s"]
                self.wait("vector", "c_s", cs)
                self.on("vector", lambda v, j=j, par=par: v.tensor_tensor(
                    out=sb["sq"][:, par:par + NT], in0=sb["sq"][:, par:par + NT],
                    in1=self._pf_lnf(j), op=AL.add))
                self.on("vector", lambda v, j=j, t0=t0, par=par: v.scalar_tensor_tensor(
                    out=sb["stream"][:, j * T + t0:j * T + t0 + NT],
                    in0=sb["stream"][:, j * T + t0:j * T + t0 + NT],
                    scalar=sb["smallB"][:, 56 + j:57 + j], in1=sb["sq"][:, par:par + NT],
                    op0=AL.add, op1=AL.add), inc="c_v")
                self.mark(("f2v", c, j), "c_v")
                if j == G - 1:
                    self.mark(("f2rd", c, G - 1), "c_v")
            self.sqgate = ("f2v", c, G - 1)
            self.mark(("f0done", c), "c_v")
        self.mark(("ffn0_done",), "c_v")


    def _pool_lnx(self, g):
        sb = self.sb
        if g < 2:
            return sb["bxb"][:, g * T:(g + 1) * T]
        if g < 5:
            return sb["xab"][:].bitcast(BF16)[:, (g - 2) * T:(g - 1) * T]
        return sb["ln0f"][:].bitcast(BF16)[:, (g - 5) * T:(g - 4) * T]

    def _pool_sg(self, g):
        sb = self.sb
        half = 1024
        if g < 2:
            return sb["hb"][:, g * half:(g + 1) * half]
        if g == 2:
            return sb["sq"][:, 0:half]
        if g < 5:
            return sb["szb"][:, (g - 3) * half:(g - 2) * half]
        if g < 7:
            return sb["xcb"][:, (g - 5) * half:(g - 4) * half]
        return sb["sgx"][:, 0:half]

    def _pool_o(self, h):
        sb = self.sb
        if h < 2:
            return sb["xcbb"][:, h * 1024:(h + 1) * 1024]
        if h < 4:
            return sb["yb"][:].bitcast(BF16)[:, (h - 2) * 1024:(h - 1) * 1024]
        if h < 6:
            return sb["ln0b"][:, (h - 4) * 1024:(h - 3) * 1024]
        if h == 6:
            return sb["psc"][:].bitcast(BF16)[:, 0:1024]
        return sb["ptmp"][:].bitcast(BF16)[:, 0:1024]

    def _pool_S(self, h):
        sb = self.sb
        if h < 4:
            return sb["bbb"][:, h * P:(h + 1) * P]
        if h < 6:
            return sb["sprime"][:, (h - 4) * P:(h - 3) * P]
        return sb["amisc"][:, (h - 6) * P:(h - 5) * P]

    def l1_attn(self):
        sb, s, i = self.sb, self.scr, self.i
        arena = sb["pin"][:].bitcast(BF16)
        kbuf = sb["convm"]
        csb = sb["rbuf"]
        rcs = sb["ymb"][:].bitcast(F32)
        qb = sb["stage"]
        vb = sb["stageb"]
        SC = 1024
        for e in ["sync", "scalar", "vector", "tensor", "gpsimd"]:
            self.wmark(e, ("ffn0_done",))
        self.wmark("sync", ("pd", "wg2"))
        if self.stage >= 4:
            for e in ["sync", "scalar", "vector", "tensor"]:
                self.wmark(e, ("pd", "w2"))
        self.dma("sync", sb["identf"][:], i["trimask"], "d_misc")
        nm = self.cnt["d_misc"]
        self.wait("vector", "d_misc", nm)
        for c in range(NC1):
            t0 = c * T1
            src = lambda g: sb["stream"][:, g * T + t0:g * T + t0 + T1]
            bank = "ps_w0" if c % 2 == 0 else "ps_w1"
            bk = sb[bank]
            ctb = self._lnstats(src, bank, ("l1ln", c))
            self.wait("vector", "c_t", ctb)
            for g in range(G):
                self.on("vector", lambda v, g=g: v.tensor_tensor(
                    out=sb["cstrip"][0:1, 0:1] if False else kbuf[:, 0:T1], in0=src(g),
                    in1=bk[:, 256:256 + T1], op=AL.mult))
                self.on("vector", lambda v, g=g, t0=t0: v.tensor_tensor(
                    out=self._pool_lnx(g)[:, t0:t0 + T1], in0=kbuf[:, 0:T1],
                    in1=bk[:, 384:384 + T1], op=AL.subtract),
                        inc=("c_v" if g == G - 1 else None))
            self.mark(("lnx", c), "c_v")
            self.mark(("bankfree", bank), "c_v")
        self.mark(("lnx_all",), "c_v")

        HT = T // 2
        NCH = HT // TC
        for half in range(2):
            th0 = half * HT
            for e in ["sync", "scalar", "vector", "tensor"]:
                self.wmark(e, ("l1half", half - 1))
            self.wmark("scalar", ("kvmm", half - 1, H - 1, NCH - 1))
            dsem = "d_w2"
            if 2 in self.slot_gate:
                self.wmark("sync", self.slot_gate[2])
            self.dma("sync", sb["wst"][:, 2 * G * P:3 * G * P], s["wg1"][:, 0:G * P], dsem)
            self.wait("tensor", dsem, self.cnt[dsem])
            self.wmark("tensor", ("lnx_all",))
            for tt in range(2):
                pb = sb["ps_m0"] if tt == 0 else sb["ps_m1"]
                if self.pmgate[tt]:
                    self.wmark("tensor", self.pmgate[tt])
                for k in range(G):
                    self.on("tensor", lambda t, k=k, tt=tt, pb=pb: t.matmul(
                        pb[:, 0:512], sb["wst"][:, 2 * G * P + k * P:2 * G * P + (k + 1) * P],
                        self._pool_lnx(k)[:, th0 + tt * 512:th0 + (tt + 1) * 512],
                        start=(k == 0), stop=(k == G - 1)), inc=("c_t" if k == G - 1 else None))
                ct = self.cnt["c_t"]
                self.mark(("g1w", half, tt), "c_t")
                self.slot_gate[2] = ("g1w", half, tt)
                self.wait("scalar", "c_t", ct)
                self.on("scalar", lambda sc, tt=tt, pb=pb: sc.copy(
                    out=arena[:, tt * 512:(tt + 1) * 512], in_=pb[:, 0:512]), inc="c_s")
                self.mark(("g1e", half, tt), "c_s")
                self.pmgate[tt] = ("g1e", half, tt)
            if 3 in self.slot_gate:
                self.wmark("sync", self.slot_gate[3])
            self.dma("sync", sb["wst"][:, 3 * G * P:4 * G * P], s["wg2"][:, 0:G * P], "d_w3")
            self.wait("tensor", "d_w3", self.cnt["d_w3"])
            self.wmark("tensor", ("g1e", half, 1))
            for j in range(G):
                for tt in range(2):
                    pbi = (j * 2 + tt) % 2
                    pb = sb["ps_m0"] if pbi == 0 else sb["ps_m1"]
                    if self.pmgate[pbi]:
                        self.wmark("tensor", self.pmgate[pbi])
                    self.on("tensor", lambda t, j=j, tt=tt, pb=pb: t.matmul(
                        pb[:, 0:512], sb["wst"][:, 3 * G * P + j * P:3 * G * P + (j + 1) * P],
                        arena[:, tt * 512:(tt + 1) * 512], start=True, stop=True), inc="c_t")
                    ct = self.cnt["c_t"]
                    self.mark(("g2w", half, j, tt), "c_t")
                    self.slot_gate[3] = ("g2w", half, j, tt)
                    self.wait("scalar", "c_t", ct)
                    self.on("scalar", lambda sc, j=j, tt=tt, pb=pb: sc.activation(
                        out=self._pool_sg(j)[:, tt * 512:(tt + 1) * 512], in_=pb[:, 0:512],
                        func=AF.Silu, bias=0.0, scale=1.0), inc="c_s")
                    self.mark(("sg", half, j, tt), "c_s")
                    self.pmgate[pbi] = ("sg", half, j, tt)
            self.mark(("sg_done", half), "c_s")
            self.mark(("g2w_all", half), "c_t")

            for h in range(H):
                self._l1_proj(s["wk"], h, kbuf, half, tag=("kp", half, h), silu=False,
                              bf16out=False, gate=("chunkdone", half, h - 1, NCH - 1),
                              gate_eng="c_v")
                cvk = self.marks[("kp", half, h)][1]
                self.wait("vector", "c_s", cvk)
                self.on("vector", lambda v: v.tensor_reduce(
                    out=csb[:, SC:SC + 1], in_=kbuf[:, 0:HT], axis=AX.X, op=AL.max), inc="c_v")
                self.on("vector", lambda v: v.tensor_scalar(
                    out=csb[:, SC + 1:SC + 2], in0=csb[:, SC:SC + 1], scalar1=-1.0, scalar2=None,
                    op0=AL.mult), inc="c_v")
                cv = self.cnt["c_v"]
                self.wait("scalar", "c_v", cv)
                if half == 0:
                    self.on("scalar", lambda sc, h=h: sc.copy(
                        out=csb[:, SC + 8 + h:SC + 9 + h], in_=csb[:, SC:SC + 1]), inc="c_s")
                self.on("scalar", lambda sc: sc.activation(
                    out=kbuf[:, 0:HT], in_=kbuf[:, 0:HT], func=AF.Exp,
                    bias=csb[:, SC + 1:SC + 2], scale=1.0), inc="c_s")
                cs0 = self.cnt["c_s"]
                self.wait("vector", "c_s", cs0)
                self.on("vector", lambda v: v.tensor_tensor_scan(
                    out=csb[:, 0:HT], data0=kbuf[:, 0:HT], data1=kbuf[:, 0:HT],
                    initial=0.0, op0=AL.add, op1=AL.bypass), inc="c_v")
                if half == 1:
                    self.on("vector", lambda v, h=h: v.tensor_tensor(
                        out=csb[:, SC + 2:SC + 3], in0=csb[:, SC + 8 + h:SC + 9 + h],
                        in1=csb[:, SC:SC + 1], op=AL.subtract), inc="c_v")
                    cv1 = self.cnt["c_v"]
                    self.wait("scalar", "c_v", cv1)
                    self.on("scalar", lambda sc: sc.activation(
                        out=csb[:, SC + 2:SC + 3], in_=csb[:, SC + 2:SC + 3], func=AF.Exp,
                        bias=0.0, scale=1.0), inc="c_s")
                    cs1 = self.cnt["c_s"]
                    self.wait("vector", "c_s", cs1)
                    self.on("vector", lambda v, h=h: v.tensor_tensor(
                        out=csb[:, SC + 3:SC + 4], in0=csb[:, SC + 16 + h:SC + 17 + h],
                        in1=csb[:, SC + 2:SC + 3], op=AL.mult), inc="c_v")
                    self.on("vector", lambda v: v.tensor_scalar(
                        out=csb[:, 0:HT], in0=csb[:, 0:HT],
                        scalar1=csb[:, SC + 3:SC + 4], scalar2=None, op0=AL.add), inc="c_v")
                else:
                    self.on("vector", lambda v, h=h: v.tensor_copy(
                        out=csb[:, SC + 16 + h:SC + 17 + h], in_=csb[:, HT - 1:HT]), inc="c_v")
                self.on("vector", lambda v: v.reciprocal(out=rcs[:, 0:HT], in_=csb[:, 0:HT]),
                        inc="c_v")
                self.mark(("csr", half, h), "c_v")
                self._l1_proj(s["wq"], h, qb, half, tag=("qp", half, h), silu=True,
                              bf16out=False, gate=("chunkdone", half, h - 1, NCH - 1),
                              gate_eng="c_v")
                self._l1_proj(s["wv"], h, vb, half, tag=("vp", half, h), silu=False,
                              bf16out=True, gate=("tmm", half, h - 1, NCH - 1),
                              gate_eng="c_t")
                for ci in range(NCH):
                    self._l1_chunk(half, h, ci)
                self.mark(("head", half, h), "c_v")
                o2scr = sb["rbuf"][:, 1184:1696]
                for tt in range(2):
                    if tt == 0:
                        self.wmark("scalar", ("osq", half, h - 1, 1))
                    else:
                        self.wmark("scalar", ("osq", half, h, 0))
                    self.wmark("scalar", ("oev", half, h, NCH - 1))
                    self.on("scalar", lambda sc, h=h, tt=tt, o2scr=o2scr: sc.activation(
                        out=o2scr, in_=self._pool_o(h)[:, tt * 512:(tt + 1) * 512],
                        func=AF.Square, bias=0.0, scale=1.0), inc="c_s")
                    cso = self.cnt["c_s"]
                    self.wait("tensor", "c_s", cso)
                    cbb = "ps_cb0" if tt == 0 else "ps_cb1"
                    if h == 0:
                        self.wmark("tensor", ("cbfree", half, tt))
                    self.on("tensor", lambda t, h=h, cbb=cbb, o2scr=o2scr: t.matmul(
                        sb[cbb][0:1, 0:512], sb["onesT"][:, 0:1], o2scr,
                        start=(h == 0), stop=(h == H - 1), skip_group_check=True), inc="c_t")
                    self.mark(("osq", half, h, tt), "c_t")
            self.wmark("vector", ("osq", half, H - 1, 0))
            self.wmark("vector", ("osq", half, H - 1, 1))
            st = sb["cstrip"]
            self.on("vector", lambda v: v.tensor_scalar(
                out=st[0:1, 0:HT] if False else sb["strip"][0:1, 0:512],
                in0=sb["ps_cb0"][0:1, 0:512], scalar1=1.0 / D, scalar2=None, op0=AL.mult))
            self.on("vector", lambda v: v.tensor_scalar(
                out=sb["rbuf"][0:1, 0:512], in0=sb["ps_cb1"][0:1, 0:512],
                scalar1=1.0 / D, scalar2=None, op0=AL.mult), inc="c_v")
            cv = self.cnt["c_v"]
            self.wait("scalar", "c_v", cv)
            self.on("scalar", lambda sc: sc.activation(
                out=sb["strip"][0:1, 0:512], in_=sb["strip"][0:1, 0:512], func=AF.Sqrt,
                bias=sb["cst"][0:1, 0:1], scale=1.0), inc="c_s")
            self.on("scalar", lambda sc: sc.activation(
                out=sb["rbuf"][0:1, 0:512], in_=sb["rbuf"][0:1, 0:512], func=AF.Sqrt,
                bias=sb["cst"][0:1, 0:1], scale=1.0), inc="c_s")
            cs2 = self.cnt["c_s"]
            self.wait("vector", "c_s", cs2)
            self.on("vector", lambda v: v.reciprocal(out=sb["strip"][0:1, 0:512],
                                                     in_=sb["strip"][0:1, 0:512]))
            self.on("vector", lambda v: v.reciprocal(out=sb["rbuf"][0:1, 0:512],
                                                     in_=sb["rbuf"][0:1, 0:512]), inc="c_v")
            cv2 = self.cnt["c_v"]
            self.wait("tensor", "c_v", cv2)
            self.on("tensor", lambda t: t.matmul(sb["ps_cb0"][:, 0:512], sb["onesT"][0:1, 0:P],
                                                 sb["strip"][0:1, 0:512], start=True, stop=True))
            self.on("tensor", lambda t: t.matmul(sb["ps_cb1"][:, 0:512], sb["onesT"][0:1, 0:P],
                                                 sb["rbuf"][0:1, 0:512], start=True, stop=True),
                    inc="c_t")
            ctb = self.cnt["c_t"]
            self.mark(("rbb", half), "c_t")
            for j in range(G):
                dsemo = "d_w2" if j % 2 == 0 else "d_w3"
                if (2 + j % 2) in self.slot_gate:
                    self.wmark("sync", self.slot_gate[2 + j % 2])
                self.dma("sync", sb["wst"][:, (2 + j % 2) * G * P:(3 + j % 2) * G * P],
                         s["wo"][:, j * G * P:(j + 1) * G * P], dsemo)
                self.wait("tensor", dsemo, self.cnt[dsemo])
                for tt in range(2):
                    pbi = tt
                    pb = sb["ps_m0"] if pbi == 0 else sb["ps_m1"]
                    if self.pmgate[pbi]:
                        self.wmark("tensor", self.pmgate[pbi])
                    for h in range(H):
                        if j == 0:
                            self.wmark("vector", ("rbb", half))
                            self.wmark("vector", ("head", half, h))
                            self.wmark("vector", ("sg_done", half))
                            cbb = sb["ps_cb0"] if tt == 0 else sb["ps_cb1"]
                            self.on("vector", lambda v, h=h, tt=tt: v.tensor_tensor(
                                out=kbuf[:, 0:512], in0=self._pool_o(h)[:, tt * 512:(tt + 1) * 512],
                                in1=self._pool_sg(h)[:, tt * 512:(tt + 1) * 512], op=AL.mult))
                            self.on("vector", lambda v, h=h, tt=tt, cbb=cbb: v.tensor_tensor(
                                out=kbuf[:, 0:512], in0=kbuf[:, 0:512], in1=cbb[:, 0:512],
                                op=AL.mult))
                            self.on("vector", lambda v, h=h, tt=tt: v.tensor_scalar(
                                out=self._pool_o(h)[:, tt * 512:(tt + 1) * 512],
                                in0=kbuf[:, 0:512],
                                scalar1=sb["smallB"][:, 40 + h:41 + h], scalar2=None,
                                op0=AL.mult), inc="c_v")
                            self.mark(("om", half, h, tt), "c_v")
                        self.wmark("tensor", ("om", half, h, tt))
                        self.on("tensor", lambda t, j=j, h=h, tt=tt, pb=pb: t.matmul(
                            pb[:, 0:512],
                            sb["wst"][:, (2 + j % 2) * G * P + h * P:(2 + j % 2) * G * P + (h + 1) * P],
                            self._pool_o(h)[:, tt * 512:(tt + 1) * 512],
                            start=(h == 0), stop=(h == H - 1)),
                            inc=("c_t" if h == H - 1 else None))
                    self.mark(("wow", half, j, tt), "c_t")
                    self.slot_gate[2 + j % 2] = ("wow", half, j, tt)
                    ct = self.cnt["c_t"]
                    self.wait("vector", "c_t", ct)
                    self.on("vector", lambda v, j=j, tt=tt, pb=pb: v.scalar_tensor_tensor(
                        out=sb["stream"][:, j * T + th0 + tt * 512:j * T + th0 + (tt + 1) * 512],
                        in0=sb["stream"][:, j * T + th0 + tt * 512:j * T + th0 + (tt + 1) * 512],
                        scalar=0.0, in1=pb[:, 0:512], op0=AL.add, op1=AL.add), inc="c_v")
                    self.mark(("wost", half, j, tt), "c_v")
                    self.pmgate[pbi] = ("wost", half, j, tt)
            self.mark(("cbfree", half + 1, 0), "c_v")
            self.mark(("cbfree", half + 1, 1), "c_v")
            self.mark(("l1half", half), "c_v")
        self.mark(("l1_done",), "c_v")

    def _l1_proj(self, wsrc, h, dstbuf, half, tag, silu, bf16out, gate=None, gate_eng=None):
        sb = self.sb
        th0 = half * (T // 2)
        dsem = "d_w0" if h % 2 == 0 else "d_w1"
        wslot = (h % 2) * G * P
        if (h % 2) in self.slot_gate:
            self.wmark("sync", self.slot_gate[h % 2])
        self.dma("sync", sb["wst"][:, wslot:wslot + G * P],
                 wsrc[:, h * G * P:(h + 1) * G * P], dsem)
        self.wait("tensor", dsem, self.cnt[dsem])
        self.wmark("tensor", ("lnx_all",))
        for tt in range(2):
            pbi = tt
            pb = sb["ps_m0"] if pbi == 0 else sb["ps_m1"]
            if self.pmgate[pbi]:
                self.wmark("tensor", self.pmgate[pbi])
            for k in range(G):
                self.on("tensor", lambda t, k=k, tt=tt, pb=pb, wslot=wslot: t.matmul(
                    pb[:, 0:512], sb["wst"][:, wslot + k * P:wslot + (k + 1) * P],
                    self._pool_lnx(k)[:, th0 + tt * 512:th0 + (tt + 1) * 512],
                    start=(k == 0), stop=(k == G - 1)), inc=("c_t" if k == G - 1 else None))
            self.mark(tag + (tt, "w"), "c_t")
            self.slot_gate[h % 2] = tag + (tt, "w")
            ct = self.cnt["c_t"]
            self.wait("scalar", "c_t", ct)
            if gate is not None and tt == 0:
                self.wmark("scalar", gate)
            fn = AF.Silu if silu else AF.Copy
            self.on("scalar", lambda sc, tt=tt, pb=pb, fn=fn: sc.activation(
                out=dstbuf[:, tt * 512:(tt + 1) * 512], in_=pb[:, 0:512], func=fn,
                bias=0.0, scale=1.0), inc="c_s")
            self.mark(tag + (tt, "e"), "c_s")
            self.pmgate[pbi] = tag + (tt, "e")
        self.mark(tag, "c_s")
        self.pump_step(2)

    def _l1_chunk(self, half, h, ci):
        sb = self.sb
        arena = sb["pin"][:].bitcast(BF16)
        kbuf = sb["convm"]
        csb = sb["rbuf"]
        rcs = sb["ymb"][:].bitcast(F32)
        qb = sb["stage"]
        vb = sb["stageb"]
        SC = 1024
        Sh = self._pool_S(h)
        t0 = ci * TC
        NCH = (T // 2) // TC
        first = (half == 0 and ci == 0)
        prevk = ((half, h, ci - 1) if ci > 0 else
                 ((half, h - 1, NCH - 1) if h > 0 else
                  ((half - 1, H - 1, NCH - 1) if half > 0 else None)))
        A_Sb = arena[:, 0:P]
        A_qn = arena[:, P:2 * P]
        A_qi = arena[:, 2 * P:3 * P]
        A_eb = arena[:, 3 * P:4 * P]
        A_Pb = arena[:, 4 * P:5 * P]
        A_KeT = arena[:, 5 * P:6 * P]
        A_VT = arena[:, 6 * P:7 * P]
        t1f = csb[:, SC + 32:SC + 32 + P]

        def pk(name):
            return None if prevk is None else (name,) + prevk

        self.wmark("vector", ("csr", half, h))
        self.wmark("vector", ("qp", half, h))
        if prevk is not None:
            self.wmark("vector", pk("pmm"))
            self.wmark("vector", pk("tmm"))
            self.wmark("vector", pk("omm"))
        if h == 0 and ci == 0:
            self.wmark("vector", ("g2w_all", half))
        self.on("vector", lambda v: v.tensor_tensor(
            out=t1f, in0=qb[:, t0:t0 + TC], in1=rcs[:, t0:t0 + TC], op=AL.mult), inc="c_v")
        self.on("vector", lambda v: v.tensor_copy(out=A_qn, in_=t1f), inc="c_v")
        if not first:
            if ci == 0:
                scal = csb[:, SC + 3:SC + 4] if half == 1 else csb[:, SC + 16 + h:SC + 17 + h]
            else:
                scal = csb[:, t0 - 1:t0]
            self.on("vector", lambda v, scal=scal: v.tensor_scalar(
                out=A_qi, in0=t1f, scalar1=scal, scalar2=None, op0=AL.mult), inc="c_v")
            self.on("vector", lambda v: v.tensor_copy(out=A_Sb, in_=Sh), inc="c_v")
        self.on("vector", lambda v: v.tensor_copy(out=A_eb, in_=kbuf[:, t0:t0 + TC]), inc="c_v")
        cv_prep = self.cnt["c_v"]
        self.mark(("prep", half, h, ci), "c_v")

        self.wait("tensor", "c_v", cv_prep)
        if prevk is not None:
            self.wmark("tensor", pk("pevac"))
            self.wmark("tensor", pk("supd"))
        self.on("tensor", lambda t: t.matmul(sb["ps_x"][:, 0:P], A_eb, A_qn,
                                             start=True, stop=True), inc="c_t")
        ct_p = self.cnt["c_t"]
        self.mark(("pmm", half, h, ci), "c_t")
        psT = sb["ps_a"][:].bitcast(BF16)
        if prevk is not None:
            self.wmark("tensor", pk("tevac"))
        self.wmark("tensor", ("vp", half, h))
        self.on("tensor", lambda t: t.transpose(psT[:, 0:P], A_eb, sb["identb"][:]))
        self.on("tensor", lambda t, t0=t0: t.transpose(psT[:, P:2 * P], vb[:, t0:t0 + TC],
                                                       sb["identb"][:]), inc="c_t")
        ct_t = self.cnt["c_t"]
        self.mark(("tmm", half, h, ci), "c_t")
        self.wait("scalar", "c_t", ct_t)
        if prevk is not None:
            self.wmark("scalar", pk("kvmm"))
            self.wmark("scalar", pk("omm"))
        self.on("scalar", lambda sc: sc.copy(out=A_KeT, in_=psT[:, 0:P]))
        self.on("scalar", lambda sc: sc.copy(out=A_VT, in_=psT[:, P:2 * P]), inc="c_s")
        cs_t = self.cnt["c_s"]
        self.mark(("tevac", half, h, ci), "c_s")
        self.wait("vector", "c_t", ct_p)
        if prevk is not None:
            self.wmark("vector", pk("omm"))
        self.on("vector", lambda v: v.tensor_tensor(out=A_Pb, in0=sb["ps_x"][:, 0:P],
                                                    in1=sb["identf"][:], op=AL.mult), inc="c_v")
        cv_pm = self.cnt["c_v"]
        self.mark(("pevac", half, h, ci), "c_v")

        ob = "ps_w0" if ci % 2 == 0 else "ps_w1"
        self.wmark("tensor", ("bankfree", ob))
        self.wait("tensor", "c_v", cv_pm)
        self.wait("tensor", "c_s", cs_t)
        if not first:
            self.on("tensor", lambda t, ob=ob: t.matmul(sb[ob][:, 0:P], A_Sb, A_qi,
                                                        start=True, stop=False))
            self.on("tensor", lambda t, ob=ob: t.matmul(sb[ob][:, 0:P], A_VT, A_Pb,
                                                        start=False, stop=True), inc="c_t")
        else:
            self.on("tensor", lambda t, ob=ob: t.matmul(sb[ob][:, 0:P], A_VT, A_Pb,
                                                        start=True, stop=True), inc="c_t")
        ct_o = self.cnt["c_t"]
        self.mark(("omm", half, h, ci), "c_t")
        self.on("tensor", lambda t: t.matmul(sb["ps_x"][:, P:2 * P], A_KeT, A_VT,
                                             start=True, stop=True), inc="c_t")
        ct_kv = self.cnt["c_t"]
        self.mark(("kvmm", half, h, ci), "c_t")

        self.wait("scalar", "c_t", ct_o)
        self.on("scalar", lambda sc, ob=ob, t0=t0, h=h: sc.activation(
            out=self._pool_o(h)[:, t0:t0 + TC], in_=sb[ob][:, 0:P], func=AF.Copy,
            bias=0.0, scale=sb["cst"][:, 2:3]), inc="c_s")
        self.mark(("oev", half, h, ci), "c_s")
        self.mark(("bankfree", ob), "c_s")

        self.wait("vector", "c_t", ct_kv)
        if first:
            scal2 = 0.0
        elif ci == 0:
            scal2 = csb[:, SC + 3:SC + 4] if half == 1 else csb[:, SC + 16 + h:SC + 17 + h]
        else:
            scal2 = csb[:, t0 - 1:t0]
        self.on("vector", lambda v, scal2=scal2, Sh=Sh: v.scalar_tensor_tensor(
            out=Sh, in0=Sh, scalar=scal2, in1=sb["ps_x"][:, P:2 * P],
            op0=AL.mult, op1=AL.add), inc="c_v")
        self.on("vector", lambda v, t0=t0: v.reciprocal(
            out=csb[:, SC + 4:SC + 5], in_=csb[:, t0 + TC - 1:t0 + TC]), inc="c_v")
        self.on("vector", lambda v, Sh=Sh: v.tensor_scalar(
            out=Sh, in0=Sh, scalar1=csb[:, SC + 4:SC + 5], scalar2=None, op0=AL.mult),
                inc="c_v")
        self.mark(("supd", half, h, ci), "c_v")
        self.mark(("chunkdone", half, h, ci), "c_v")
        self.pump_step(2)

    def _ternarize(self, wsrc, dst, kchunks, mchunks):
        sb = self.sb
        ncols = mchunks * P
        npieces_per_k = (ncols + 1023) // 1024
        acc = sb["rbuf"]
        nacc = 0
        for k in range(kchunks):
            for c0 in range(0, ncols, 1024):
                w = min(1024, ncols - c0)
                self.wmark("sync", ("tz_used",))
                self.dma("sync", sb["stage"][:, 0:w], wsrc[k * P:(k + 1) * P, c0:c0 + w], "d_c")
                dc = self.cnt["d_c"]
                self.wait("scalar", "d_c", dc)
                self.wmark("scalar", ("tz2_used",))
                self.on("scalar", lambda sc, w=w, nacc=nacc: sc.activation(
                    out=sb["convm"][:, 0:w], in_=sb["stage"][:, 0:w], func=AF.Abs,
                    bias=0.0, scale=1.0, accum_out=acc[:, 1024 + nacc:1025 + nacc]), inc="c_s")
                self.mark(("tz_used",), "c_s")
                nacc += 1
        css = self.cnt["c_s"]
        self.wait("vector", "c_s", css)
        self.on("vector", lambda v, nacc=nacc: v.tensor_reduce(
            out=acc[:, 1024 + nacc:1025 + nacc], in_=acc[:, 1024:1024 + nacc],
            axis=AX.X, op=AL.add), inc="c_v")
        cv = self.cnt["c_v"]
        self.wait("tensor", "c_v", cv)
        self.wmark("tensor", ("tzps",))
        self.on("tensor", lambda t, nacc=nacc: t.matmul(
            sb["ps_x"][0:1, 0:1], acc[:, 1024 + nacc:1025 + nacc], sb["onesT"][:, 0:1],
            start=True, stop=True), inc="c_t")
        ct = self.cnt["c_t"]
        self.wait("vector", "c_t", ct)
        self.on("vector", lambda v: v.tensor_scalar(
            out=acc[0:1, 1023:1024], in0=sb["ps_x"][0:1, 0:1],
            scalar1=1.0 / (kchunks * P * ncols), scalar2=1e-8, op0=AL.mult, op1=AL.max),
                inc="c_v")
        self.on("vector", lambda v: v.reciprocal(out=acc[0:1, 1023:1024],
                                                 in_=acc[0:1, 1023:1024]), inc="c_v")
        cv2 = self.cnt["c_v"]
        self.wait("tensor", "c_v", cv2)
        self.on("tensor", lambda t: t.matmul(sb["ps_x"][:, 4:5], sb["onesT"][0:1, 0:P],
                                             acc[0:1, 1023:1024], start=True, stop=True),
                inc="c_t")
        ct2 = self.cnt["c_t"]
        self.mark(("tzps",), "c_t")
        self.wait("vector", "c_t", ct2)
        self.on("vector", lambda v: v.tensor_copy(out=acc[:, 1022:1023], in_=sb["ps_x"][:, 4:5]),
                inc="c_v")
        self.mark(("tz_recip",), "c_v")
        recip = acc[:, 1022:1023]
        dst4 = dst.rearrange("p (m kk q) -> p m kk q", kk=kchunks, q=P)
        qflt = sb["szb"]
        self.wait("sync", "c_s", css)
        for k in range(kchunks):
            for c0 in range(0, ncols, 1024):
                w = min(1024, ncols - c0)
                self.wait("sync", "d_cs", self.cnt["d_cs"])
                self.wmark("sync", ("tz2_used",))
                self.dma("sync", sb["stage"][:, 0:w], wsrc[k * P:(k + 1) * P, c0:c0 + w], "d_c")
                dc = self.cnt["d_c"]
                self.wait("scalar", "d_c", dc)
                self.wmark("scalar", ("tz_recip",))
                self.wmark("scalar", ("tz2v",))
                self.on("scalar", lambda sc, w=w, recip=recip: sc.activation(
                    out=sb["convm"][:, 0:w], in_=sb["stage"][:, 0:w], func=AF.Abs,
                    bias=0.0, scale=recip))
                self.on("scalar", lambda sc, w=w: sc.activation(
                    out=qflt[:, 0:w], in_=sb["stage"][:, 0:w], func=AF.Sign,
                    bias=0.0, scale=1.0), inc="c_s")
                cs2v = self.cnt["c_s"]
                self.mark(("tz2_used",), "c_s")
                self.wait("vector", "c_s", cs2v)
                self.wait("vector", "d_cs", self.cnt["d_cs"])
                self.on("vector", lambda v, w=w: v.tensor_scalar(
                    out=sb["convm"][:, 0:w], in0=sb["convm"][:, 0:w],
                    scalar1=0.5, scalar2=None, op0=AL.is_gt))
                self.on("vector", lambda v, w=w: v.tensor_tensor(
                    out=sb["stageb"][:, 0:w], in0=sb["convm"][:, 0:w], in1=qflt[:, 0:w],
                    op=AL.mult), inc="c_v")
                self.mark(("tz2v",), "c_v")
                self.wait("sync", "c_v", self.cnt["c_v"])
                m0, nm = c0 // P, w // P
                self.dma("sync", dst4[:, m0:m0 + nm, k, :],
                         sb["stageb"][:, 0:w].rearrange("p (m q) -> p m q", q=P), "d_cs")

    def ffn1(self):
        sb, s = self.sb, self.scr
        self.pump_drain()
        for e in ["sync", "scalar", "vector", "tensor", "gpsimd"]:
            self.wmark(e, ("l1_done",))
        self.wmark("sync", ("pd", "w1"))
        self.wmark("sync", ("pd", "w2"))
        NT = 512
        NCF = T // NT
        for c in range(NCF):
            t0 = c * NT
            src = lambda g: sb["stream"][:, g * T + t0:g * T + t0 + NT]
            ctb = self._lnstats512(src, ("t3", c))
            self.wait("vector", "c_t", ctb)
            self.wmark("vector", ("g1rd", c - 1, GF - 1))
            for g in range(G):
                self.on("vector", lambda v, g=g: v.tensor_tensor(
                    out=sb["convm"][:, 0:NT], in0=src(g), in1=sb["ps_w0"][:, 0:NT], op=AL.mult))
                self.on("vector", lambda v, g=g: v.tensor_tensor(
                    out=self._pf_h2(g), in0=sb["convm"][:, 0:NT], in1=sb["ps_w1"][:, 0:NT],
                    op=AL.subtract), inc=("c_v" if g == G - 1 else None))
            self.mark(("xln", c), "c_v")
            self.mark(("bankfree", "ps_w0"), "c_v")
            self.mark(("bankfree", "ps_w1"), "c_v")
            def t1dma(ff):
                wslot = (ff % 2) * G * P
                dsem = "d_w0" if ff % 2 == 0 else "d_w1"
                if (ff % 2) in self.slot_gate:
                    self.wmark("sync", self.slot_gate[ff % 2])
                self.dma("sync", sb["wst"][:, wslot:wslot + G * P],
                         s["w1"][:, ff * G * P:(ff + 1) * G * P], dsem)
                self.wstfeed[("t1", c, ff)] = self.cnt[dsem]
            t1dma(0)
            for ff in range(GF):
                wslot = (ff % 2) * G * P
                dsem = "d_w0" if ff % 2 == 0 else "d_w1"
                if ff + 1 < GF:
                    t1dma(ff + 1)
                self.wait("tensor", dsem, self.wstfeed[("t1", c, ff)])
                self.wmark("tensor", ("xln", c))
                pbi = ff % 2
                pb = sb["ps_m0"] if pbi == 0 else sb["ps_m1"]
                if self.pmgate[pbi]:
                    self.wmark("tensor", self.pmgate[pbi])
                self.wmark("tensor", ("g1rd", c - 1, ff))
                for k in range(G):
                    self.on("tensor", lambda t, k=k, pb=pb, wslot=wslot: t.matmul(
                        pb[:, 0:NT], sb["wst"][:, wslot + k * P:wslot + (k + 1) * P],
                        self._pf_h2(k), start=(k == 0), stop=(k == G - 1)),
                            inc=("c_t" if k == G - 1 else None))
                self.mark(("t1w", c, ff), "c_t")
                self.slot_gate[ff % 2] = ("t1w", c, ff)
                ct = self.cnt["c_t"]
                self.wait("scalar", "c_t", ct)
                self.on("scalar", lambda sc, ff=ff, pb=pb: sc.activation(
                    out=self._pf_y1(ff), in_=pb[:, 0:NT], func=AF.Gelu,
                    bias=sb["smallA"][:, 96 + ff:97 + ff], scale=1.0), inc="c_s")
                self.mark(("t1e", c, ff), "c_s")
                self.pmgate[pbi] = ("t1e", c, ff)
            for j in range(G):
                pbi = j % 2
                pb = sb["ps_m0"] if pbi == 0 else sb["ps_m1"]
                if self.pmgate[pbi]:
                    self.wmark("tensor", self.pmgate[pbi])
                self.wmark("tensor", ("t1e", c, GF - 1))
                def w2dma(j2, piece):
                    wslot = (2 + (piece % 2)) * G * P
                    dsem = "d_w2" if piece % 2 == 0 else "d_w3"
                    if (2 + piece % 2) in self.slot_gate:
                        self.wmark("sync", self.slot_gate[2 + piece % 2])
                    self.dma("sync", sb["wst"][:, wslot:wslot + G * P],
                             s["w2"][:, (j2 * GF + piece * 8) * P:(j2 * GF + piece * 8 + 8) * P],
                             dsem)
                    self.wstfeed[("w2", c, j2, piece)] = self.cnt[dsem]
                if j == 0:
                    w2dma(0, 0)
                for piece in range(4):
                    wslot = (2 + (piece % 2)) * G * P
                    dsem = "d_w2" if piece % 2 == 0 else "d_w3"
                    if piece + 1 < 4:
                        w2dma(j, piece + 1)
                    elif j + 1 < G:
                        w2dma(j + 1, 0)
                    self.wait("tensor", dsem, self.wstfeed[("w2", c, j, piece)])
                    for k8 in range(G):
                        kk = piece * 8 + k8
                        self.on("tensor", lambda t, k8=k8, kk=kk, pb=pb, wslot=wslot: t.matmul(
                            pb[:, 0:NT], sb["wst"][:, wslot + k8 * P:wslot + (k8 + 1) * P],
                            self._pf_y1(kk), start=(kk == 0), stop=(kk == GF - 1)),
                            inc=("c_t" if k8 == G - 1 else None))
                    self.mark(("t2w", c, j, piece), "c_t")
                    self.slot_gate[2 + piece % 2] = ("t2w", c, j, piece)
                self.on("tensor", lambda t: t.nop(), inc="c_t")
                ct = self.cnt["c_t"]
                if j == G - 1:
                    for ff in range(GF):
                        self.mark(("g1rd", c, ff), "c_t")
                self.wait("vector", "c_t", ct)
                self.on("vector", lambda v, j=j, t0=t0, pb=pb: v.scalar_tensor_tensor(
                    out=sb["stream"][:, j * T + t0:j * T + t0 + NT],
                    in0=sb["stream"][:, j * T + t0:j * T + t0 + NT],
                    scalar=sb["smallB"][:, 48 + j:49 + j],
                    in1=pb[:, 0:NT], op0=AL.add, op1=AL.add), inc="c_v")
                self.mark(("t2e", c, j), "c_v")
                self.pmgate[pbi] = ("t2e", c, j)
            self.mark(("f1done", c), "c_v")
        self.mark(("ffn1_done",), "c_v")


    def epilogue(self):
        sb = self.sb
        key = {1: ("blk1", NC1 - 1), 2: ("ffn0_done",), 3: ("l1_done",), 4: ("ffn1_done",)}[self.stage]
        sem, val = self.marks[key]
        self.wait("sync", sem, val)
        for g in range(G):
            self.dma("sync", self.y_out[g * P:(g + 1) * P, :], sb["stream"][:, g * T:(g + 1) * T],
                     "d_out")
        self.wait("sync", "d_out", self.cnt["d_out"])

    def build(self):
        blk, me = self.block, self

        @blk.sync
        def _(sync):
            me._reset("sync", sync)
            me.walk()

        @blk.vector
        def _(v):
            me._reset("vector", v)
            me.walk()

        @blk.scalar
        def _(s):
            me._reset("scalar", s)
            me.walk()

        @blk.tensor
        def _(t):
            me._reset("tensor", t)
            me.walk()

        @blk.gpsimd
        def _(g):
            me._reset("gpsimd", g)
            me.walk()

        for cm in reversed(self._ctx):
            cm.__exit__(None, None, None)
        self.nc.finalize()
        return self.nc


def _prep_weights(inp, stage):
    f = np.ascontiguousarray
    a = lambda k: np.asarray(inp[k], np.float32)
    d = {}
    d["alog_r"] = f(a("L0_attn_A_log").reshape(G, P, NST).transpose(1, 0, 2).reshape(P, GN))
    d["inwT"] = f(a("L0_attn_in_w").T)
    d["brow_l0"] = f(a("L0_attn_out_b").reshape(1, 1024))
    d["inb_xa_r"] = f(a("L0_attn_in_b")[:D].reshape(G, P).T)
    d["inb_z_r"] = f(a("L0_attn_in_b")[D:].reshape(G, P).T)
    d["convw_r"] = f(a("L0_attn_conv_w").reshape(D, 4).reshape(G, P, 4)
                     .transpose(1, 0, 2).reshape(P, 4 * G))
    d["convb_r"] = f(a("L0_attn_conv_b").reshape(G, P).T)
    d["bcwT"] = f(np.concatenate([a("L0_attn_B_w").T, a("L0_attn_C_w").T], axis=1))
    d["bcb"] = f(np.concatenate([a("L0_attn_B_b"), a("L0_attn_C_b")]).reshape(1, 2 * NST))
    d["outwT"] = f(a("L0_attn_out_w").T)
    d["lnw_r"] = f(a("L0_attn_ln_w").reshape(G, P).T)
    d["lnb_r"] = f(a("L0_attn_ln_b").reshape(G, P).T)
    d["ones"] = np.ones((P, 512), np.float32)
    d["identity"] = np.eye(P, dtype=np.float32)
    if stage >= 2:
        d["ffg_r"] = f(a("L0_ffn_ln_gamma").reshape(G, P).T)
        d["ffb_r"] = f(a("L0_ffn_ln_beta").reshape(G, P).T)
        d["fc1q"] = f(np.asarray(inp["L0_ffn_fc1_q"], np.int32))
        d["fc1s"] = f(a("L0_ffn_fc1_scales"))
        d["fc1b_r"] = f(a("L0_ffn_fc1_bias").reshape(GF, P).T)
        d["fc2q"] = f(np.asarray(inp["L0_ffn_fc2_q"], np.int32))
        d["fc2s"] = f(a("L0_ffn_fc2_scales"))
        d["fc2b_r"] = f(a("L0_ffn_fc2_bias").reshape(G, P).T)
    if stage >= 3:
        d["wqT"] = f(a("L1_attn_Wq").T)
        d["wkT"] = f(a("L1_attn_Wk").T)
        d["wvT"] = f(a("L1_attn_Wv").T)
        d["wg1T"] = f(a("L1_attn_Wg1").T)
        d["wg2T"] = f(a("L1_attn_Wg2").T)
        d["gnorm_r"] = f(a("L1_attn_g_norm_weight").reshape(H, P).T)
        d["woT"] = f(a("L1_attn_Wo").T)
        d["trimask"] = np.triu(np.ones((P, P), np.float32))
    if stage >= 4:
        d["w1T"] = f(a("L1_ffn_W1").T)
        d["w1b_r"] = f(a("L1_ffn_b1").reshape(GF, P).T)
        d["w2T"] = f(a("L1_ffn_W2").T)
        d["w2b_r"] = f(a("L1_ffn_b2").reshape(G, P).T)
    return d


_CACHE = {}


def run_stage(inputs, stage, cores=None):
    if stage not in _CACHE:
        _CACHE[stage] = Bld(stage=stage).build()
    nc = _CACHE[stage]
    w = _prep_weights(inputs, stage)
    x = np.asarray(inputs["L0_attn_x"], np.float32)
    nb = x.shape[0] if cores is None else cores
    in_maps = []
    for bi in range(nb):
        m = dict(w)
        m["x_cm"] = np.ascontiguousarray(x[bi].T)
        in_maps.append(m)
    res = run_bass_kernel_spmd(nc, in_maps, core_ids=list(range(nb)))
    return np.stack([np.ascontiguousarray(np.asarray(r["y_cm"]).T) for r in res.results], axis=0)


def kernel(**inputs):
    return run_stage(inputs, stage=4)
